# version 23
# baseline (speedup 1.0000x reference)
import numpy as np
import ml_dtypes
from contextlib import ExitStack

import concourse.bass as bass
import concourse.mybir as mybir
from concourse.bass_utils import run_bass_kernel_spmd

F32 = mybir.dt.float32
BF16 = mybir.dt.bfloat16
AF = mybir.ActivationFunctionType
ALU = mybir.AluOpType
AX = mybir.AxisListType

B, S, D = 4, 4096, 1024
H, HD = 16, 64
DFF = 4096
NCORE = 8
TOK = 2048
T = 512
NT = TOK // T
NCH = D // 128
NFC = DFF // 128
EPS = 1e-5
QEPS = 1e-5

ENGINES = ("tensor", "vector", "scalar", "sync", "gpsimd")


class Em:

    def __init__(self):
        self.ops = {e: [] for e in ENGINES}
        self.count = {}
        self.last_w = {}
        self.readers = {}
        self.waited = {e: {} for e in ENGINES}
        self.dma_sems = {}

    def op(self, eng, fn, reads=(), writes=(), sem=None, batch=False):
        is_dma = sem is not None
        if is_dma:
            semname = sem
            if semname not in self.dma_sems:
                self.dma_sems[semname] = batch
        else:
            semname = eng
        need = {}

        def add(dep):
            s2, v = dep
            if s2 == semname and eng == "tensor" and not is_dma:
                return
            if v > need.get(s2, 0):
                need[s2] = v

        for r in reads:
            lw = self.last_w.get(r)
            if lw:
                add(lw)
        for w in writes:
            lw = self.last_w.get(w)
            if lw:
                add(lw)
            for rd in self.readers.get(w, ()):
                add(rd)
        if is_dma and not batch:
            prev = self.count.get(semname, 0)
            if prev > need.get(semname, 0):
                need[semname] = prev
        wd = self.waited[eng]
        waits = []
        for s2, v in sorted(need.items()):
            if v > wd.get(s2, 0):
                waits.append((s2, v))
                wd[s2] = v
        inc = 16 if is_dma else 1
        self.count[semname] = self.count.get(semname, 0) + inc
        me = (semname, self.count[semname])
        for r in reads:
            self.readers.setdefault(r, []).append(me)
        for w in writes:
            self.last_w[w] = me
            self.readers[w] = []
        self.ops[eng].append((fn, waits, inc, semname))

    def settle(self, resources, semname):
        me = (semname, self.count.get(semname, 0))
        for r in resources:
            self.last_w[r] = me
            self.readers[r] = []

    def final_wait(self, semnames):
        waits = [(s, self.count.get(s, 0)) for s in semnames]
        self.ops["scalar"].append((None, waits, 0, "scalar"))


def build_nc():
    nc = bass.Bass("TRN2", target_bir_lowering=False)

    xfm = nc.dram_tensor("xfm", [D, TOK], F32, kind="ExternalInput").ap()
    xgin = nc.dram_tensor("xgin", [D, 16], F32, kind="ExternalInput").ap()
    w1t = nc.dram_tensor("w1t", [D, DFF], F32, kind="ExternalInput").ap()
    w2t = nc.dram_tensor("w2t", [DFF, D], F32, kind="ExternalInput").ap()
    wqt = nc.dram_tensor("wqt", [D, D], F32, kind="ExternalInput").ap()
    wkt = nc.dram_tensor("wkt", [D, D], F32, kind="ExternalInput").ap()
    wvt = nc.dram_tensor("wvt", [D, D], F32, kind="ExternalInput").ap()
    wot = nc.dram_tensor("wot", [D, D], F32, kind="ExternalInput").ap()
    b1in = nc.dram_tensor("b1in", [128, NFC], F32, kind="ExternalInput").ap()
    b2row = nc.dram_tensor("b2row", [1, D], F32, kind="ExternalInput").ap()
    identin = nc.dram_tensor("identin", [128, 128], BF16, kind="ExternalInput").ap()
    onesmean_in = nc.dram_tensor("onesmean_in", [128, 1], BF16, kind="ExternalInput").ap()
    onesrow_in = nc.dram_tensor("onesrow_in", [1, 128], F32, kind="ExternalInput").ap()
    onescol_in = nc.dram_tensor("onescol_in", [128, 1], F32, kind="ExternalInput").ap()
    onesrow512_in = nc.dram_tensor("onesrow512_in", [1, 512], BF16, kind="ExternalInput").ap()
    maskstd_in = nc.dram_tensor("maskstd_in", [128, 80], BF16, kind="ExternalInput").ap()
    maskq0_in = nc.dram_tensor("maskq0_in", [128, 80], BF16, kind="ExternalInput").ap()
    esel_in = nc.dram_tensor("esel_in", [16, D], BF16, kind="ExternalInput").ap()
    yout = nc.dram_tensor("y", [D, TOK], F32, kind="ExternalOutput").ap()

    w1q_s = nc.dram_tensor("w1q_s", [NFC * 128 * NCH * 128], BF16, kind="Internal").ap()
    w2q_s = nc.dram_tensor("w2q_s", [NCH * 128 * NFC * 128], BF16, kind="Internal").ap()
    wq_s = nc.dram_tensor("wq_s", [128 * NCH * D], BF16, kind="Internal").ap()
    wk_s = nc.dram_tensor("wk_s", [128 * NCH * D], BF16, kind="Internal").ap()
    wv_s = nc.dram_tensor("wv_s", [128 * NCH * D], BF16, kind="Internal").ap()
    wo_s = nc.dram_tensor("wo_s", [128 * NCH * D], BF16, kind="Internal").ap()

    with ExitStack() as st:
        E = st.enter_context
        xres = [E(nc.sbuf_tensor("xres0", [128, NCH * T], F32))] * 2
        xsq = E(nc.sbuf_tensor("xsq", [128, 2 * NCH * T], BF16))
        xhat = E(nc.sbuf_tensor("xhat", [128, NCH * T], BF16))
        h = E(nc.sbuf_tensor("h", [128, NFC * T], BF16))
        q_hd = E(nc.sbuf_tensor("q_hd", [64, H * T], BF16))
        k_hd = E(nc.sbuf_tensor("k_hd", [64, H * T], BF16))
        vaug = E(nc.sbuf_tensor("vaug", [128, (T // 64) * D], BF16))
        kg_hd = E(nc.sbuf_tensor("kg_hd", [64, H * 16], BF16))
        attnf = E(nc.sbuf_tensor("attnf", [128, NCH * T], BF16))
        esrc = E(nc.sbuf_tensor("esrc", [128, 2 * 8 * 80], F32))
        psb = E(nc.sbuf_tensor("psb", [128, 3 * 8 * 80], BF16))
        ptsb = E(nc.sbuf_tensor("ptsb", [128, 2 * 8 * 128], BF16))
        negm = E(nc.sbuf_tensor("negm", [128, 16], F32))
        sums = E(nc.sbuf_tensor("sums", [128, 16], F32))
        rinvf = E(nc.sbuf_tensor("rinvf", [128, 16], F32))
        rinvb = E(nc.sbuf_tensor("rinvb", [128, 16], BF16))
        mb16 = E(nc.sbuf_tensor("mb16", [128, T], BF16))
        rb16 = E(nc.sbuf_tensor("rb16", [128, T], BF16))
        w1b = [E(nc.sbuf_tensor(f"w1b{i}", [128, 4 * NCH * 128], BF16)) for i in range(2)]
        w2b = [E(nc.sbuf_tensor(f"w2b{i}", [128, NFC * 128], BF16)) for i in range(2)]
        pjb = [E(nc.sbuf_tensor(f"pjb{i}", [128, NCH * 512], BF16)) for i in range(2)]
        xgres = E(nc.sbuf_tensor("xgres", [128, NCH * 16], F32))
        xgsq = E(nc.sbuf_tensor("xgsq", [128, 2 * NCH * 16], BF16))
        xghat = E(nc.sbuf_tensor("xghat", [128, NCH * 16], BF16))
        hg = E(nc.sbuf_tensor("hg", [128, NFC * 16], BF16))
        rowm = E(nc.sbuf_tensor("rowm", [1, T], F32))
        rowmb = E(nc.sbuf_tensor("rowmb", [1, T], BF16))
        rowrb = E(nc.sbuf_tensor("rowrb", [1, T], BF16))
        rowvar = E(nc.sbuf_tensor("rowvar", [1, T], F32))
        rows2 = rowvar
        rowr = E(nc.sbuf_tensor("rowr", [1, T], F32))
        growm = E(nc.sbuf_tensor("growm", [1, 16], F32))
        grows2 = E(nc.sbuf_tensor("grows2", [1, 16], F32))
        growvar = E(nc.sbuf_tensor("growvar", [1, 16], F32))
        growr = E(nc.sbuf_tensor("growr", [1, 16], F32))
        b1s = E(nc.sbuf_tensor("b1s", [128, NFC], F32))
        b2rb = E(nc.sbuf_tensor("b2rb", [1, D], BF16))
        ident = E(nc.sbuf_tensor("ident", [128, 128], BF16))
        onesmean = E(nc.sbuf_tensor("onesmean", [128, 1], BF16))
        onesrow = E(nc.sbuf_tensor("onesrow", [1, 128], F32))
        onescol = E(nc.sbuf_tensor("onescol", [128, 1], F32))
        onesrow512 = E(nc.sbuf_tensor("onesrow512", [1, 512], BF16))
        maskstd = E(nc.sbuf_tensor("maskstd", [128, 80], BF16))
        maskq0 = E(nc.sbuf_tensor("maskq0", [128, 80], BF16))
        gacc1 = E(nc.sbuf_tensor("gacc1", [128, NCH], F32))
        gacc2 = E(nc.sbuf_tensor("gacc2", [128, 16], F32))
        growtot = E(nc.sbuf_tensor("growtot", [1, 2], F32))
        throw = E(nc.sbuf_tensor("throw", [1, 2], F32))
        nthrow = E(nc.sbuf_tensor("nthrow", [1, 2], F32))
        thrc = E(nc.sbuf_tensor("thrc", [128, 2], F32))
        nthrc = E(nc.sbuf_tensor("nthrc", [128, 2], F32))
        SA = E(nc.psum_tensor("SA", [128, 1024], F32))
        SB = E(nc.psum_tensor("SB", [128, 1024], F32))
        PSB2 = E(nc.psum_tensor("PSB2", [128, 1024], F32))
        PSB3 = E(nc.psum_tensor("PSB3", [128, 1024], F32))
        SA_bf = SA.bitcast(BF16)
        SB_bf = SB.bitcast(BF16)
        PSB2_bf = PSB2.bitcast(BF16)
        h32 = h.bitcast(F32)
        wstage = [h32[:, 0:4096], h32[:, 4096:8192]]
        wstage_full = h32[:, 0:8192]
        tmps3 = [xsq[:, 0:4096], xsq[:, 4096:8192], attnf[:, 0:4096]]
        TMPAL = [["xb"], ["sq"], ["attnf"]]
        tmps = [xsq[:, 0:4096], xsq[:, 4096:8192]]
        tmp_full = xsq[:, 0:8192]

        em = Em()
        sems = {}
        P4 = [(PSB2, 0, 4), (PSB2, T, 5), (PSB3, 0, 6), (PSB3, T, 7)]

        def p4view(i, rows=128, width=T):
            tns, off, bankid = P4[i % 4]
            return tns[0:rows, off:off + width], [f"pb{bankid}"]

        def bk(*ids):
            return [f"pb{i}" for i in ids]

        def dma(fn, reads=(), writes=(), sem=None, batch=False):
            em.op("sync", fn, reads, writes, sem=sem, batch=batch)

        def ln(xr, xrname, glob):
            em.op("scalar", lambda e: e.activation(xsq[:, 0:2048], xr[:, 0:2048],
                                                   AF.Copy),
                  reads=["xr_lo"], writes=["xb"])
            em.op("scalar", lambda e: e.activation(xsq[:, 2048:4096], xr[:, 2048:4096],
                                                   AF.Copy),
                  reads=["xr_hi"], writes=["xb"])
            em.op("vector", lambda e: e.tensor_tensor(xsq[:, 4096:8192], xsq[:, 0:4096],
                                                      xsq[:, 0:4096], ALU.mult),
                  reads=["xb"], writes=["sq"])
            for c in range(NCH):
                em.op("tensor",
                      lambda e, c=c: e.matmul(SA[0:1, 0:T], onesmean[:],
                                              xsq[:, c * T:(c + 1) * T],
                                              start=(c == 0), stop=(c == NCH - 1)),
                      reads=["xb", "onesmean"], writes=bk(0))
            for c in range(NCH):
                em.op("tensor",
                      lambda e, c=c: e.matmul(SA[0:1, 512:512 + T], onesmean[:],
                                              xsq[:, 4096 + c * T:4096 + (c + 1) * T],
                                              start=(c == 0), stop=(c == NCH - 1)),
                      reads=["sq", "onesmean"], writes=bk(1))
            em.op("vector", lambda e: e.tensor_copy(rowmb[:], SA[0:1, 0:T]),
                  reads=bk(0), writes=["rowmb"])
            em.op("vector", lambda e: e.tensor_tensor(SA[0:1, 0:T], rowmb[:], rowmb[:],
                                                      ALU.mult),
                  reads=["rowmb"], writes=bk(0))
            em.op("vector", lambda e: e.tensor_copy(rowvar[:], SA[0:1, 512:512 + T]),
                  reads=bk(1), writes=["rowvar"])
            em.op("vector", lambda e: e.tensor_tensor(rowvar[:], rowvar[:], SA[0:1, 0:T],
                                                      ALU.subtract),
                  reads=["rowvar"] + bk(0), writes=["rowvar"])
            em.op("vector", lambda e: e.tensor_scalar(rowvar[:], rowvar[:], 1.0, EPS,
                                                      ALU.mult, ALU.add),
                  reads=["rowvar"], writes=["rowvar"])
            em.op("scalar", lambda e: e.activation(rowr[:], rowvar[:], AF.Ln),
                  reads=["rowvar"], writes=["rowr"])
            em.op("scalar", lambda e: e.activation(rowrb[:], rowr[:], AF.Exp, scale=-0.5),
                  reads=["rowr"], writes=["rowrb"])
            em.op("tensor", lambda e: e.matmul(SB[:, 0:T], onesrow512[0:1, 0:128],
                                               rowmb[:], start=True, stop=True),
                  reads=["rowmb", "onesrow512"], writes=bk(2))
            em.op("tensor", lambda e: e.matmul(SB[:, 512:512 + T], onesrow512[0:1, 0:128],
                                               rowrb[:], start=True, stop=True),
                  reads=["rowrb", "onesrow512"], writes=bk(3))
            em.op("vector", lambda e: e.tensor_copy(mb16[:], SB[:, 0:T]),
                  reads=bk(2), writes=["mb16"])
            em.op("vector", lambda e: e.tensor_copy(rb16[:], SB[:, 512:512 + T]),
                  reads=bk(3), writes=["rb16"])
            for c in range(NCH):
                em.op("vector",
                      lambda e, c=c: e.tensor_tensor(xhat[:, c * T:(c + 1) * T],
                                                     xsq[:, c * T:(c + 1) * T],
                                                     mb16[:], ALU.subtract),
                      reads=["xb_lo" if c < 4 else "xb_hi", "mb16"],
                      writes=[f"xhs{c}"])
            for c in range(NCH):
                em.op("vector",
                      lambda e, c=c: e.tensor_tensor(xhat[:, c * T:(c + 1) * T],
                                                     xhat[:, c * T:(c + 1) * T],
                                                     rb16[:], ALU.mult),
                      reads=[f"xhs{c}", "rb16"], writes=[f"xh{c}", f"xhs{c}"])
            if glob:
                em.op("scalar", lambda e: e.activation(xgsq[:, 0:128], xgres[:], AF.Copy),
                      reads=["xgres"], writes=["gxb"])
                em.op("vector", lambda e: e.tensor_tensor(xgsq[:, 128:256], xgsq[:, 0:128],
                                                          xgsq[:, 0:128], ALU.mult),
                      reads=["gxb"], writes=["gsq"])
                for c in range(NCH):
                    em.op("tensor",
                          lambda e, c=c: e.matmul(PSB3[0:1, 0:16], onesmean[:],
                                                  xgsq[:, c * 16:(c + 1) * 16],
                                                  start=(c == 0), stop=(c == NCH - 1)),
                          reads=["gxb", "onesmean"], writes=bk(6))
                for c in range(NCH):
                    em.op("tensor",
                          lambda e, c=c: e.matmul(PSB3[0:1, 512:528], onesmean[:],
                                                  xgsq[:, 128 + c * 16:128 + (c + 1) * 16],
                                                  start=(c == 0), stop=(c == NCH - 1)),
                          reads=["gsq", "onesmean"], writes=bk(7))
                em.op("vector", lambda e: e.tensor_copy(growm[:], PSB3[0:1, 0:16]),
                      reads=bk(6), writes=["growm"])
                em.op("vector", lambda e: e.tensor_copy(grows2[:], PSB3[0:1, 512:528]),
                      reads=bk(7), writes=["grows2"])
                em.op("vector", lambda e: e.tensor_tensor(growvar[:], growm[:], growm[:],
                                                          ALU.mult),
                      reads=["growm"], writes=["growvar"])
                em.op("vector", lambda e: e.tensor_tensor(growvar[:], grows2[:], growvar[:],
                                                          ALU.subtract),
                      reads=["grows2", "growvar"], writes=["growvar"])
                em.op("vector", lambda e: e.tensor_scalar(growvar[:], growvar[:], 1.0, EPS,
                                                          ALU.mult, ALU.add),
                      reads=["growvar"], writes=["growvar"])
                em.op("scalar", lambda e: e.activation(growr[:], growvar[:], AF.Ln),
                      reads=["growvar"], writes=["growr"])
                em.op("scalar", lambda e: e.activation(growr[:], growr[:], AF.Exp, scale=-0.5),
                      reads=["growr"], writes=["growr"])
                em.op("tensor", lambda e: e.matmul(PSB3[:, 16:32], onesrow[:], growm[:],
                                                   start=True, stop=True),
                      reads=["growm", "onesrow"], writes=bk(6))
                em.op("tensor", lambda e: e.matmul(PSB3[:, 528:544], onesrow[:], growr[:],
                                                   start=True, stop=True),
                      reads=["growr", "onesrow"], writes=bk(7))
                mgb = PSB3[:, 16:32].rearrange("p k -> p () k").broadcast_to(
                    [128, NCH, 16])
                rgb = PSB3[:, 528:544].rearrange("p k -> p () k").broadcast_to(
                    [128, NCH, 16])
                em.op("vector",
                      lambda e, mgb=mgb: e.tensor_tensor(
                          xghat[:].rearrange("p (c k) -> p c k", c=NCH),
                          xgres[:].rearrange("p (c k) -> p c k", c=NCH), mgb, ALU.subtract),
                      reads=["xgres"] + bk(6), writes=["xghat"])
                em.op("vector",
                      lambda e, rgb=rgb: e.tensor_tensor(
                          xghat[:].rearrange("p (c k) -> p c k", c=NCH),
                          xghat[:].rearrange("p (c k) -> p c k", c=NCH), rgb, ALU.mult),
                      reads=["xghat"] + bk(7), writes=["xghat"])

        def ffn1(xr, xrname, glob, side_jobs=None):
            def w1dma(g):
                src = bass.AP(w1q_s.tensor, g * NCH * 128 * 512,
                              [[512, 128], [128 * 512, NCH], [1, 512]])
                buf = w1b[g % 2]
                em.op("sync",
                      lambda e, src=src, buf=buf: e.dma_start(
                          buf[:].rearrange("p (c r) -> p c r", c=NCH), src),
                      reads=["w1q_s"], writes=[f"w1b{g % 2}"], sem=f"w1b{g % 2}")
            w1dma(0)
            w1dma(1)
            for j in range(NFC):
                g = j // 4
                buf = w1b[g % 2]
                pview, bank = p4view(j)
                for c in range(NCH):
                    lhs = buf[:, c * 512 + (j % 4) * 128:c * 512 + (j % 4) * 128 + 128]
                    em.op("tensor",
                          lambda e, lhs=lhs, c=c, pview=pview: e.matmul(
                              pview, lhs, xhat[:, c * T:(c + 1) * T],
                              start=(c == 0), stop=(c == NCH - 1)),
                          reads=[f"w1b{g % 2}", f"xh{c}"], writes=bank)
                em.op("scalar",
                      lambda e, j=j, pview=pview: e.activation(
                          h[:, j * T:(j + 1) * T], pview, AF.Relu, bias=b1s[:, j:j + 1]),
                      reads=bank + ["b1s"], writes=[f"hh{j % 4}"])
                if glob:
                    gview = SA[:, 0:16] if j % 2 == 0 else SA[:, 512:528]
                    gbank = bk(0 + j % 2)
                    for c in range(NCH):
                        lhs = buf[:, c * 512 + (j % 4) * 128:c * 512 + (j % 4) * 128 + 128]
                        em.op("tensor",
                              lambda e, lhs=lhs, c=c, gview=gview: e.matmul(
                                  gview, lhs, xghat[:, c * 16:(c + 1) * 16],
                                  start=(c == 0), stop=(c == NCH - 1)),
                              reads=[f"w1b{g % 2}", "xghat"], writes=gbank)
                    em.op("scalar",
                          lambda e, j=j, gview=gview: e.activation(
                              hg[:, j * 16:(j + 1) * 16], gview, AF.Relu,
                              bias=b1s[:, j:j + 1]),
                          reads=gbank + ["b1s"], writes=["hg"])
                if j % 4 == 3 and g + 2 < 8:
                    w1dma(g + 2)
                if side_jobs:
                    for _ in range(2):
                        if side_jobs:
                            side_jobs.pop(0)()

        def ffn2(xr, xrname, glob):
            def w2dma(k):
                src = bass.AP(w2q_s.tensor, k * 8 * 128 * 512,
                              [[512, 128], [128 * 512, 8], [1, 512]])
                buf = w2b[k % 2]
                em.op("sync", lambda e, src=src, buf=buf: e.dma_start(
                          buf[:].rearrange("p (cg r) -> p cg r", cg=8), src),
                      reads=["w2q_s"], writes=[f"w2b{k % 2}"], sem=f"w2b{k % 2}")
            w2dma(0)
            w2dma(1)
            for k in range(NCH):
                buf = w2b[k % 2]
                pview, bank = p4view(k + 2)
                for c2 in range(NFC):
                    em.op("tensor",
                          lambda e, buf=buf, c2=c2, pview=pview: e.matmul(
                              pview, buf[:, c2 * 128:(c2 + 1) * 128],
                              h[:, c2 * T:(c2 + 1) * T], start=(c2 == 0), stop=False),
                          reads=[f"w2b{k % 2}", f"hh{c2 % 4}"], writes=bank)
                em.op("tensor",
                      lambda e, k=k, pview=pview: e.matmul(
                          pview, b2rb[0:1, k * 128:(k + 1) * 128], onesrow512[0:1, :],
                          start=False, stop=True),
                      reads=["b2rb", "onesrow512"], writes=bank)
                half = "lo" if k < 4 else "hi"
                em.op("vector",
                      lambda e, k=k, pview=pview: e.scalar_tensor_tensor(
                          xr[:, k * T:(k + 1) * T], pview, 0.5, xr[:, k * T:(k + 1) * T],
                          ALU.mult, ALU.add),
                      reads=bank + [f"xr_{half}"], writes=[f"xr_{half}"])
                if glob:
                    gview = SA[:, 0:16] if k % 2 == 0 else SA[:, 512:528]
                    gbank = bk(0 + k % 2)
                    for c2 in range(NFC):
                        em.op("tensor",
                              lambda e, buf=buf, c2=c2, gview=gview: e.matmul(
                                  gview, buf[:, c2 * 128:(c2 + 1) * 128],
                                  hg[:, c2 * 16:(c2 + 1) * 16],
                                  start=(c2 == 0), stop=False),
                              reads=[f"w2b{k % 2}", "hg"], writes=gbank)
                    em.op("tensor",
                          lambda e, k=k, gview=gview: e.matmul(
                              gview, b2rb[0:1, k * 128:(k + 1) * 128],
                              onesrow512[0:1, 0:16], start=False, stop=True),
                          reads=["b2rb", "onesrow512"], writes=gbank)
                    em.op("vector",
                          lambda e, k=k, gview=gview: e.scalar_tensor_tensor(
                              xgres[:, k * 16:(k + 1) * 16], gview, 0.5,
                              xgres[:, k * 16:(k + 1) * 16], ALU.mult, ALU.add),
                          reads=gbank + ["xgres"], writes=["xgres"])
                if k + 2 < NCH:
                    w2dma(k + 2)


        def ffn(xr, xrname, glob):
            ffn1(xr, xrname, glob)
            ffn2(xr, xrname, glob)

        def proj(glob):
            stream_q = [("q", 0), ("q", 1), ("k", 0), ("k", 1), ("v", 0), ("v", 1)]

            def pjdma(idx):
                pname, half = stream_q[idx]
                wscr = {"q": wq_s, "k": wk_s, "v": wv_s}[pname]
                bid = idx % 2
                src = bass.AP(wscr.tensor, half * 512,
                              [[NCH * D, 128], [D, NCH], [1, 512]])
                buf = pjb[bid]
                em.op("sync",
                      lambda e, src=src, buf=buf: e.dma_start(
                          buf[:].rearrange("p (c i) -> p c i", c=NCH), src),
                      reads=[f"{pname}_s"], writes=[f"pjb{bid}"], sem=f"pjb{bid}")
            pjdma(0)
            pjdma(1)
            bi = 0
            for pname, dstb, dstn in (("q", q_hd, "q_hd"), ("k", k_hd, "k_hd")):
                for half in range(2):
                    bid = bi % 2
                    buf = pjb[bid]
                    bi += 1
                    for hh in range(8):
                        hd = half * 8 + hh
                        pview, bank = p4view(hd)
                        pview = pview[0:64, :]
                        for c in range(NCH):
                            lhs = buf[:, c * 512 + hh * 64:c * 512 + hh * 64 + 64]
                            em.op("tensor",
                                  lambda e, lhs=lhs, c=c, pview=pview: e.matmul(
                                      pview, lhs, xhat[:, c * T:(c + 1) * T],
                                      start=(c == 0), stop=(c == NCH - 1)),
                                  reads=[f"pjb{bid}", f"xh{c}"], writes=bank)
                        em.op("scalar",
                              lambda e, hd=hd, pview=pview, dstb=dstb: e.activation(
                                  dstb[0:64, hd * T:(hd + 1) * T], pview, AF.Copy),
                              reads=bank, writes=[dstn])
                    if bi + 1 < 6:
                        pjdma(bi + 1)
            for half in range(2):
                bid = bi % 2
                buf = pjb[bid]
                bi += 1
                for blk in range(T // 64):
                    pview = PSB3[:, 0:T] if blk % 2 == 0 else PSB3[:, T:2 * T]
                    bank = bk(6 + blk % 2)
                    for c in range(NCH):
                        em.op("tensor",
                              lambda e, buf=buf, c=c, blk=blk, pview=pview: e.matmul(
                                  pview[0:64, :],
                                  xhat[:, c * T + blk * 64:c * T + blk * 64 + 64],
                                  buf[:, c * 512:(c + 1) * 512],
                                  start=(c == 0), stop=(c == NCH - 1)),
                              reads=[f"pjb{bid}", f"xh{c}"], writes=bank)
                    em.op("vector",
                          lambda e, blk=blk, half=half, pview=pview: e.tensor_copy(
                              vaug[0:64, blk * D + half * 512:blk * D + half * 512 + 512],
                              pview[0:64, :]),
                          reads=bank, writes=["vaug"])
                if bi + 1 < 6:
                    pjdma(bi + 1)
                if glob:
                    gview = PSB2[:, 0:T]
                    gbank = bk(4)
                    for c in range(NCH):
                        em.op("tensor",
                              lambda e, buf=buf, c=c, gview=gview: e.matmul(
                                  gview[64:80, :], xghat[:, c * 16:(c + 1) * 16],
                                  buf[:, c * 512:(c + 1) * 512],
                                  start=(c == 0), stop=(c == NCH - 1),
                                  tile_position=(0, 64)),
                              reads=[f"pjb{bid}", "xghat"], writes=gbank)
                    for blk in range(T // 64):
                        em.op("vector",
                              lambda e, blk=blk, half=half, gview=gview: e.tensor_copy(
                                  vaug[64:80,
                                       blk * D + half * 512:blk * D + half * 512 + 512],
                                  gview[64:80, :]),
                              reads=gbank, writes=["vaug"])
            if glob:
                kb = []
                for half in range(2):
                    bid = bi % 2
                    buf = pjb[bid]
                    bi += 1
                    src = bass.AP(wk_s.tensor, half * 4096,
                                  [[NCH * D, 128], [1, 4096]])
                    em.op("sync",
                          lambda e, src=src, buf=buf: e.dma_start(buf[:, 0:4096], src),
                          reads=["k_s"], writes=[f"pjb{bid}"], sem=f"pjb{bid}")
                    kb.append((buf, bid))
                for hd in range(H):
                    gview = PSB3[0:64, 0:16] if hd % 2 == 0 else PSB3[0:64, 512:528]
                    gbank = bk(6 + hd % 2)
                    for c in range(NCH):
                        buf, bid = kb[c // 4]
                        lhs = buf[:, (c % 4) * 1024 + hd * 64:(c % 4) * 1024 + hd * 64 + 64]
                        em.op("tensor",
                              lambda e, lhs=lhs, gview=gview, c=c: e.matmul(
                                  gview, lhs, xghat[:, c * 16:(c + 1) * 16],
                                  start=(c == 0), stop=(c == NCH - 1)),
                              reads=[f"pjb{bid}", "xghat"], writes=gbank)
                    em.op("scalar",
                          lambda e, hd=hd, gview=gview: e.activation(
                              kg_hd[0:64, hd * 16:(hd + 1) * 16], gview, AF.Copy),
                          reads=gbank, writes=["kg_hd"])

        def attn(t, xr, xrname):
            halves = [(p, g) for p in range(NT) for g in range(2)]

            def scores_stage(k):
                p, g = halves[k]
                hp = k % 2
                Sreg = SA if hp == 0 else SB
                sbanks = bk(2 * hp, 2 * hp + 1)
                for hh in range(8):
                    hd = g * 8 + hh
                    for m in range(2):
                        qb = 2 * p + m
                        qs = hd * T + qb * 64
                        em.op("tensor",
                              lambda e, qs=qs, hh=hh, m=m, Sreg=Sreg: e.matmul(
                                  Sreg[64 * m:64 * m + 64, 128 * hh:128 * hh + 64],
                                  q_hd[0:64, qs:qs + 64],
                                  k_hd[0:64, qs:qs + 64],
                                  start=True, stop=True,
                                  tile_position=(0, 64 * m)),
                              reads=["q_hd", "k_hd"], writes=sbanks)
                        em.op("tensor",
                              lambda e, qs=qs, hh=hh, m=m, hd=hd, Sreg=Sreg: e.matmul(
                                  Sreg[64 * m:64 * m + 64,
                                       128 * hh + 64:128 * hh + 80],
                                  q_hd[0:64, qs:qs + 64],
                                  kg_hd[0:64, hd * 16:hd * 16 + 16],
                                  start=True, stop=True,
                                  tile_position=(0, 64 * m)),
                              reads=["q_hd", "kg_hd"], writes=sbanks)

            def smx_stage(k):
                p, g = halves[k]
                hp = k % 2
                hq = k % 3
                Sreg = SA if hp == 0 else SB
                sbanks = bk(2 * hp, 2 * hp + 1)
                nmv = negm[:, 8 * hp:8 * hp + 8]
                smv = sums[:, 8 * hp:8 * hp + 8]
                esv = esrc[:, 640 * hp:640 * hp + 640]
                psv = psb[:, 640 * hq:640 * hq + 640]
                s3d = Sreg[:].rearrange("p (h k) -> p h k", h=8)[:, :, 0:80]
                em.op("vector",
                      lambda e, s3d=s3d, nmv=nmv: e.tensor_reduce(
                          nmv, s3d, axis=AX.X, op=ALU.max, negate=True),
                      reads=sbanks, writes=[f"negm{hp}"])
                nb = nmv.rearrange("p h -> p h ()").broadcast_to([128, 8, 80])
                em.op("vector",
                      lambda e, s3d=s3d, nb=nb, esv=esv: e.tensor_tensor(
                          esv.rearrange("p (h k) -> p h k", h=8), s3d, nb, ALU.add),
                      reads=sbanks + [f"negm{hp}"], writes=[f"esrc{hp}"])
                em.op("scalar",
                      lambda e, psv=psv, esv=esv: e.activation(psv, esv, AF.Exp),
                      reads=[f"esrc{hp}"], writes=[f"psb{hq}"])
                mask = maskq0 if (t == 0 and p == 0) else maskstd
                mname = "maskq0" if (t == 0 and p == 0) else "maskstd"
                mb = mask[:].rearrange("p k -> p () k").broadcast_to([128, 8, 80])
                em.op("gpsimd",
                      lambda e, mb=mb, psv=psv: e.tensor_tensor(
                          psv.rearrange("p (h k) -> p h k", h=8),
                          psv.rearrange("p (h k) -> p h k", h=8), mb, ALU.mult),
                      reads=[f"psb{hq}", mname], writes=[f"psb{hq}"])
                em.op("vector",
                      lambda e, psv=psv, smv=smv: e.tensor_reduce(
                          smv, psv.rearrange("p (h k) -> p h k", h=8),
                          axis=AX.X, op=ALU.add),
                      reads=[f"psb{hq}"], writes=[f"sums{hp}"])
                em.op("vector",
                      lambda e, hp=hp, smv=smv: e.reciprocal(
                          rinvf[:, 8 * hp:8 * hp + 8], smv),
                      reads=[f"sums{hp}"], writes=[f"rinvf{hp}"])
                em.op("vector",
                      lambda e, hp=hp: e.tensor_copy(rinvb[:, 8 * hp:8 * hp + 8],
                                                     rinvf[:, 8 * hp:8 * hp + 8]),
                      reads=[f"rinvf{hp}"], writes=[f"rinvb{hp}"])
                rb = rinvb[:, 8 * hp:8 * hp + 8].rearrange(
                    "p h -> p h ()").broadcast_to([128, 8, 80])
                em.op("gpsimd",
                      lambda e, rb=rb, psv=psv: e.tensor_tensor(
                          psv.rearrange("p (h k) -> p h k", h=8),
                          psv.rearrange("p (h k) -> p h k", h=8), rb, ALU.mult),
                      reads=[f"psb{hq}", f"rinvb{hp}"], writes=[f"psb{hq}"])

            def pt_stage(k):
                p, g = halves[k]
                hp = k % 2
                hq = k % 3
                psv = psb[:, 640 * hq:640 * hq + 640]
                ph = hp
                ptbank = bk(4 + ph)
                for hh in range(8):
                    em.op("tensor",
                          lambda e, hh=hh, ph=ph, psv=psv: e.transpose(
                              PSB2_bf[0:80, ph * 1024 + 128 * hh:
                                      ph * 1024 + 128 * hh + 128],
                              psv[:, 80 * hh:80 * hh + 80], ident[:]),
                          reads=[f"psb{hq}", "ident"], writes=ptbank)
                em.op("scalar",
                      lambda e, ph=ph: e.copy(
                          ptsb[0:80, ph * 1024:ph * 1024 + 1024],
                          PSB2_bf[0:80, ph * 1024:ph * 1024 + 1024]),
                      reads=ptbank, writes=[f"ptsb{ph}"])

            def av_stage(k):
                p, g = halves[k]
                hp = k % 2
                Sreg = SA if hp == 0 else SB
                sbanks = bk(2 * hp, 2 * hp + 1)
                ph = hp
                avbank = bk(6 + hp)
                av = PSB3[:, hp * 512:hp * 512 + 512]
                for hh in range(8):
                    hd = g * 8 + hh
                    cw = hh // 2
                    ob = 64 * (hd % 2)
                    for m in range(2):
                        qb = 2 * p + m
                        em.op("tensor",
                              lambda e, hd=hd, hh=hh, m=m, qb=qb, cw=cw, ob=ob,
                              av=av, ph=ph: e.matmul(
                                  av[ob:ob + 64,
                                     128 * cw + 64 * m:128 * cw + 64 * m + 64],
                                  vaug[0:80, qb * D + 64 * hd:qb * D + 64 * hd + 64],
                                  ptsb[0:80, ph * 1024 + 128 * hh + 64 * m:
                                       ph * 1024 + 128 * hh + 64 * m + 64],
                                  start=True, stop=True, tile_position=(0, ob)),
                              reads=["vaug", f"ptsb{ph}"], writes=avbank)
                em.op("scalar",
                      lambda e, g=g, p=p, av=av: e.copy(
                          bass.AP(attnf, (4 * g) * T + p * 128,
                                  [[NCH * T, 128], [T, 4], [1, 128]]),
                          av.rearrange("p (c q) -> p c q", c=4)),
                      reads=avbank, writes=["attnf"])


            def ptav_stage(k):
                p, g = halves[k]
                hp = k % 2
                Sreg = SA if hp == 0 else SB
                sbanks = bk(2 * hp, 2 * hp + 1)
                psv = psb[:, 640 * hp:640 * hp + 640]
                ph = hp
                ptbank = bk(4 + ph)
                for hh in range(8):
                    em.op("tensor",
                          lambda e, hh=hh, ph=ph, psv=psv: e.transpose(
                              PSB2_bf[0:80, ph * 1024 + 128 * hh:
                                      ph * 1024 + 128 * hh + 128],
                              psv[:, 80 * hh:80 * hh + 80], ident[:]),
                          reads=[f"psb{hp}", "ident"], writes=ptbank)
                em.op("scalar",
                      lambda e, ph=ph: e.copy(
                          ptsb[0:80, ph * 1024:ph * 1024 + 1024],
                          PSB2_bf[0:80, ph * 1024:ph * 1024 + 1024]),
                      reads=ptbank, writes=[f"ptsb{ph}"])
                avbank = bk(6 + hp)
                av = PSB3[:, hp * 512:hp * 512 + 512]
                for hh in range(8):
                    hd = g * 8 + hh
                    cw = hh // 2
                    ob = 64 * (hd % 2)
                    for m in range(2):
                        qb = 2 * p + m
                        em.op("tensor",
                              lambda e, hd=hd, hh=hh, m=m, qb=qb, cw=cw, ob=ob,
                              av=av, ph=ph: e.matmul(
                                  av[ob:ob + 64,
                                     128 * cw + 64 * m:128 * cw + 64 * m + 64],
                                  vaug[0:80, qb * D + 64 * hd:qb * D + 64 * hd + 64],
                                  ptsb[0:80, ph * 1024 + 128 * hh + 64 * m:
                                       ph * 1024 + 128 * hh + 64 * m + 64],
                                  start=True, stop=True, tile_position=(0, ob)),
                              reads=["vaug", f"ptsb{ph}"], writes=avbank)
                em.op("scalar",
                      lambda e, g=g, p=p, av=av: e.copy(
                          bass.AP(attnf, (4 * g) * T + p * 128,
                                  [[NCH * T, 128], [T, 4], [1, 128]]),
                          av.rearrange("p (c q) -> p c q", c=4)),
                      reads=avbank, writes=["attnf"])


            NH = len(halves)
            for k in range(NH):
                scores_stage(k)
                smx_stage(k)
                if k >= 2:
                    pt_stage(k - 2)
                if k >= 3:
                    av_stage(k - 3)
            pt_stage(NH - 2)
            av_stage(NH - 3)
            pt_stage(NH - 1)
            av_stage(NH - 2)
            av_stage(NH - 1)
            for half in range(2):
                src = bass.AP(wo_s.tensor, half * 512, [[NCH * D, 128], [D, NCH], [1, 512]])
                buf = pjb[half]
                em.op("sync",
                      lambda e, src=src, buf=buf: e.dma_start(
                          buf[:].rearrange("p (c i) -> p c i", c=NCH), src),
                      reads=["o_s"], writes=[f"pjb{half}"], sem=f"pjb{half}")
            for jc in range(NCH):
                half = jc // 4
                buf = pjb[half]
                pview, bank = p4view(jc)
                for c in range(NCH):
                    lhs = buf[:, c * 512 + (jc % 4) * 128:c * 512 + (jc % 4) * 128 + 128]
                    em.op("tensor",
                          lambda e, lhs=lhs, c=c, pview=pview: e.matmul(
                              pview, lhs, attnf[:, c * T:(c + 1) * T],
                              start=(c == 0), stop=(c == NCH - 1)),
                          reads=[f"pjb{half}", "attnf"], writes=bank)
                rhalf = "lo" if jc < 4 else "hi"
                em.op("vector",
                      lambda e, jc=jc, pview=pview: e.scalar_tensor_tensor(
                          xr[:, jc * T:(jc + 1) * T], pview, 1.0,
                          xr[:, jc * T:(jc + 1) * T], ALU.mult, ALU.add),
                      reads=bank + [f"xr_{rhalf}"], writes=[f"xr_{rhalf}"])

        def xload(t, hf):
            src = bass.AP(xfm.tensor, (hf * 4) * 128 * TOK + t * T,
                          [[TOK, 128], [128 * TOK, 4], [1, T]])
            xr = xres[0]
            dma(lambda e, src=src, xr=xr, hf=hf: e.dma_start(
                xr[:, hf * 4 * T:(hf * 4 + 4) * T].rearrange("p (c k) -> p c k", c=4),
                src),
                writes=[f"xr_{'lo' if hf == 0 else 'hi'}"], sem=f"xl{hf}")

        dma(lambda e: e.dma_start(b1s[:], b1in), writes=["b1s"], sem="const", batch=True)
        dma(lambda e: e.dma_start(ident[:], identin), writes=["ident"], sem="const", batch=True)
        dma(lambda e: e.dma_start(onesmean[:], onesmean_in), writes=["onesmean"], sem="const", batch=True)
        dma(lambda e: e.dma_start(onesrow[:], onesrow_in), writes=["onesrow"], sem="const", batch=True)
        dma(lambda e: e.dma_start(onescol[:], onescol_in), writes=["onescol"], sem="const", batch=True)
        dma(lambda e: e.dma_start(onesrow512[:], onesrow512_in), writes=["onesrow512"], sem="const", batch=True)
        dma(lambda e: e.dma_start(maskstd[:], maskstd_in), writes=["maskstd"], sem="const", batch=True)
        dma(lambda e: e.dma_start(maskq0[:], maskq0_in), writes=["maskq0"], sem="const", batch=True)
        dma(lambda e: e.dma_start(rowm[0:1, 0:512], b2row[0:1, 0:512]), writes=["rowm"], sem="const", batch=True)
        dma(lambda e: e.dma_start(rows2[0:1, 0:512], b2row[0:1, 512:1024]), writes=["rows2"], sem="const", batch=True)
        em.settle(["b1s", "ident", "onesmean", "onesrow", "onescol", "onesrow512", "maskstd", "maskq0", "esel", "rowm", "rows2"], "const")
        em.op("scalar", lambda e: e.activation(b2rb[0:1, 0:512], rowm[0:1, 0:512], AF.Copy),
              reads=["rowm"], writes=["b2rb"])
        em.op("scalar", lambda e: e.activation(b2rb[0:1, 512:1024], rows2[0:1, 0:512], AF.Copy),
              reads=["rows2"], writes=["b2rb"])

        xload(0, 0)
        xload(0, 1)
        gsrc = bass.AP(xgin.tensor, 0, [[16, 128], [128 * 16, NCH], [1, 16]])
        dma(lambda e, gsrc=gsrc: e.dma_start(
            xgres[:].rearrange("p (c k) -> p c k", c=NCH), gsrc),
            writes=["xgres"], sem="xg")
        ln(xres[0], "xres0", True)

        for c in range(NCH):
            stg = wstage[c % 2]
            src = w1t[c * 128:(c + 1) * 128, :]
            dma(lambda e, stg=stg, src=src: e.dma_start(stg, src), writes=[f"wst{c % 2}"],
                sem=f"stg{c % 2}")
            em.op("vector",
                  lambda e, stg=stg, c=c: e.tensor_reduce(
                      gacc1[:, c:c + 1], stg, axis=AX.X, op=ALU.add,
                      apply_absolute_value=True),
                  reads=[f"wst{c % 2}"], writes=["gacc1"])
        em.op("tensor", lambda e: e.matmul(SA[0:1, 0:NCH], onescol[:], gacc1[:],
                                           start=True, stop=True),
              reads=["gacc1", "onescol"], writes=bk(0))
        em.op("vector", lambda e: e.tensor_reduce(growtot[0:1, 0:1], SA[0:1, 0:NCH],
                                                  axis=AX.X, op=ALU.add),
              reads=bk(0), writes=["growtot"])
        em.op("scalar", lambda e: e.activation(throw[0:1, 0:1], growtot[0:1, 0:1], AF.Copy,
                                               scale=0.5 / (D * DFF), bias=0.5 * QEPS),
              reads=["growtot"], writes=["throw"])
        em.op("scalar", lambda e: e.activation(nthrow[0:1, 0:1], growtot[0:1, 0:1], AF.Copy,
                                               scale=-0.5 / (D * DFF), bias=-0.5 * QEPS),
              reads=["growtot"], writes=["nthrow"])
        em.op("tensor", lambda e: e.matmul(SA[:, 0:1], onesrow[:], throw[0:1, 0:1],
                                           start=True, stop=True),
              reads=["throw", "onesrow"], writes=bk(0))
        em.op("tensor", lambda e: e.matmul(SA[:, 512:513], onesrow[:], nthrow[0:1, 0:1],
                                           start=True, stop=True),
              reads=["nthrow", "onesrow"], writes=bk(1))
        em.op("scalar", lambda e: e.copy(thrc[:, 0:1], SA[:, 0:1]), reads=bk(0),
              writes=["thrc"])
        em.op("scalar", lambda e: e.copy(nthrc[:, 0:1], SA[:, 512:513]), reads=bk(1),
              writes=["nthrc"])

        for c in range(NCH):
            stg = wstage[c % 2]
            src = w1t[c * 128:(c + 1) * 128, :]
            ia, ib = (2 * c) % 3, (2 * c + 1) % 3
            ta, tb = tmps3[ia], tmps3[ib]
            dma(lambda e, stg=stg, src=src: e.dma_start(stg, src),
                reads=[f"w1q{c % 2}"], writes=[f"wst{c % 2}"], sem=f"stg{c % 2}")
            em.op("vector",
                  lambda e, stg=stg, ta=ta: e.tensor_scalar(
                      ta, stg, thrc[:, 0:1], None, ALU.is_gt),
                  reads=[f"wst{c % 2}", "thrc"], writes=[f"tmp{ia}"] + TMPAL[ia])
            em.op("vector",
                  lambda e, stg=stg, tb=tb: e.tensor_scalar(
                      tb, stg, nthrc[:, 0:1], None, ALU.is_lt),
                  reads=[f"wst{c % 2}", "nthrc"], writes=[f"tmp{ib}"] + TMPAL[ib])
            em.op("vector",
                  lambda e, ta=ta, tb=tb: e.tensor_tensor(ta, ta, tb, ALU.subtract),
                  reads=[f"tmp{ia}", f"tmp{ib}"], writes=[f"tmp{ia}", f"wst{c % 2}done"])
            dst = bass.AP(w1q_s.tensor, c * 128 * 512,
                          [[512, 128], [NCH * 128 * 512, 8], [1, 512]])
            em.op("scalar",
                  lambda e, ta=ta, dst=dst: e.dma_start(
                      dst, ta.rearrange("p (jg r) -> p jg r", jg=8)),
                  reads=[f"tmp{ia}"], writes=["w1q_s", f"w1q{c % 2}"],
                  sem="w1qs", batch=True)

        vaug32 = vaug.bitcast(F32)
        vstg = [vaug32[:, 0:2048], vaug32[:, 2048:4096]]
        w2jobs = []

        def w2p1_job(i):
            stg = vstg[i % 2]
            srcw = bass.AP(w2t.tensor, (2 * i) * 128 * D,
                           [[D, 128], [128 * D, 2], [1, D]])
            dma(lambda e, stg=stg, srcw=srcw: e.dma_start(
                stg.rearrange("p (c e) -> p c e", c=2), srcw),
                writes=[f"w2stg{i % 2}"], sem=f"stg2{i % 2}")
            em.op("vector",
                  lambda e, stg=stg, i=i: e.tensor_reduce(
                      gacc2[:, i:i + 1], stg, axis=AX.X, op=ALU.add,
                      apply_absolute_value=True),
                  reads=[f"w2stg{i % 2}"], writes=["gacc2"])

        def w2gamma_job():
            em.op("tensor", lambda e: e.matmul(SA[0:1, 0:16], onescol[:], gacc2[:],
                                               start=True, stop=True),
                  reads=["gacc2", "onescol"], writes=bk(0))
            em.op("vector", lambda e: e.tensor_reduce(growtot[0:1, 1:2], SA[0:1, 0:16],
                                                      axis=AX.X, op=ALU.add),
                  reads=bk(0), writes=["growtot"])
            em.op("scalar", lambda e: e.activation(throw[0:1, 1:2], growtot[0:1, 1:2],
                                                   AF.Copy, scale=0.5 / (D * DFF),
                                                   bias=0.5 * QEPS),
                  reads=["growtot"], writes=["throw"])
            em.op("scalar", lambda e: e.activation(nthrow[0:1, 1:2], growtot[0:1, 1:2],
                                                   AF.Copy, scale=-0.5 / (D * DFF),
                                                   bias=-0.5 * QEPS),
                  reads=["growtot"], writes=["nthrow"])
            em.op("tensor", lambda e: e.matmul(SA[:, 0:1], onesrow[:], throw[0:1, 1:2],
                                               start=True, stop=True),
                  reads=["throw", "onesrow"], writes=bk(0))
            em.op("tensor", lambda e: e.matmul(SA[:, 512:513], onesrow[:],
                                               nthrow[0:1, 1:2], start=True, stop=True),
                  reads=["nthrow", "onesrow"], writes=bk(1))
            em.op("scalar", lambda e: e.copy(thrc[:, 1:2], SA[:, 0:1]), reads=bk(0),
                  writes=["thrc"])
            em.op("scalar", lambda e: e.copy(nthrc[:, 1:2], SA[:, 512:513]), reads=bk(1),
                  writes=["nthrc"])

        def w2p2_job(i):
            stg = vstg[i % 2]
            srcw = bass.AP(w2t.tensor, (2 * i) * 128 * D,
                           [[D, 128], [128 * D, 2], [1, D]])
            isl = i % 3
            slot = tmps3[isl]
            ta = slot[:, 0:2048].rearrange("p (k c e) -> p k c e", k=NCH, c=2)
            tb = slot[:, 2048:4096].rearrange("p (k c e) -> p k c e", k=NCH, c=2)
            sview = stg.rearrange("p (c k e) -> p k c e", c=2, k=NCH)
            dma(lambda e, stg=stg, srcw=srcw: e.dma_start(
                stg.rearrange("p (c e) -> p c e", c=2), srcw),
                reads=[f"w2w{i % 2}"], writes=[f"w2stg{i % 2}"], sem=f"stg2{i % 2}")
            em.op("vector",
                  lambda e, sview=sview, ta=ta: e.tensor_scalar(
                      ta, sview, thrc[:, 1:2], None, ALU.is_gt),
                  reads=[f"w2stg{i % 2}", "thrc"], writes=[f"tmp{isl}"] + TMPAL[isl])
            em.op("vector",
                  lambda e, sview=sview, tb=tb: e.tensor_scalar(
                      tb, sview, nthrc[:, 1:2], None, ALU.is_lt),
                  reads=[f"w2stg{i % 2}", "nthrc"],
                  writes=[f"tmp{isl}", f"w2w{i % 2}"])
            em.op("vector",
                  lambda e, ta=ta, tb=tb: e.tensor_tensor(ta, ta, tb, ALU.subtract),
                  reads=[f"tmp{isl}"], writes=[f"tmp{isl}"])
            dstw = bass.AP(w2q_s.tensor, (i // 2) * 128 * 512 + (i % 2) * 256,
                           [[512, 128], [8 * 128 * 512, NCH], [1, 256]])
            em.op("scalar",
                  lambda e, slot=slot, dstw=dstw: e.dma_start(
                      dstw, slot[:, 0:2048].rearrange("p (k r) -> p k r", k=NCH)),
                  reads=[f"tmp{isl}"], writes=["w2q_s"], sem="w2qs", batch=True)

        for i in range(16):
            w2jobs.append(lambda i=i: w2p1_job(i))
        w2jobs.append(w2gamma_job)
        for i in range(16):
            w2jobs.append(lambda i=i: w2p2_job(i))


        ffn1(xres[0], "xres0", True, side_jobs=w2jobs)
        while w2jobs:
            w2jobs.pop(0)()

        ffn2(xres[0], "xres0", True)


        qi = 0
        for name, win, wscr, scale in (("q", wqt, wq_s, 0.125), ("k", wkt, wk_s, 1.0),
                                       ("v", wvt, wv_s, 1.0), ("o", wot, wo_s, 1.0)):
            for half in range(2):
                srcw = bass.AP(win.tensor, (half * 4) * 128 * D,
                               [[D, 128], [128 * D, 4], [1, D]])
                isl = qi % 3
                slot = tmps3[isl]
                em.op("scalar", lambda e, srcw=srcw: e.dma_start(
                    vaug32[:].rearrange("p (c i) -> p c i", c=4), srcw),
                    writes=["w2stg0", "w2stg1"], sem="stg20")
                em.op("scalar",
                      lambda e, scale=scale, slot=slot: e.activation(
                          slot, vaug32[:], AF.Copy, scale=scale),
                      reads=["w2stg0", "w2stg1"],
                      writes=[f"tmp{isl}"] + TMPAL[isl])
                em.op("scalar",
                      lambda e, wscr=wscr, half=half, slot=slot: e.dma_start(
                          bass.AP(wscr.tensor, half * 4096, [[8192, 128], [1, 4096]]),
                          slot),
                      reads=[f"tmp{isl}"], writes=[f"{name}_s"], sem="qkvs")
                qi += 1

        em.op("vector", lambda e: e.engine_nop(),
              writes=["xb", "sq", "wst0", "wst1", "tmp0", "tmp1",
                      "tmp2", "w2stg0", "w2stg1", "vaug", "attnf"])

        import os
        dbg = os.environ.get("MK_DEBUG_PHASES", "")
        for t in range(NT):
            xr = xres[0]
            xrname = "xres0"
            if t > 0:
                xload(t, 0)
                xload(t, 1)
            glob = (t == 0)
            if t > 0:
                ln(xr, xrname, glob)
            if "noffn1" not in dbg and t > 0:
                ffn(xr, xrname, glob)
            if "s1" not in dbg:
                ln(xr, xrname, glob)
                proj(glob)
                if "noattn" not in dbg:
                    attn(t, xr, xrname)
                if "s2" not in dbg:
                    ln(xr, xrname, False)
                    ffn(xr, xrname, False)
            for hf in range(2):
                dst = bass.AP(yout.tensor, (hf * 4) * 128 * TOK + t * T,
                              [[TOK, 128], [128 * TOK, 4], [1, T]])
                dma(lambda e, dst=dst, xr=xr, hf=hf: e.dma_start(
                    dst, xr[:, hf * 4 * T:(hf * 4 + 4) * T].rearrange(
                        "p (c k) -> p c k", c=4)),
                    reads=[f"xr_{'lo' if hf == 0 else 'hi'}"],
                    writes=[f"yout{t}{hf}"], sem=f"ys{hf}")

        em.final_wait(["ys0", "ys1"])

        for e in ENGINES:
            if e != "sync":
                sems[e] = E(nc.semaphore(f"sem_{e}"))
        for s in em.dma_sems:
            sems[s] = E(nc.semaphore(f"sem_{s}"))
        block = E(nc.Block())

        def emit(engobj, ename):
            for fn, waits, inc, semname in em.ops[ename]:
                for (e2, v) in waits:
                    engobj.wait_ge(sems[e2], v)
                if fn is not None:
                    ins = fn(engobj)
                    if inc:
                        ins.then_inc(sems[semname], inc)

        @block.sync
        def _(sync):
            emit(sync, "sync")

        @block.tensor
        def _(tensor):
            emit(tensor, "tensor")

        @block.vector
        def _(vector):
            emit(vector, "vector")

        @block.scalar
        def _(scalar):
            emit(scalar, "scalar")

        @block.gpsimd
        def _(gpsimd):
            emit(gpsimd, "gpsimd")

    return nc


_NC_CACHE = None


def kernel(L0_attn_hidden_states, L0_attn_Wq, L0_attn_Wk, L0_attn_Wv, L0_attn_Wo,
           L0_ffn_W1, L0_ffn_b1, L0_ffn_W2, L0_ffn_b2):
    global _NC_CACHE
    x = np.asarray(L0_attn_hidden_states, dtype=np.float32)
    bf = ml_dtypes.bfloat16
    w1tA = np.ascontiguousarray(np.asarray(L0_ffn_W1, dtype=np.float32).T)
    w2tA = np.ascontiguousarray(np.asarray(L0_ffn_W2, dtype=np.float32).T)
    wqtA = np.ascontiguousarray(np.asarray(L0_attn_Wq, dtype=np.float32).T)
    wktA = np.ascontiguousarray(np.asarray(L0_attn_Wk, dtype=np.float32).T)
    wvtA = np.ascontiguousarray(np.asarray(L0_attn_Wv, dtype=np.float32).T)
    wotA = np.ascontiguousarray(np.asarray(L0_attn_Wo, dtype=np.float32).T)
    b1r = np.ascontiguousarray(np.asarray(L0_ffn_b1, dtype=np.float32).reshape(NFC, 128).T)
    b2r = np.ascontiguousarray(np.asarray(L0_ffn_b2, dtype=np.float32)[None, :])
    ident = np.eye(128, dtype=bf)
    onesmean = np.full((128, 1), 1.0 / D, dtype=np.float32).astype(bf)
    onesrow = np.ones((1, 128), dtype=np.float32)
    onescol = np.ones((128, 1), dtype=np.float32)
    onesrow512 = np.ones((1, 512), dtype=bf)
    tril = np.tril(np.ones((64, 64), dtype=np.float32))
    mrow = np.concatenate([tril, np.ones((64, 16), np.float32)], 1)
    maskstd = np.concatenate([mrow, mrow], 0).astype(bf)
    eselA = np.zeros((16, D), dtype=np.float32)
    for hd in range(16):
        eselA[hd, hd * 64:(hd + 1) * 64] = 1.0
    eselA = eselA.astype(bf)

    in_maps = []
    for c in range(NCORE):
        b, half = c // 2, c % 2
        xfmA = np.ascontiguousarray(x[b, half * TOK:(half + 1) * TOK, :].T)
        xgA = np.ascontiguousarray(x[b, 0:16, :].T)
        gm = 1.0 if half == 1 else 0.0
        mq0 = np.concatenate([
            np.concatenate([tril, np.full((64, 16), gm, np.float32)], 1), mrow],
            0).astype(bf)
        in_maps.append({
            "xfm": xfmA, "xgin": xgA,
            "w1t": w1tA, "w2t": w2tA, "wqt": wqtA, "wkt": wktA, "wvt": wvtA, "wot": wotA,
            "b1in": b1r, "b2row": b2r,
            "identin": ident, "onesmean_in": onesmean, "onesrow_in": onesrow,
            "onescol_in": onescol, "onesrow512_in": onesrow512,
            "maskstd_in": maskstd, "maskq0_in": mq0, "esel_in": eselA,
        })

    if _NC_CACHE is None:
        _NC_CACHE = build_nc()
    res = run_bass_kernel_spmd(_NC_CACHE, in_maps, core_ids=list(range(NCORE)))

    out = np.empty((B, S, D), dtype=np.float32)
    for c in range(NCORE):
        b, half = c // 2, c % 2
        out[b, half * TOK:(half + 1) * TOK, :] = np.asarray(
            res.results[c]["y"], dtype=np.float32).T
    return out


# version 24
# speedup vs baseline: 1.0051x; 1.0051x over previous
import numpy as np
import ml_dtypes
from contextlib import ExitStack

import concourse.bass as bass
import concourse.mybir as mybir
from concourse.bass_utils import run_bass_kernel_spmd

F32 = mybir.dt.float32
BF16 = mybir.dt.bfloat16
AF = mybir.ActivationFunctionType
ALU = mybir.AluOpType
AX = mybir.AxisListType

B, S, D = 4, 4096, 1024
H, HD = 16, 64
DFF = 4096
NCORE = 8
TOK = 2048
T = 512
NT = TOK // T
NCH = D // 128
NFC = DFF // 128
EPS = 1e-5
QEPS = 1e-5

ENGINES = ("tensor", "vector", "scalar", "sync", "gpsimd")


class Em:

    def __init__(self):
        self.ops = {e: [] for e in ENGINES}
        self.count = {}
        self.last_w = {}
        self.readers = {}
        self.waited = {e: {} for e in ENGINES}
        self.dma_sems = {}

    def op(self, eng, fn, reads=(), writes=(), sem=None, batch=False):
        is_dma = sem is not None
        if is_dma:
            semname = sem
            if semname not in self.dma_sems:
                self.dma_sems[semname] = batch
        else:
            semname = eng
        need = {}

        def add(dep):
            s2, v = dep
            if s2 == semname and eng == "tensor" and not is_dma:
                return
            if v > need.get(s2, 0):
                need[s2] = v

        for r in reads:
            lw = self.last_w.get(r)
            if lw:
                add(lw)
        for w in writes:
            lw = self.last_w.get(w)
            if lw:
                add(lw)
            for rd in self.readers.get(w, ()):
                add(rd)
        if is_dma and not batch:
            prev = self.count.get(semname, 0)
            if prev > need.get(semname, 0):
                need[semname] = prev
        wd = self.waited[eng]
        waits = []
        for s2, v in sorted(need.items()):
            if v > wd.get(s2, 0):
                waits.append((s2, v))
                wd[s2] = v
        inc = 16 if is_dma else 1
        self.count[semname] = self.count.get(semname, 0) + inc
        me = (semname, self.count[semname])
        for r in reads:
            self.readers.setdefault(r, []).append(me)
        for w in writes:
            self.last_w[w] = me
            self.readers[w] = []
        self.ops[eng].append((fn, waits, inc, semname))

    def settle(self, resources, semname):
        me = (semname, self.count.get(semname, 0))
        for r in resources:
            self.last_w[r] = me
            self.readers[r] = []

    def final_wait(self, semnames):
        waits = [(s, self.count.get(s, 0)) for s in semnames]
        self.ops["scalar"].append((None, waits, 0, "scalar"))


def build_nc():
    nc = bass.Bass("TRN2", target_bir_lowering=False)

    xfm = nc.dram_tensor("xfm", [D, TOK], F32, kind="ExternalInput").ap()
    xgin = nc.dram_tensor("xgin", [D, 16], F32, kind="ExternalInput").ap()
    w1t = nc.dram_tensor("w1t", [D, DFF], F32, kind="ExternalInput").ap()
    w2t = nc.dram_tensor("w2t", [DFF, D], F32, kind="ExternalInput").ap()
    wqt = nc.dram_tensor("wqt", [D, D], F32, kind="ExternalInput").ap()
    wkt = nc.dram_tensor("wkt", [D, D], F32, kind="ExternalInput").ap()
    wvt = nc.dram_tensor("wvt", [D, D], F32, kind="ExternalInput").ap()
    wot = nc.dram_tensor("wot", [D, D], F32, kind="ExternalInput").ap()
    b1in = nc.dram_tensor("b1in", [128, NFC], F32, kind="ExternalInput").ap()
    b2row = nc.dram_tensor("b2row", [1, D], F32, kind="ExternalInput").ap()
    identin = nc.dram_tensor("identin", [128, 128], BF16, kind="ExternalInput").ap()
    onesmean_in = nc.dram_tensor("onesmean_in", [128, 1], BF16, kind="ExternalInput").ap()
    onesrow_in = nc.dram_tensor("onesrow_in", [1, 128], F32, kind="ExternalInput").ap()
    onescol_in = nc.dram_tensor("onescol_in", [128, 1], F32, kind="ExternalInput").ap()
    onesrow512_in = nc.dram_tensor("onesrow512_in", [1, 512], BF16, kind="ExternalInput").ap()
    maskstd_in = nc.dram_tensor("maskstd_in", [128, 80], BF16, kind="ExternalInput").ap()
    maskq0_in = nc.dram_tensor("maskq0_in", [128, 80], BF16, kind="ExternalInput").ap()
    esel_in = nc.dram_tensor("esel_in", [16, D], BF16, kind="ExternalInput").ap()
    yout = nc.dram_tensor("y", [D, TOK], F32, kind="ExternalOutput").ap()

    w1q_s = nc.dram_tensor("w1q_s", [NFC * 128 * NCH * 128], BF16, kind="Internal").ap()
    w2q_s = nc.dram_tensor("w2q_s", [NCH * 128 * NFC * 128], BF16, kind="Internal").ap()
    wq_s = nc.dram_tensor("wq_s", [128 * NCH * D], BF16, kind="Internal").ap()
    wk_s = nc.dram_tensor("wk_s", [128 * NCH * D], BF16, kind="Internal").ap()
    wv_s = nc.dram_tensor("wv_s", [128 * NCH * D], BF16, kind="Internal").ap()
    wo_s = nc.dram_tensor("wo_s", [128 * NCH * D], BF16, kind="Internal").ap()

    with ExitStack() as st:
        E = st.enter_context
        xres = [E(nc.sbuf_tensor("xres0", [128, NCH * T], F32))] * 2
        xsq = E(nc.sbuf_tensor("xsq", [128, 2 * NCH * T], BF16))
        xhat = E(nc.sbuf_tensor("xhat", [128, NCH * T], BF16))
        h = E(nc.sbuf_tensor("h", [128, NFC * T], BF16))
        q_hd = E(nc.sbuf_tensor("q_hd", [64, H * T], BF16))
        k_hd = E(nc.sbuf_tensor("k_hd", [64, H * T], BF16))
        vaug = E(nc.sbuf_tensor("vaug", [128, (T // 64) * D], BF16))
        kg_hd = E(nc.sbuf_tensor("kg_hd", [64, H * 16], BF16))
        attnf = E(nc.sbuf_tensor("attnf", [128, NCH * T], BF16))
        esrc = E(nc.sbuf_tensor("esrc", [128, 2 * 8 * 80], F32))
        psb = E(nc.sbuf_tensor("psb", [128, 3 * 8 * 80], BF16))
        ptsb = E(nc.sbuf_tensor("ptsb", [128, 2 * 8 * 128], BF16))
        negm = E(nc.sbuf_tensor("negm", [128, 16], F32))
        sums = E(nc.sbuf_tensor("sums", [128, 16], F32))
        rinvf = E(nc.sbuf_tensor("rinvf", [128, 16], F32))
        rinvb = E(nc.sbuf_tensor("rinvb", [128, 16], BF16))
        mb16 = E(nc.sbuf_tensor("mb16", [128, T], BF16))
        rb16 = E(nc.sbuf_tensor("rb16", [128, T], BF16))
        w1b = [E(nc.sbuf_tensor(f"w1b{i}", [128, 4 * NCH * 128], BF16)) for i in range(2)]
        w2b = [E(nc.sbuf_tensor(f"w2b{i}", [128, NFC * 128], BF16)) for i in range(2)]
        pjb = [E(nc.sbuf_tensor(f"pjb{i}", [128, NCH * 512], BF16)) for i in range(2)]
        xgres = E(nc.sbuf_tensor("xgres", [128, NCH * 16], F32))
        xgsq = E(nc.sbuf_tensor("xgsq", [128, 2 * NCH * 16], BF16))
        xghat = E(nc.sbuf_tensor("xghat", [128, NCH * 16], BF16))
        hg = E(nc.sbuf_tensor("hg", [128, NFC * 16], BF16))
        rowm = E(nc.sbuf_tensor("rowm", [1, T], F32))
        rowmb = E(nc.sbuf_tensor("rowmb", [1, T], BF16))
        rowrb = E(nc.sbuf_tensor("rowrb", [1, T], BF16))
        rowvar = E(nc.sbuf_tensor("rowvar", [1, T], F32))
        rows2 = rowvar
        rowr = E(nc.sbuf_tensor("rowr", [1, T], F32))
        growm = E(nc.sbuf_tensor("growm", [1, 16], F32))
        grows2 = E(nc.sbuf_tensor("grows2", [1, 16], F32))
        growvar = E(nc.sbuf_tensor("growvar", [1, 16], F32))
        growr = E(nc.sbuf_tensor("growr", [1, 16], F32))
        b1s = E(nc.sbuf_tensor("b1s", [128, NFC], F32))
        b2rb = E(nc.sbuf_tensor("b2rb", [1, D], BF16))
        ident = E(nc.sbuf_tensor("ident", [128, 128], BF16))
        onesmean = E(nc.sbuf_tensor("onesmean", [128, 1], BF16))
        onesrow = E(nc.sbuf_tensor("onesrow", [1, 128], F32))
        onescol = E(nc.sbuf_tensor("onescol", [128, 1], F32))
        onesrow512 = E(nc.sbuf_tensor("onesrow512", [1, 512], BF16))
        maskstd = E(nc.sbuf_tensor("maskstd", [128, 80], BF16))
        maskq0 = E(nc.sbuf_tensor("maskq0", [128, 80], BF16))
        gacc1 = E(nc.sbuf_tensor("gacc1", [128, NCH], F32))
        gacc2 = E(nc.sbuf_tensor("gacc2", [128, 16], F32))
        growtot = E(nc.sbuf_tensor("growtot", [1, 2], F32))
        throw = E(nc.sbuf_tensor("throw", [1, 2], F32))
        nthrow = E(nc.sbuf_tensor("nthrow", [1, 2], F32))
        thrc = E(nc.sbuf_tensor("thrc", [128, 2], F32))
        nthrc = E(nc.sbuf_tensor("nthrc", [128, 2], F32))
        SA = E(nc.psum_tensor("SA", [128, 1024], F32))
        SB = E(nc.psum_tensor("SB", [128, 1024], F32))
        PSB2 = E(nc.psum_tensor("PSB2", [128, 1024], F32))
        PSB3 = E(nc.psum_tensor("PSB3", [128, 1024], F32))
        SA_bf = SA.bitcast(BF16)
        SB_bf = SB.bitcast(BF16)
        PSB2_bf = PSB2.bitcast(BF16)
        h32 = h.bitcast(F32)
        wstage = [h32[:, 0:4096], h32[:, 4096:8192]]
        wstage_full = h32[:, 0:8192]
        tmps3 = [xsq[:, 0:4096], xsq[:, 4096:8192], attnf[:, 0:4096]]
        TMPAL = [["xb"], ["sq"], ["attnf"]]
        tmps = [xsq[:, 0:4096], xsq[:, 4096:8192]]
        tmp_full = xsq[:, 0:8192]

        em = Em()
        sems = {}
        P4 = [(PSB2, 0, 4), (PSB2, T, 5), (PSB3, 0, 6), (PSB3, T, 7)]

        def p4view(i, rows=128, width=T):
            tns, off, bankid = P4[i % 4]
            return tns[0:rows, off:off + width], [f"pb{bankid}"]

        def bk(*ids):
            return [f"pb{i}" for i in ids]

        def dma(fn, reads=(), writes=(), sem=None, batch=False):
            em.op("sync", fn, reads, writes, sem=sem, batch=batch)

        def ln(xr, xrname, glob):
            em.op("scalar", lambda e: e.activation(xsq[:, 0:2048], xr[:, 0:2048],
                                                   AF.Copy),
                  reads=["xr_lo"], writes=["xb", "xb_lo"])
            em.op("scalar", lambda e: e.activation(xsq[:, 2048:4096], xr[:, 2048:4096],
                                                   AF.Copy),
                  reads=["xr_hi"], writes=["xb", "xb_hi"])
            em.op("vector", lambda e: e.tensor_tensor(xsq[:, 4096:8192], xsq[:, 0:4096],
                                                      xsq[:, 0:4096], ALU.mult),
                  reads=["xb_lo", "xb_hi"], writes=["sq"])
            for c in range(NCH):
                em.op("tensor",
                      lambda e, c=c: e.matmul(SA[0:1, 0:T], onesmean[:],
                                              xsq[:, c * T:(c + 1) * T],
                                              start=(c == 0), stop=(c == NCH - 1)),
                      reads=["xb_lo" if c < 4 else "xb_hi", "onesmean"], writes=bk(0))
            for c in range(NCH):
                em.op("tensor",
                      lambda e, c=c: e.matmul(SA[0:1, 512:512 + T], onesmean[:],
                                              xsq[:, 4096 + c * T:4096 + (c + 1) * T],
                                              start=(c == 0), stop=(c == NCH - 1)),
                      reads=["sq", "onesmean"], writes=bk(1))
            em.op("vector", lambda e: e.tensor_copy(rowmb[:], SA[0:1, 0:T]),
                  reads=bk(0), writes=["rowmb"])
            em.op("vector", lambda e: e.tensor_tensor(SA[0:1, 0:T], rowmb[:], rowmb[:],
                                                      ALU.mult),
                  reads=["rowmb"], writes=bk(0))
            em.op("vector", lambda e: e.tensor_copy(rowvar[:], SA[0:1, 512:512 + T]),
                  reads=bk(1), writes=["rowvar"])
            em.op("vector", lambda e: e.tensor_tensor(rowvar[:], rowvar[:], SA[0:1, 0:T],
                                                      ALU.subtract),
                  reads=["rowvar"] + bk(0), writes=["rowvar"])
            em.op("vector", lambda e: e.tensor_scalar(rowvar[:], rowvar[:], 1.0, EPS,
                                                      ALU.mult, ALU.add),
                  reads=["rowvar"], writes=["rowvar"])
            em.op("scalar", lambda e: e.activation(rowr[:], rowvar[:], AF.Ln),
                  reads=["rowvar"], writes=["rowr"])
            em.op("scalar", lambda e: e.activation(rowrb[:], rowr[:], AF.Exp, scale=-0.5),
                  reads=["rowr"], writes=["rowrb"])
            em.op("tensor", lambda e: e.matmul(SB[:, 0:T], onesrow512[0:1, 0:128],
                                               rowmb[:], start=True, stop=True),
                  reads=["rowmb", "onesrow512"], writes=bk(2))
            em.op("tensor", lambda e: e.matmul(SB[:, 512:512 + T], onesrow512[0:1, 0:128],
                                               rowrb[:], start=True, stop=True),
                  reads=["rowrb", "onesrow512"], writes=bk(3))
            em.op("vector", lambda e: e.tensor_copy(mb16[:], SB[:, 0:T]),
                  reads=bk(2), writes=["mb16"])
            em.op("vector", lambda e: e.tensor_copy(rb16[:], SB[:, 512:512 + T]),
                  reads=bk(3), writes=["rb16"])
            for c in range(NCH):
                em.op("vector",
                      lambda e, c=c: e.tensor_tensor(xhat[:, c * T:(c + 1) * T],
                                                     xsq[:, c * T:(c + 1) * T],
                                                     mb16[:], ALU.subtract),
                      reads=["xb_lo" if c < 4 else "xb_hi", "mb16"],
                      writes=[f"xhs{c}"])
            for c in range(NCH):
                em.op("vector",
                      lambda e, c=c: e.tensor_tensor(xhat[:, c * T:(c + 1) * T],
                                                     xhat[:, c * T:(c + 1) * T],
                                                     rb16[:], ALU.mult),
                      reads=[f"xhs{c}", "rb16"], writes=[f"xh{c}", f"xhs{c}"])
            if glob:
                em.op("scalar", lambda e: e.activation(xgsq[:, 0:128], xgres[:], AF.Copy),
                      reads=["xgres"], writes=["gxb"])
                em.op("vector", lambda e: e.tensor_tensor(xgsq[:, 128:256], xgsq[:, 0:128],
                                                          xgsq[:, 0:128], ALU.mult),
                      reads=["gxb"], writes=["gsq"])
                for c in range(NCH):
                    em.op("tensor",
                          lambda e, c=c: e.matmul(PSB3[0:1, 0:16], onesmean[:],
                                                  xgsq[:, c * 16:(c + 1) * 16],
                                                  start=(c == 0), stop=(c == NCH - 1)),
                          reads=["gxb", "onesmean"], writes=bk(6))
                for c in range(NCH):
                    em.op("tensor",
                          lambda e, c=c: e.matmul(PSB3[0:1, 512:528], onesmean[:],
                                                  xgsq[:, 128 + c * 16:128 + (c + 1) * 16],
                                                  start=(c == 0), stop=(c == NCH - 1)),
                          reads=["gsq", "onesmean"], writes=bk(7))
                em.op("vector", lambda e: e.tensor_copy(growm[:], PSB3[0:1, 0:16]),
                      reads=bk(6), writes=["growm"])
                em.op("vector", lambda e: e.tensor_copy(grows2[:], PSB3[0:1, 512:528]),
                      reads=bk(7), writes=["grows2"])
                em.op("vector", lambda e: e.tensor_tensor(growvar[:], growm[:], growm[:],
                                                          ALU.mult),
                      reads=["growm"], writes=["growvar"])
                em.op("vector", lambda e: e.tensor_tensor(growvar[:], grows2[:], growvar[:],
                                                          ALU.subtract),
                      reads=["grows2", "growvar"], writes=["growvar"])
                em.op("vector", lambda e: e.tensor_scalar(growvar[:], growvar[:], 1.0, EPS,
                                                          ALU.mult, ALU.add),
                      reads=["growvar"], writes=["growvar"])
                em.op("scalar", lambda e: e.activation(growr[:], growvar[:], AF.Ln),
                      reads=["growvar"], writes=["growr"])
                em.op("scalar", lambda e: e.activation(growr[:], growr[:], AF.Exp, scale=-0.5),
                      reads=["growr"], writes=["growr"])
                em.op("tensor", lambda e: e.matmul(PSB3[:, 16:32], onesrow[:], growm[:],
                                                   start=True, stop=True),
                      reads=["growm", "onesrow"], writes=bk(6))
                em.op("tensor", lambda e: e.matmul(PSB3[:, 528:544], onesrow[:], growr[:],
                                                   start=True, stop=True),
                      reads=["growr", "onesrow"], writes=bk(7))
                mgb = PSB3[:, 16:32].rearrange("p k -> p () k").broadcast_to(
                    [128, NCH, 16])
                rgb = PSB3[:, 528:544].rearrange("p k -> p () k").broadcast_to(
                    [128, NCH, 16])
                em.op("vector",
                      lambda e, mgb=mgb: e.tensor_tensor(
                          xghat[:].rearrange("p (c k) -> p c k", c=NCH),
                          xgres[:].rearrange("p (c k) -> p c k", c=NCH), mgb, ALU.subtract),
                      reads=["xgres"] + bk(6), writes=["xghat"])
                em.op("vector",
                      lambda e, rgb=rgb: e.tensor_tensor(
                          xghat[:].rearrange("p (c k) -> p c k", c=NCH),
                          xghat[:].rearrange("p (c k) -> p c k", c=NCH), rgb, ALU.mult),
                      reads=["xghat"] + bk(7), writes=["xghat"])

        def ffn1(xr, xrname, glob, side_jobs=None):
            def w1dma(g):
                src = bass.AP(w1q_s.tensor, g * NCH * 128 * 512,
                              [[512, 128], [128 * 512, NCH], [1, 512]])
                buf = w1b[g % 2]
                em.op("sync",
                      lambda e, src=src, buf=buf: e.dma_start(
                          buf[:].rearrange("p (c r) -> p c r", c=NCH), src),
                      reads=["w1q_s"], writes=[f"w1b{g % 2}"], sem=f"w1b{g % 2}")
            w1dma(0)
            w1dma(1)
            for j in range(NFC):
                g = j // 4
                buf = w1b[g % 2]
                pview, bank = p4view(j)
                for c in range(NCH):
                    lhs = buf[:, c * 512 + (j % 4) * 128:c * 512 + (j % 4) * 128 + 128]
                    em.op("tensor",
                          lambda e, lhs=lhs, c=c, pview=pview: e.matmul(
                              pview, lhs, xhat[:, c * T:(c + 1) * T],
                              start=(c == 0), stop=(c == NCH - 1)),
                          reads=[f"w1b{g % 2}", f"xh{c}"], writes=bank)
                em.op("scalar",
                      lambda e, j=j, pview=pview: e.activation(
                          h[:, j * T:(j + 1) * T], pview, AF.Relu, bias=b1s[:, j:j + 1]),
                      reads=bank + ["b1s"], writes=[f"hh{j % 4}"])
                if glob:
                    gview = SA[:, 0:16] if j % 2 == 0 else SA[:, 512:528]
                    gbank = bk(0 + j % 2)
                    for c in range(NCH):
                        lhs = buf[:, c * 512 + (j % 4) * 128:c * 512 + (j % 4) * 128 + 128]
                        em.op("tensor",
                              lambda e, lhs=lhs, c=c, gview=gview: e.matmul(
                                  gview, lhs, xghat[:, c * 16:(c + 1) * 16],
                                  start=(c == 0), stop=(c == NCH - 1)),
                              reads=[f"w1b{g % 2}", "xghat"], writes=gbank)
                    em.op("scalar",
                          lambda e, j=j, gview=gview: e.activation(
                              hg[:, j * 16:(j + 1) * 16], gview, AF.Relu,
                              bias=b1s[:, j:j + 1]),
                          reads=gbank + ["b1s"], writes=["hg"])
                if j % 4 == 3 and g + 2 < 8:
                    w1dma(g + 2)
                if side_jobs:
                    for _ in range(2):
                        if side_jobs:
                            side_jobs.pop(0)()

        def ffn2(xr, xrname, glob, store_t=None):
            def w2dma(k):
                src = bass.AP(w2q_s.tensor, k * 8 * 128 * 512,
                              [[512, 128], [128 * 512, 8], [1, 512]])
                buf = w2b[k % 2]
                em.op("sync", lambda e, src=src, buf=buf: e.dma_start(
                          buf[:].rearrange("p (cg r) -> p cg r", cg=8), src),
                      reads=["w2q_s"], writes=[f"w2b{k % 2}"], sem=f"w2b{k % 2}")
            w2dma(0)
            w2dma(1)
            for k in range(NCH):
                buf = w2b[k % 2]
                pview, bank = p4view(k + 2)
                for c2 in range(NFC):
                    em.op("tensor",
                          lambda e, buf=buf, c2=c2, pview=pview: e.matmul(
                              pview, buf[:, c2 * 128:(c2 + 1) * 128],
                              h[:, c2 * T:(c2 + 1) * T], start=(c2 == 0), stop=False),
                          reads=[f"w2b{k % 2}", f"hh{c2 % 4}"], writes=bank)
                em.op("tensor",
                      lambda e, k=k, pview=pview: e.matmul(
                          pview, b2rb[0:1, k * 128:(k + 1) * 128], onesrow512[0:1, :],
                          start=False, stop=True),
                      reads=["b2rb", "onesrow512"], writes=bank)
                half = "lo" if k < 4 else "hi"
                em.op("vector",
                      lambda e, k=k, pview=pview: e.scalar_tensor_tensor(
                          xr[:, k * T:(k + 1) * T], pview, 0.5, xr[:, k * T:(k + 1) * T],
                          ALU.mult, ALU.add),
                      reads=bank + [f"xr_{half}"], writes=[f"xr_{half}"])
                if store_t is not None and (k == 3 or k == 7):
                    hf = 0 if k == 3 else 1
                    dst = bass.AP(yout.tensor, (hf * 4) * 128 * TOK + store_t * T,
                                  [[TOK, 128], [128 * TOK, 4], [1, T]])
                    dma(lambda e, dst=dst, xr=xr, hf=hf: e.dma_start(
                        dst, xr[:, hf * 4 * T:(hf * 4 + 4) * T].rearrange(
                            "p (c k) -> p c k", c=4)),
                        reads=[f"xr_{half}"], writes=[f"yout{store_t}{hf}"],
                        sem=f"ys{hf}")
                if glob:
                    gview = SA[:, 0:16] if k % 2 == 0 else SA[:, 512:528]
                    gbank = bk(0 + k % 2)
                    for c2 in range(NFC):
                        em.op("tensor",
                              lambda e, buf=buf, c2=c2, gview=gview: e.matmul(
                                  gview, buf[:, c2 * 128:(c2 + 1) * 128],
                                  hg[:, c2 * 16:(c2 + 1) * 16],
                                  start=(c2 == 0), stop=False),
                              reads=[f"w2b{k % 2}", "hg"], writes=gbank)
                    em.op("tensor",
                          lambda e, k=k, gview=gview: e.matmul(
                              gview, b2rb[0:1, k * 128:(k + 1) * 128],
                              onesrow512[0:1, 0:16], start=False, stop=True),
                          reads=["b2rb", "onesrow512"], writes=gbank)
                    em.op("vector",
                          lambda e, k=k, gview=gview: e.scalar_tensor_tensor(
                              xgres[:, k * 16:(k + 1) * 16], gview, 0.5,
                              xgres[:, k * 16:(k + 1) * 16], ALU.mult, ALU.add),
                          reads=gbank + ["xgres"], writes=["xgres"])
                if k + 2 < NCH:
                    w2dma(k + 2)


        def ffn(xr, xrname, glob, store_t=None):
            ffn1(xr, xrname, glob)
            ffn2(xr, xrname, glob, store_t=store_t)

        def proj(glob):
            stream_q = [("q", 0), ("q", 1), ("k", 0), ("k", 1), ("v", 0), ("v", 1)]

            def pjdma(idx):
                pname, half = stream_q[idx]
                wscr = {"q": wq_s, "k": wk_s, "v": wv_s}[pname]
                bid = idx % 2
                src = bass.AP(wscr.tensor, half * 512,
                              [[NCH * D, 128], [D, NCH], [1, 512]])
                buf = pjb[bid]
                em.op("sync",
                      lambda e, src=src, buf=buf: e.dma_start(
                          buf[:].rearrange("p (c i) -> p c i", c=NCH), src),
                      reads=[f"{pname}_s"], writes=[f"pjb{bid}"], sem=f"pjb{bid}")
            pjdma(0)
            pjdma(1)
            bi = 0
            for pname, dstb, dstn in (("q", q_hd, "q_hd"), ("k", k_hd, "k_hd")):
                for half in range(2):
                    bid = bi % 2
                    buf = pjb[bid]
                    bi += 1
                    for hh in range(8):
                        hd = half * 8 + hh
                        pview, bank = p4view(hd)
                        pview = pview[0:64, :]
                        for c in range(NCH):
                            lhs = buf[:, c * 512 + hh * 64:c * 512 + hh * 64 + 64]
                            em.op("tensor",
                                  lambda e, lhs=lhs, c=c, pview=pview: e.matmul(
                                      pview, lhs, xhat[:, c * T:(c + 1) * T],
                                      start=(c == 0), stop=(c == NCH - 1)),
                                  reads=[f"pjb{bid}", f"xh{c}"], writes=bank)
                        em.op("scalar",
                              lambda e, hd=hd, pview=pview, dstb=dstb: e.activation(
                                  dstb[0:64, hd * T:(hd + 1) * T], pview, AF.Copy),
                              reads=bank, writes=[dstn])
                    if bi + 1 < 6:
                        pjdma(bi + 1)
            for half in range(2):
                bid = bi % 2
                buf = pjb[bid]
                bi += 1
                for blk in range(T // 64):
                    pview = PSB3[:, 0:T] if blk % 2 == 0 else PSB3[:, T:2 * T]
                    bank = bk(6 + blk % 2)
                    for c in range(NCH):
                        em.op("tensor",
                              lambda e, buf=buf, c=c, blk=blk, pview=pview: e.matmul(
                                  pview[0:64, :],
                                  xhat[:, c * T + blk * 64:c * T + blk * 64 + 64],
                                  buf[:, c * 512:(c + 1) * 512],
                                  start=(c == 0), stop=(c == NCH - 1)),
                              reads=[f"pjb{bid}", f"xh{c}"], writes=bank)
                    em.op("vector",
                          lambda e, blk=blk, half=half, pview=pview: e.tensor_copy(
                              vaug[0:64, blk * D + half * 512:blk * D + half * 512 + 512],
                              pview[0:64, :]),
                          reads=bank, writes=["vaug"])
                if bi + 1 < 6:
                    pjdma(bi + 1)
                if glob:
                    gview = PSB2[:, 0:T]
                    gbank = bk(4)
                    for c in range(NCH):
                        em.op("tensor",
                              lambda e, buf=buf, c=c, gview=gview: e.matmul(
                                  gview[64:80, :], xghat[:, c * 16:(c + 1) * 16],
                                  buf[:, c * 512:(c + 1) * 512],
                                  start=(c == 0), stop=(c == NCH - 1),
                                  tile_position=(0, 64)),
                              reads=[f"pjb{bid}", "xghat"], writes=gbank)
                    for blk in range(T // 64):
                        em.op("vector",
                              lambda e, blk=blk, half=half, gview=gview: e.tensor_copy(
                                  vaug[64:80,
                                       blk * D + half * 512:blk * D + half * 512 + 512],
                                  gview[64:80, :]),
                              reads=gbank, writes=["vaug"])
            if glob:
                kb = []
                for half in range(2):
                    bid = bi % 2
                    buf = pjb[bid]
                    bi += 1
                    src = bass.AP(wk_s.tensor, half * 4096,
                                  [[NCH * D, 128], [1, 4096]])
                    em.op("sync",
                          lambda e, src=src, buf=buf: e.dma_start(buf[:, 0:4096], src),
                          reads=["k_s"], writes=[f"pjb{bid}"], sem=f"pjb{bid}")
                    kb.append((buf, bid))
                for hd in range(H):
                    gview = PSB3[0:64, 0:16] if hd % 2 == 0 else PSB3[0:64, 512:528]
                    gbank = bk(6 + hd % 2)
                    for c in range(NCH):
                        buf, bid = kb[c // 4]
                        lhs = buf[:, (c % 4) * 1024 + hd * 64:(c % 4) * 1024 + hd * 64 + 64]
                        em.op("tensor",
                              lambda e, lhs=lhs, gview=gview, c=c: e.matmul(
                                  gview, lhs, xghat[:, c * 16:(c + 1) * 16],
                                  start=(c == 0), stop=(c == NCH - 1)),
                              reads=[f"pjb{bid}", "xghat"], writes=gbank)
                    em.op("scalar",
                          lambda e, hd=hd, gview=gview: e.activation(
                              kg_hd[0:64, hd * 16:(hd + 1) * 16], gview, AF.Copy),
                          reads=gbank, writes=["kg_hd"])

        def attn(t, xr, xrname):
            halves = [(p, g) for p in range(NT) for g in range(2)]

            def scores_stage(k):
                p, g = halves[k]
                hp = k % 2
                Sreg = SA if hp == 0 else SB
                sbanks = bk(2 * hp, 2 * hp + 1)
                for hh in range(8):
                    hd = g * 8 + hh
                    for m in range(2):
                        qb = 2 * p + m
                        qs = hd * T + qb * 64
                        em.op("tensor",
                              lambda e, qs=qs, hh=hh, m=m, Sreg=Sreg: e.matmul(
                                  Sreg[64 * m:64 * m + 64, 128 * hh:128 * hh + 64],
                                  q_hd[0:64, qs:qs + 64],
                                  k_hd[0:64, qs:qs + 64],
                                  start=True, stop=True,
                                  tile_position=(0, 64 * m)),
                              reads=["q_hd", "k_hd"], writes=sbanks)
                        em.op("tensor",
                              lambda e, qs=qs, hh=hh, m=m, hd=hd, Sreg=Sreg: e.matmul(
                                  Sreg[64 * m:64 * m + 64,
                                       128 * hh + 64:128 * hh + 80],
                                  q_hd[0:64, qs:qs + 64],
                                  kg_hd[0:64, hd * 16:hd * 16 + 16],
                                  start=True, stop=True,
                                  tile_position=(0, 64 * m)),
                              reads=["q_hd", "kg_hd"], writes=sbanks)

            def smx_stage(k):
                p, g = halves[k]
                hp = k % 2
                hq = k % 3
                Sreg = SA if hp == 0 else SB
                sbanks = bk(2 * hp, 2 * hp + 1)
                nmv = negm[:, 8 * hp:8 * hp + 8]
                smv = sums[:, 8 * hp:8 * hp + 8]
                esv = esrc[:, 640 * hp:640 * hp + 640]
                psv = psb[:, 640 * hq:640 * hq + 640]
                s3d = Sreg[:].rearrange("p (h k) -> p h k", h=8)[:, :, 0:80]
                em.op("vector",
                      lambda e, s3d=s3d, nmv=nmv: e.tensor_reduce(
                          nmv, s3d, axis=AX.X, op=ALU.max, negate=True),
                      reads=sbanks, writes=[f"negm{hp}"])
                nb = nmv.rearrange("p h -> p h ()").broadcast_to([128, 8, 80])
                em.op("vector",
                      lambda e, s3d=s3d, nb=nb, esv=esv: e.tensor_tensor(
                          esv.rearrange("p (h k) -> p h k", h=8), s3d, nb, ALU.add),
                      reads=sbanks + [f"negm{hp}"], writes=[f"esrc{hp}"])
                em.op("scalar",
                      lambda e, psv=psv, esv=esv: e.activation(psv, esv, AF.Exp),
                      reads=[f"esrc{hp}"], writes=[f"psb{hq}"])
                mask = maskq0 if (t == 0 and p == 0) else maskstd
                mname = "maskq0" if (t == 0 and p == 0) else "maskstd"
                mb = mask[:].rearrange("p k -> p () k").broadcast_to([128, 8, 80])
                em.op("gpsimd",
                      lambda e, mb=mb, psv=psv: e.tensor_tensor(
                          psv.rearrange("p (h k) -> p h k", h=8),
                          psv.rearrange("p (h k) -> p h k", h=8), mb, ALU.mult),
                      reads=[f"psb{hq}", mname], writes=[f"psb{hq}"])
                em.op("vector",
                      lambda e, psv=psv, smv=smv: e.tensor_reduce(
                          smv, psv.rearrange("p (h k) -> p h k", h=8),
                          axis=AX.X, op=ALU.add),
                      reads=[f"psb{hq}"], writes=[f"sums{hp}"])
                em.op("vector",
                      lambda e, hp=hp, smv=smv: e.reciprocal(
                          rinvf[:, 8 * hp:8 * hp + 8], smv),
                      reads=[f"sums{hp}"], writes=[f"rinvf{hp}"])
                em.op("vector",
                      lambda e, hp=hp: e.tensor_copy(rinvb[:, 8 * hp:8 * hp + 8],
                                                     rinvf[:, 8 * hp:8 * hp + 8]),
                      reads=[f"rinvf{hp}"], writes=[f"rinvb{hp}"])
                rb = rinvb[:, 8 * hp:8 * hp + 8].rearrange(
                    "p h -> p h ()").broadcast_to([128, 8, 80])
                em.op("gpsimd",
                      lambda e, rb=rb, psv=psv: e.tensor_tensor(
                          psv.rearrange("p (h k) -> p h k", h=8),
                          psv.rearrange("p (h k) -> p h k", h=8), rb, ALU.mult),
                      reads=[f"psb{hq}", f"rinvb{hp}"], writes=[f"psb{hq}"])

            def pt_stage(k):
                p, g = halves[k]
                hp = k % 2
                hq = k % 3
                psv = psb[:, 640 * hq:640 * hq + 640]
                ph = hp
                ptbank = bk(4 + ph)
                for hh in range(8):
                    em.op("tensor",
                          lambda e, hh=hh, ph=ph, psv=psv: e.transpose(
                              PSB2_bf[0:80, ph * 1024 + 128 * hh:
                                      ph * 1024 + 128 * hh + 128],
                              psv[:, 80 * hh:80 * hh + 80], ident[:]),
                          reads=[f"psb{hq}", "ident"], writes=ptbank)
                em.op("scalar",
                      lambda e, ph=ph: e.copy(
                          ptsb[0:80, ph * 1024:ph * 1024 + 1024],
                          PSB2_bf[0:80, ph * 1024:ph * 1024 + 1024]),
                      reads=ptbank, writes=[f"ptsb{ph}"])

            def av_stage(k):
                p, g = halves[k]
                hp = k % 2
                Sreg = SA if hp == 0 else SB
                sbanks = bk(2 * hp, 2 * hp + 1)
                ph = hp
                avbank = bk(6 + hp)
                av = PSB3[:, hp * 512:hp * 512 + 512]
                for hh in range(8):
                    hd = g * 8 + hh
                    cw = hh // 2
                    ob = 64 * (hd % 2)
                    for m in range(2):
                        qb = 2 * p + m
                        em.op("tensor",
                              lambda e, hd=hd, hh=hh, m=m, qb=qb, cw=cw, ob=ob,
                              av=av, ph=ph: e.matmul(
                                  av[ob:ob + 64,
                                     128 * cw + 64 * m:128 * cw + 64 * m + 64],
                                  vaug[0:80, qb * D + 64 * hd:qb * D + 64 * hd + 64],
                                  ptsb[0:80, ph * 1024 + 128 * hh + 64 * m:
                                       ph * 1024 + 128 * hh + 64 * m + 64],
                                  start=True, stop=True, tile_position=(0, ob)),
                              reads=["vaug", f"ptsb{ph}"], writes=avbank)
                em.op("scalar",
                      lambda e, g=g, p=p, av=av: e.copy(
                          bass.AP(attnf, (4 * g) * T + p * 128,
                                  [[NCH * T, 128], [T, 4], [1, 128]]),
                          av.rearrange("p (c q) -> p c q", c=4)),
                      reads=avbank, writes=["attnf"])


            def ptav_stage(k):
                p, g = halves[k]
                hp = k % 2
                Sreg = SA if hp == 0 else SB
                sbanks = bk(2 * hp, 2 * hp + 1)
                psv = psb[:, 640 * hp:640 * hp + 640]
                ph = hp
                ptbank = bk(4 + ph)
                for hh in range(8):
                    em.op("tensor",
                          lambda e, hh=hh, ph=ph, psv=psv: e.transpose(
                              PSB2_bf[0:80, ph * 1024 + 128 * hh:
                                      ph * 1024 + 128 * hh + 128],
                              psv[:, 80 * hh:80 * hh + 80], ident[:]),
                          reads=[f"psb{hp}", "ident"], writes=ptbank)
                em.op("scalar",
                      lambda e, ph=ph: e.copy(
                          ptsb[0:80, ph * 1024:ph * 1024 + 1024],
                          PSB2_bf[0:80, ph * 1024:ph * 1024 + 1024]),
                      reads=ptbank, writes=[f"ptsb{ph}"])
                avbank = bk(6 + hp)
                av = PSB3[:, hp * 512:hp * 512 + 512]
                for hh in range(8):
                    hd = g * 8 + hh
                    cw = hh // 2
                    ob = 64 * (hd % 2)
                    for m in range(2):
                        qb = 2 * p + m
                        em.op("tensor",
                              lambda e, hd=hd, hh=hh, m=m, qb=qb, cw=cw, ob=ob,
                              av=av, ph=ph: e.matmul(
                                  av[ob:ob + 64,
                                     128 * cw + 64 * m:128 * cw + 64 * m + 64],
                                  vaug[0:80, qb * D + 64 * hd:qb * D + 64 * hd + 64],
                                  ptsb[0:80, ph * 1024 + 128 * hh + 64 * m:
                                       ph * 1024 + 128 * hh + 64 * m + 64],
                                  start=True, stop=True, tile_position=(0, ob)),
                              reads=["vaug", f"ptsb{ph}"], writes=avbank)
                em.op("scalar",
                      lambda e, g=g, p=p, av=av: e.copy(
                          bass.AP(attnf, (4 * g) * T + p * 128,
                                  [[NCH * T, 128], [T, 4], [1, 128]]),
                          av.rearrange("p (c q) -> p c q", c=4)),
                      reads=avbank, writes=["attnf"])


            NH = len(halves)
            for k in range(NH):
                scores_stage(k)
                smx_stage(k)
                if k >= 2:
                    pt_stage(k - 2)
                if k >= 3:
                    av_stage(k - 3)
            pt_stage(NH - 2)
            av_stage(NH - 3)
            pt_stage(NH - 1)
            av_stage(NH - 2)
            av_stage(NH - 1)
            for half in range(2):
                src = bass.AP(wo_s.tensor, half * 512, [[NCH * D, 128], [D, NCH], [1, 512]])
                buf = pjb[half]
                em.op("sync",
                      lambda e, src=src, buf=buf: e.dma_start(
                          buf[:].rearrange("p (c i) -> p c i", c=NCH), src),
                      reads=["o_s"], writes=[f"pjb{half}"], sem=f"pjb{half}")
            for jc in range(NCH):
                half = jc // 4
                buf = pjb[half]
                pview, bank = p4view(jc)
                for c in range(NCH):
                    lhs = buf[:, c * 512 + (jc % 4) * 128:c * 512 + (jc % 4) * 128 + 128]
                    em.op("tensor",
                          lambda e, lhs=lhs, c=c, pview=pview: e.matmul(
                              pview, lhs, attnf[:, c * T:(c + 1) * T],
                              start=(c == 0), stop=(c == NCH - 1)),
                          reads=[f"pjb{half}", "attnf"], writes=bank)
                rhalf = "lo" if jc < 4 else "hi"
                em.op("vector",
                      lambda e, jc=jc, pview=pview: e.scalar_tensor_tensor(
                          xr[:, jc * T:(jc + 1) * T], pview, 1.0,
                          xr[:, jc * T:(jc + 1) * T], ALU.mult, ALU.add),
                      reads=bank + [f"xr_{rhalf}"], writes=[f"xr_{rhalf}"])

        def xload(t, hf):
            src = bass.AP(xfm.tensor, (hf * 4) * 128 * TOK + t * T,
                          [[TOK, 128], [128 * TOK, 4], [1, T]])
            xr = xres[0]
            dma(lambda e, src=src, xr=xr, hf=hf: e.dma_start(
                xr[:, hf * 4 * T:(hf * 4 + 4) * T].rearrange("p (c k) -> p c k", c=4),
                src),
                writes=[f"xr_{'lo' if hf == 0 else 'hi'}"], sem=f"xl{hf}")

        dma(lambda e: e.dma_start(b1s[:], b1in), writes=["b1s"], sem="const", batch=True)
        dma(lambda e: e.dma_start(ident[:], identin), writes=["ident"], sem="const", batch=True)
        dma(lambda e: e.dma_start(onesmean[:], onesmean_in), writes=["onesmean"], sem="const", batch=True)
        dma(lambda e: e.dma_start(onesrow[:], onesrow_in), writes=["onesrow"], sem="const", batch=True)
        dma(lambda e: e.dma_start(onescol[:], onescol_in), writes=["onescol"], sem="const", batch=True)
        dma(lambda e: e.dma_start(onesrow512[:], onesrow512_in), writes=["onesrow512"], sem="const", batch=True)
        dma(lambda e: e.dma_start(maskstd[:], maskstd_in), writes=["maskstd"], sem="const", batch=True)
        dma(lambda e: e.dma_start(maskq0[:], maskq0_in), writes=["maskq0"], sem="const", batch=True)
        dma(lambda e: e.dma_start(rowm[0:1, 0:512], b2row[0:1, 0:512]), writes=["rowm"], sem="const", batch=True)
        dma(lambda e: e.dma_start(rows2[0:1, 0:512], b2row[0:1, 512:1024]), writes=["rows2"], sem="const", batch=True)
        em.settle(["b1s", "ident", "onesmean", "onesrow", "onescol", "onesrow512", "maskstd", "maskq0", "esel", "rowm", "rows2"], "const")
        em.op("scalar", lambda e: e.activation(b2rb[0:1, 0:512], rowm[0:1, 0:512], AF.Copy),
              reads=["rowm"], writes=["b2rb"])
        em.op("scalar", lambda e: e.activation(b2rb[0:1, 512:1024], rows2[0:1, 0:512], AF.Copy),
              reads=["rows2"], writes=["b2rb"])

        xload(0, 0)
        xload(0, 1)
        gsrc = bass.AP(xgin.tensor, 0, [[16, 128], [128 * 16, NCH], [1, 16]])
        dma(lambda e, gsrc=gsrc: e.dma_start(
            xgres[:].rearrange("p (c k) -> p c k", c=NCH), gsrc),
            writes=["xgres"], sem="xg")
        ln(xres[0], "xres0", True)

        for c in range(NCH):
            stg = wstage[c % 2]
            src = w1t[c * 128:(c + 1) * 128, :]
            dma(lambda e, stg=stg, src=src: e.dma_start(stg, src), writes=[f"wst{c % 2}"],
                sem=f"stg{c % 2}")
            em.op("vector",
                  lambda e, stg=stg, c=c: e.tensor_reduce(
                      gacc1[:, c:c + 1], stg, axis=AX.X, op=ALU.add,
                      apply_absolute_value=True),
                  reads=[f"wst{c % 2}"], writes=["gacc1"])
        em.op("tensor", lambda e: e.matmul(SA[0:1, 0:NCH], onescol[:], gacc1[:],
                                           start=True, stop=True),
              reads=["gacc1", "onescol"], writes=bk(0))
        em.op("vector", lambda e: e.tensor_reduce(growtot[0:1, 0:1], SA[0:1, 0:NCH],
                                                  axis=AX.X, op=ALU.add),
              reads=bk(0), writes=["growtot"])
        em.op("scalar", lambda e: e.activation(throw[0:1, 0:1], growtot[0:1, 0:1], AF.Copy,
                                               scale=0.5 / (D * DFF), bias=0.5 * QEPS),
              reads=["growtot"], writes=["throw"])
        em.op("scalar", lambda e: e.activation(nthrow[0:1, 0:1], growtot[0:1, 0:1], AF.Copy,
                                               scale=-0.5 / (D * DFF), bias=-0.5 * QEPS),
              reads=["growtot"], writes=["nthrow"])
        em.op("tensor", lambda e: e.matmul(SA[:, 0:1], onesrow[:], throw[0:1, 0:1],
                                           start=True, stop=True),
              reads=["throw", "onesrow"], writes=bk(0))
        em.op("tensor", lambda e: e.matmul(SA[:, 512:513], onesrow[:], nthrow[0:1, 0:1],
                                           start=True, stop=True),
              reads=["nthrow", "onesrow"], writes=bk(1))
        em.op("scalar", lambda e: e.copy(thrc[:, 0:1], SA[:, 0:1]), reads=bk(0),
              writes=["thrc"])
        em.op("scalar", lambda e: e.copy(nthrc[:, 0:1], SA[:, 512:513]), reads=bk(1),
              writes=["nthrc"])

        for c in range(NCH):
            stg = wstage[c % 2]
            src = w1t[c * 128:(c + 1) * 128, :]
            ia, ib = (2 * c) % 3, (2 * c + 1) % 3
            ta, tb = tmps3[ia], tmps3[ib]
            dma(lambda e, stg=stg, src=src: e.dma_start(stg, src),
                reads=[f"w1q{c % 2}"], writes=[f"wst{c % 2}"], sem=f"stg{c % 2}")
            em.op("vector",
                  lambda e, stg=stg, ta=ta: e.tensor_scalar(
                      ta, stg, thrc[:, 0:1], None, ALU.is_gt),
                  reads=[f"wst{c % 2}", "thrc"], writes=[f"tmp{ia}"] + TMPAL[ia])
            em.op("vector",
                  lambda e, stg=stg, tb=tb: e.tensor_scalar(
                      tb, stg, nthrc[:, 0:1], None, ALU.is_lt),
                  reads=[f"wst{c % 2}", "nthrc"], writes=[f"tmp{ib}"] + TMPAL[ib])
            em.op("vector",
                  lambda e, ta=ta, tb=tb: e.tensor_tensor(ta, ta, tb, ALU.subtract),
                  reads=[f"tmp{ia}", f"tmp{ib}"], writes=[f"tmp{ia}", f"wst{c % 2}done"])
            dst = bass.AP(w1q_s.tensor, c * 128 * 512,
                          [[512, 128], [NCH * 128 * 512, 8], [1, 512]])
            em.op("scalar",
                  lambda e, ta=ta, dst=dst: e.dma_start(
                      dst, ta.rearrange("p (jg r) -> p jg r", jg=8)),
                  reads=[f"tmp{ia}"], writes=["w1q_s", f"w1q{c % 2}"],
                  sem="w1qs", batch=True)

        vaug32 = vaug.bitcast(F32)
        vstg = [vaug32[:, 0:2048], vaug32[:, 2048:4096]]
        w2jobs = []

        def w2p1_job(i):
            stg = vstg[i % 2]
            srcw = bass.AP(w2t.tensor, (2 * i) * 128 * D,
                           [[D, 128], [128 * D, 2], [1, D]])
            dma(lambda e, stg=stg, srcw=srcw: e.dma_start(
                stg.rearrange("p (c e) -> p c e", c=2), srcw),
                writes=[f"w2stg{i % 2}"], sem=f"stg2{i % 2}")
            em.op("vector",
                  lambda e, stg=stg, i=i: e.tensor_reduce(
                      gacc2[:, i:i + 1], stg, axis=AX.X, op=ALU.add,
                      apply_absolute_value=True),
                  reads=[f"w2stg{i % 2}"], writes=["gacc2"])

        def w2gamma_job():
            em.op("tensor", lambda e: e.matmul(SA[0:1, 0:16], onescol[:], gacc2[:],
                                               start=True, stop=True),
                  reads=["gacc2", "onescol"], writes=bk(0))
            em.op("vector", lambda e: e.tensor_reduce(growtot[0:1, 1:2], SA[0:1, 0:16],
                                                      axis=AX.X, op=ALU.add),
                  reads=bk(0), writes=["growtot"])
            em.op("scalar", lambda e: e.activation(throw[0:1, 1:2], growtot[0:1, 1:2],
                                                   AF.Copy, scale=0.5 / (D * DFF),
                                                   bias=0.5 * QEPS),
                  reads=["growtot"], writes=["throw"])
            em.op("scalar", lambda e: e.activation(nthrow[0:1, 1:2], growtot[0:1, 1:2],
                                                   AF.Copy, scale=-0.5 / (D * DFF),
                                                   bias=-0.5 * QEPS),
                  reads=["growtot"], writes=["nthrow"])
            em.op("tensor", lambda e: e.matmul(SA[:, 0:1], onesrow[:], throw[0:1, 1:2],
                                               start=True, stop=True),
                  reads=["throw", "onesrow"], writes=bk(0))
            em.op("tensor", lambda e: e.matmul(SA[:, 512:513], onesrow[:],
                                               nthrow[0:1, 1:2], start=True, stop=True),
                  reads=["nthrow", "onesrow"], writes=bk(1))
            em.op("scalar", lambda e: e.copy(thrc[:, 1:2], SA[:, 0:1]), reads=bk(0),
                  writes=["thrc"])
            em.op("scalar", lambda e: e.copy(nthrc[:, 1:2], SA[:, 512:513]), reads=bk(1),
                  writes=["nthrc"])

        def w2p2_job(i):
            stg = vstg[i % 2]
            srcw = bass.AP(w2t.tensor, (2 * i) * 128 * D,
                           [[D, 128], [128 * D, 2], [1, D]])
            isl = i % 3
            slot = tmps3[isl]
            ta = slot[:, 0:2048].rearrange("p (k c e) -> p k c e", k=NCH, c=2)
            tb = slot[:, 2048:4096].rearrange("p (k c e) -> p k c e", k=NCH, c=2)
            sview = stg.rearrange("p (c k e) -> p k c e", c=2, k=NCH)
            dma(lambda e, stg=stg, srcw=srcw: e.dma_start(
                stg.rearrange("p (c e) -> p c e", c=2), srcw),
                reads=[f"w2w{i % 2}"], writes=[f"w2stg{i % 2}"], sem=f"stg2{i % 2}")
            em.op("vector",
                  lambda e, sview=sview, ta=ta: e.tensor_scalar(
                      ta, sview, thrc[:, 1:2], None, ALU.is_gt),
                  reads=[f"w2stg{i % 2}", "thrc"], writes=[f"tmp{isl}"] + TMPAL[isl])
            em.op("vector",
                  lambda e, sview=sview, tb=tb: e.tensor_scalar(
                      tb, sview, nthrc[:, 1:2], None, ALU.is_lt),
                  reads=[f"w2stg{i % 2}", "nthrc"],
                  writes=[f"tmp{isl}", f"w2w{i % 2}"])
            em.op("vector",
                  lambda e, ta=ta, tb=tb: e.tensor_tensor(ta, ta, tb, ALU.subtract),
                  reads=[f"tmp{isl}"], writes=[f"tmp{isl}"])
            dstw = bass.AP(w2q_s.tensor, (i // 2) * 128 * 512 + (i % 2) * 256,
                           [[512, 128], [8 * 128 * 512, NCH], [1, 256]])
            em.op("scalar",
                  lambda e, slot=slot, dstw=dstw: e.dma_start(
                      dstw, slot[:, 0:2048].rearrange("p (k r) -> p k r", k=NCH)),
                  reads=[f"tmp{isl}"], writes=["w2q_s"], sem="w2qs", batch=True)

        for i in range(16):
            w2jobs.append(lambda i=i: w2p1_job(i))
        w2jobs.append(w2gamma_job)
        for i in range(16):
            w2jobs.append(lambda i=i: w2p2_job(i))


        ffn1(xres[0], "xres0", True, side_jobs=w2jobs)
        while w2jobs:
            w2jobs.pop(0)()

        ffn2(xres[0], "xres0", True)


        qi = 0
        for name, win, wscr, scale in (("q", wqt, wq_s, 0.125), ("k", wkt, wk_s, 1.0),
                                       ("v", wvt, wv_s, 1.0), ("o", wot, wo_s, 1.0)):
            for half in range(2):
                srcw = bass.AP(win.tensor, (half * 4) * 128 * D,
                               [[D, 128], [128 * D, 4], [1, D]])
                isl = qi % 3
                slot = tmps3[isl]
                em.op("scalar", lambda e, srcw=srcw: e.dma_start(
                    vaug32[:].rearrange("p (c i) -> p c i", c=4), srcw),
                    writes=["w2stg0", "w2stg1"], sem="stg20")
                em.op("scalar",
                      lambda e, scale=scale, slot=slot: e.activation(
                          slot, vaug32[:], AF.Copy, scale=scale),
                      reads=["w2stg0", "w2stg1"],
                      writes=[f"tmp{isl}"] + TMPAL[isl])
                em.op("scalar",
                      lambda e, wscr=wscr, half=half, slot=slot: e.dma_start(
                          bass.AP(wscr.tensor, half * 4096, [[8192, 128], [1, 4096]]),
                          slot),
                      reads=[f"tmp{isl}"], writes=[f"{name}_s"], sem="qkvs")
                qi += 1

        em.op("vector", lambda e: e.engine_nop(),
              writes=["xb", "sq", "wst0", "wst1", "tmp0", "tmp1",
                      "tmp2", "w2stg0", "w2stg1", "vaug", "attnf"])

        import os
        dbg = os.environ.get("MK_DEBUG_PHASES", "")
        for t in range(NT):
            xr = xres[0]
            xrname = "xres0"
            if t > 0:
                xload(t, 0)
                xload(t, 1)
            glob = (t == 0)
            if t > 0:
                ln(xr, xrname, glob)
            if "noffn1" not in dbg and t > 0:
                ffn(xr, xrname, glob)
            if "s1" not in dbg:
                ln(xr, xrname, glob)
                proj(glob)
                if "noattn" not in dbg:
                    attn(t, xr, xrname)
                if "s2" not in dbg:
                    ln(xr, xrname, False)
                    ffn(xr, xrname, False, store_t=t)
                else:
                    for hf in range(2):
                        dst = bass.AP(yout.tensor, (hf * 4) * 128 * TOK + t * T,
                                      [[TOK, 128], [128 * TOK, 4], [1, T]])
                        dma(lambda e, dst=dst, xr=xr, hf=hf: e.dma_start(
                            dst, xr[:, hf * 4 * T:(hf * 4 + 4) * T].rearrange(
                                "p (c k) -> p c k", c=4)),
                            reads=[f"xr_{'lo' if hf == 0 else 'hi'}"],
                            writes=[f"yout{t}{hf}"], sem=f"ys{hf}")

        em.final_wait(["ys0", "ys1"])

        for e in ENGINES:
            if e != "sync":
                sems[e] = E(nc.semaphore(f"sem_{e}"))
        for s in em.dma_sems:
            sems[s] = E(nc.semaphore(f"sem_{s}"))
        block = E(nc.Block())

        def emit(engobj, ename):
            for fn, waits, inc, semname in em.ops[ename]:
                for (e2, v) in waits:
                    engobj.wait_ge(sems[e2], v)
                if fn is not None:
                    ins = fn(engobj)
                    if inc:
                        ins.then_inc(sems[semname], inc)

        @block.sync
        def _(sync):
            emit(sync, "sync")

        @block.tensor
        def _(tensor):
            emit(tensor, "tensor")

        @block.vector
        def _(vector):
            emit(vector, "vector")

        @block.scalar
        def _(scalar):
            emit(scalar, "scalar")

        @block.gpsimd
        def _(gpsimd):
            emit(gpsimd, "gpsimd")

    return nc


_NC_CACHE = None


def kernel(L0_attn_hidden_states, L0_attn_Wq, L0_attn_Wk, L0_attn_Wv, L0_attn_Wo,
           L0_ffn_W1, L0_ffn_b1, L0_ffn_W2, L0_ffn_b2):
    global _NC_CACHE
    x = np.asarray(L0_attn_hidden_states, dtype=np.float32)
    bf = ml_dtypes.bfloat16
    w1tA = np.ascontiguousarray(np.asarray(L0_ffn_W1, dtype=np.float32).T)
    w2tA = np.ascontiguousarray(np.asarray(L0_ffn_W2, dtype=np.float32).T)
    wqtA = np.ascontiguousarray(np.asarray(L0_attn_Wq, dtype=np.float32).T)
    wktA = np.ascontiguousarray(np.asarray(L0_attn_Wk, dtype=np.float32).T)
    wvtA = np.ascontiguousarray(np.asarray(L0_attn_Wv, dtype=np.float32).T)
    wotA = np.ascontiguousarray(np.asarray(L0_attn_Wo, dtype=np.float32).T)
    b1r = np.ascontiguousarray(np.asarray(L0_ffn_b1, dtype=np.float32).reshape(NFC, 128).T)
    b2r = np.ascontiguousarray(np.asarray(L0_ffn_b2, dtype=np.float32)[None, :])
    ident = np.eye(128, dtype=bf)
    onesmean = np.full((128, 1), 1.0 / D, dtype=np.float32).astype(bf)
    onesrow = np.ones((1, 128), dtype=np.float32)
    onescol = np.ones((128, 1), dtype=np.float32)
    onesrow512 = np.ones((1, 512), dtype=bf)
    tril = np.tril(np.ones((64, 64), dtype=np.float32))
    mrow = np.concatenate([tril, np.ones((64, 16), np.float32)], 1)
    maskstd = np.concatenate([mrow, mrow], 0).astype(bf)
    eselA = np.zeros((16, D), dtype=np.float32)
    for hd in range(16):
        eselA[hd, hd * 64:(hd + 1) * 64] = 1.0
    eselA = eselA.astype(bf)

    in_maps = []
    for c in range(NCORE):
        b, half = c // 2, c % 2
        xfmA = np.ascontiguousarray(x[b, half * TOK:(half + 1) * TOK, :].T)
        xgA = np.ascontiguousarray(x[b, 0:16, :].T)
        gm = 1.0 if half == 1 else 0.0
        mq0 = np.concatenate([
            np.concatenate([tril, np.full((64, 16), gm, np.float32)], 1), mrow],
            0).astype(bf)
        in_maps.append({
            "xfm": xfmA, "xgin": xgA,
            "w1t": w1tA, "w2t": w2tA, "wqt": wqtA, "wkt": wktA, "wvt": wvtA, "wot": wotA,
            "b1in": b1r, "b2row": b2r,
            "identin": ident, "onesmean_in": onesmean, "onesrow_in": onesrow,
            "onescol_in": onescol, "onesrow512_in": onesrow512,
            "maskstd_in": maskstd, "maskq0_in": mq0, "esel_in": eselA,
        })

    if _NC_CACHE is None:
        _NC_CACHE = build_nc()
    res = run_bass_kernel_spmd(_NC_CACHE, in_maps, core_ids=list(range(NCORE)))

    out = np.empty((B, S, D), dtype=np.float32)
    for c in range(NCORE):
        b, half = c // 2, c % 2
        out[b, half * TOK:(half + 1) * TOK, :] = np.asarray(
            res.results[c]["y"], dtype=np.float32).T
    return out


# version 25
# speedup vs baseline: 1.0103x; 1.0051x over previous
import numpy as np
import ml_dtypes
from contextlib import ExitStack

import concourse.bass as bass
import concourse.mybir as mybir
from concourse.bass_utils import run_bass_kernel_spmd

F32 = mybir.dt.float32
BF16 = mybir.dt.bfloat16
AF = mybir.ActivationFunctionType
ALU = mybir.AluOpType
AX = mybir.AxisListType

B, S, D = 4, 4096, 1024
H, HD = 16, 64
DFF = 4096
NCORE = 8
TOK = 2048
T = 512
NT = TOK // T
NCH = D // 128
NFC = DFF // 128
EPS = 1e-5
QEPS = 1e-5

ENGINES = ("tensor", "vector", "scalar", "sync", "gpsimd")


class Em:

    def __init__(self):
        self.ops = {e: [] for e in ENGINES}
        self.count = {}
        self.last_w = {}
        self.readers = {}
        self.waited = {e: {} for e in ENGINES}
        self.dma_sems = {}

    def op(self, eng, fn, reads=(), writes=(), sem=None, batch=False):
        is_dma = sem is not None
        if is_dma:
            semname = sem
            if semname not in self.dma_sems:
                self.dma_sems[semname] = batch
        else:
            semname = eng
        need = {}

        def add(dep):
            s2, v = dep
            if s2 == semname and eng == "tensor" and not is_dma:
                return
            if v > need.get(s2, 0):
                need[s2] = v

        for r in reads:
            lw = self.last_w.get(r)
            if lw:
                add(lw)
        for w in writes:
            lw = self.last_w.get(w)
            if lw:
                add(lw)
            for rd in self.readers.get(w, ()):
                add(rd)
        if is_dma and not batch:
            prev = self.count.get(semname, 0)
            if prev > need.get(semname, 0):
                need[semname] = prev
        wd = self.waited[eng]
        waits = []
        for s2, v in sorted(need.items()):
            if v > wd.get(s2, 0):
                waits.append((s2, v))
                wd[s2] = v
        inc = 16 if is_dma else 1
        self.count[semname] = self.count.get(semname, 0) + inc
        me = (semname, self.count[semname])
        for r in reads:
            self.readers.setdefault(r, []).append(me)
        for w in writes:
            self.last_w[w] = me
            self.readers[w] = []
        self.ops[eng].append((fn, waits, inc, semname, self.count[semname]))

    def settle(self, resources, semname):
        me = (semname, self.count.get(semname, 0))
        for r in resources:
            self.last_w[r] = me
            self.readers[r] = []

    def final_wait(self, semnames):
        waits = [(s, self.count.get(s, 0)) for s in semnames]
        self.ops["scalar"].append((None, waits, 0, "scalar", 0))

    def sparsify_incs(self):
        waited = {}
        for eng in ENGINES:
            for op in self.ops[eng]:
                for s, v in op[1]:
                    waited.setdefault(s, set()).add(v)
        comp = set(ENGINES)
        rank = {}
        for s in list(waited):
            if s in comp:
                rank[s] = {v: i + 1 for i, v in enumerate(sorted(waited[s]))}
        new_ops = {}
        for eng in ENGINES:
            lst = []
            for fn, waits, inc, semname, myval in self.ops[eng]:
                nw = []
                for s, v in waits:
                    if s in rank:
                        nw.append((s, rank[s][v]))
                    else:
                        nw.append((s, v))
                if semname in rank and inc:
                    inc = 1 if myval in waited.get(semname, ()) else 0
                lst.append((fn, nw, inc, semname, myval))
            new_ops[eng] = lst
        self.ops = new_ops


def build_nc():
    nc = bass.Bass("TRN2", target_bir_lowering=False)

    xfm = nc.dram_tensor("xfm", [D, TOK], F32, kind="ExternalInput").ap()
    xgin = nc.dram_tensor("xgin", [D, 16], F32, kind="ExternalInput").ap()
    w1t = nc.dram_tensor("w1t", [D, DFF], F32, kind="ExternalInput").ap()
    w2t = nc.dram_tensor("w2t", [DFF, D], F32, kind="ExternalInput").ap()
    wqt = nc.dram_tensor("wqt", [D, D], F32, kind="ExternalInput").ap()
    wkt = nc.dram_tensor("wkt", [D, D], F32, kind="ExternalInput").ap()
    wvt = nc.dram_tensor("wvt", [D, D], F32, kind="ExternalInput").ap()
    wot = nc.dram_tensor("wot", [D, D], F32, kind="ExternalInput").ap()
    b1in = nc.dram_tensor("b1in", [128, NFC], F32, kind="ExternalInput").ap()
    b2row = nc.dram_tensor("b2row", [1, D], F32, kind="ExternalInput").ap()
    identin = nc.dram_tensor("identin", [128, 128], BF16, kind="ExternalInput").ap()
    onesmean_in = nc.dram_tensor("onesmean_in", [128, 1], BF16, kind="ExternalInput").ap()
    onesrow_in = nc.dram_tensor("onesrow_in", [1, 128], F32, kind="ExternalInput").ap()
    onescol_in = nc.dram_tensor("onescol_in", [128, 1], F32, kind="ExternalInput").ap()
    onesrow512_in = nc.dram_tensor("onesrow512_in", [1, 512], BF16, kind="ExternalInput").ap()
    maskstd_in = nc.dram_tensor("maskstd_in", [128, 80], BF16, kind="ExternalInput").ap()
    maskq0_in = nc.dram_tensor("maskq0_in", [128, 80], BF16, kind="ExternalInput").ap()
    esel_in = nc.dram_tensor("esel_in", [16, D], BF16, kind="ExternalInput").ap()
    yout = nc.dram_tensor("y", [D, TOK], F32, kind="ExternalOutput").ap()

    w1q_s = nc.dram_tensor("w1q_s", [NFC * 128 * NCH * 128], BF16, kind="Internal").ap()
    w2q_s = nc.dram_tensor("w2q_s", [NCH * 128 * NFC * 128], BF16, kind="Internal").ap()
    wq_s = nc.dram_tensor("wq_s", [128 * NCH * D], BF16, kind="Internal").ap()
    wk_s = nc.dram_tensor("wk_s", [128 * NCH * D], BF16, kind="Internal").ap()
    wv_s = nc.dram_tensor("wv_s", [128 * NCH * D], BF16, kind="Internal").ap()
    wo_s = nc.dram_tensor("wo_s", [128 * NCH * D], BF16, kind="Internal").ap()

    with ExitStack() as st:
        E = st.enter_context
        xres = [E(nc.sbuf_tensor("xres0", [128, NCH * T], F32))] * 2
        xsq = E(nc.sbuf_tensor("xsq", [128, 2 * NCH * T], BF16))
        xhat = E(nc.sbuf_tensor("xhat", [128, NCH * T], BF16))
        h = E(nc.sbuf_tensor("h", [128, NFC * T], BF16))
        q_hd = E(nc.sbuf_tensor("q_hd", [64, H * T], BF16))
        k_hd = E(nc.sbuf_tensor("k_hd", [64, H * T], BF16))
        vaug = E(nc.sbuf_tensor("vaug", [128, (T // 64) * D], BF16))
        kg_hd = E(nc.sbuf_tensor("kg_hd", [64, H * 16], BF16))
        attnf = E(nc.sbuf_tensor("attnf", [128, NCH * T], BF16))
        esrc = E(nc.sbuf_tensor("esrc", [128, 2 * 8 * 80], F32))
        psb = E(nc.sbuf_tensor("psb", [128, 3 * 8 * 80], BF16))
        ptsb = E(nc.sbuf_tensor("ptsb", [128, 2 * 8 * 128], BF16))
        negm = E(nc.sbuf_tensor("negm", [128, 16], F32))
        sums = E(nc.sbuf_tensor("sums", [128, 16], F32))
        rinvf = E(nc.sbuf_tensor("rinvf", [128, 16], F32))
        rinvb = E(nc.sbuf_tensor("rinvb", [128, 16], BF16))
        mb16 = E(nc.sbuf_tensor("mb16", [128, T], BF16))
        rb16 = E(nc.sbuf_tensor("rb16", [128, T], BF16))
        w1b = [E(nc.sbuf_tensor(f"w1b{i}", [128, 4 * NCH * 128], BF16)) for i in range(2)]
        w2b = [E(nc.sbuf_tensor(f"w2b{i}", [128, NFC * 128], BF16)) for i in range(2)]
        pjb = [E(nc.sbuf_tensor(f"pjb{i}", [128, NCH * 512], BF16)) for i in range(2)]
        xgres = E(nc.sbuf_tensor("xgres", [128, NCH * 16], F32))
        xgsq = E(nc.sbuf_tensor("xgsq", [128, 2 * NCH * 16], BF16))
        xghat = E(nc.sbuf_tensor("xghat", [128, NCH * 16], BF16))
        hg = E(nc.sbuf_tensor("hg", [128, NFC * 16], BF16))
        rowm = E(nc.sbuf_tensor("rowm", [1, T], F32))
        rowmb = E(nc.sbuf_tensor("rowmb", [1, T], BF16))
        rowrb = E(nc.sbuf_tensor("rowrb", [1, T], BF16))
        rowvar = E(nc.sbuf_tensor("rowvar", [1, T], F32))
        rows2 = rowvar
        rowr = E(nc.sbuf_tensor("rowr", [1, T], F32))
        growm = E(nc.sbuf_tensor("growm", [1, 16], F32))
        grows2 = E(nc.sbuf_tensor("grows2", [1, 16], F32))
        growvar = E(nc.sbuf_tensor("growvar", [1, 16], F32))
        growr = E(nc.sbuf_tensor("growr", [1, 16], F32))
        b1s = E(nc.sbuf_tensor("b1s", [128, NFC], F32))
        b2rb = E(nc.sbuf_tensor("b2rb", [1, D], BF16))
        ident = E(nc.sbuf_tensor("ident", [128, 128], BF16))
        onesmean = E(nc.sbuf_tensor("onesmean", [128, 1], BF16))
        onesrow = E(nc.sbuf_tensor("onesrow", [1, 128], F32))
        onescol = E(nc.sbuf_tensor("onescol", [128, 1], F32))
        onesrow512 = E(nc.sbuf_tensor("onesrow512", [1, 512], BF16))
        maskstd = E(nc.sbuf_tensor("maskstd", [128, 80], BF16))
        maskq0 = E(nc.sbuf_tensor("maskq0", [128, 80], BF16))
        gacc1 = E(nc.sbuf_tensor("gacc1", [128, NCH], F32))
        gacc2 = E(nc.sbuf_tensor("gacc2", [128, 16], F32))
        growtot = E(nc.sbuf_tensor("growtot", [1, 2], F32))
        throw = E(nc.sbuf_tensor("throw", [1, 2], F32))
        nthrow = E(nc.sbuf_tensor("nthrow", [1, 2], F32))
        thrc = E(nc.sbuf_tensor("thrc", [128, 2], F32))
        nthrc = E(nc.sbuf_tensor("nthrc", [128, 2], F32))
        SA = E(nc.psum_tensor("SA", [128, 1024], F32))
        SB = E(nc.psum_tensor("SB", [128, 1024], F32))
        PSB2 = E(nc.psum_tensor("PSB2", [128, 1024], F32))
        PSB3 = E(nc.psum_tensor("PSB3", [128, 1024], F32))
        SA_bf = SA.bitcast(BF16)
        SB_bf = SB.bitcast(BF16)
        PSB2_bf = PSB2.bitcast(BF16)
        h32 = h.bitcast(F32)
        wstage = [h32[:, 0:4096], h32[:, 4096:8192]]
        wstage_full = h32[:, 0:8192]
        tmps3 = [xsq[:, 0:4096], xsq[:, 4096:8192], attnf[:, 0:4096]]
        TMPAL = [["xb"], ["sq"], ["attnf"]]
        tmps = [xsq[:, 0:4096], xsq[:, 4096:8192]]
        tmp_full = xsq[:, 0:8192]

        em = Em()
        sems = {}
        P4 = [(PSB2, 0, 4), (PSB2, T, 5), (PSB3, 0, 6), (PSB3, T, 7)]

        def p4view(i, rows=128, width=T):
            tns, off, bankid = P4[i % 4]
            return tns[0:rows, off:off + width], [f"pb{bankid}"]

        def bk(*ids):
            return [f"pb{i}" for i in ids]

        def dma(fn, reads=(), writes=(), sem=None, batch=False):
            em.op("sync", fn, reads, writes, sem=sem, batch=batch)

        def ln(xr, xrname, glob):
            em.op("scalar", lambda e: e.activation(xsq[:, 0:2048], xr[:, 0:2048],
                                                   AF.Copy),
                  reads=["xr_lo"], writes=["xb"])
            em.op("scalar", lambda e: e.activation(xsq[:, 2048:4096], xr[:, 2048:4096],
                                                   AF.Copy),
                  reads=["xr_hi"], writes=["xb"])
            em.op("vector", lambda e: e.tensor_tensor(xsq[:, 4096:8192], xsq[:, 0:4096],
                                                      xsq[:, 0:4096], ALU.mult),
                  reads=["xb"], writes=["sq"])
            for c in range(NCH):
                em.op("tensor",
                      lambda e, c=c: e.matmul(SA[0:1, 0:T], onesmean[:],
                                              xsq[:, c * T:(c + 1) * T],
                                              start=(c == 0), stop=(c == NCH - 1)),
                      reads=["xb", "onesmean"], writes=bk(0))
            for c in range(NCH):
                em.op("tensor",
                      lambda e, c=c: e.matmul(SA[0:1, 512:512 + T], onesmean[:],
                                              xsq[:, 4096 + c * T:4096 + (c + 1) * T],
                                              start=(c == 0), stop=(c == NCH - 1)),
                      reads=["sq", "onesmean"], writes=bk(1))
            em.op("vector", lambda e: e.tensor_copy(rowmb[:], SA[0:1, 0:T]),
                  reads=bk(0), writes=["rowmb"])
            em.op("vector", lambda e: e.tensor_tensor(SA[0:1, 0:T], rowmb[:], rowmb[:],
                                                      ALU.mult),
                  reads=["rowmb"], writes=bk(0))
            em.op("vector", lambda e: e.tensor_copy(rowvar[:], SA[0:1, 512:512 + T]),
                  reads=bk(1), writes=["rowvar"])
            em.op("vector", lambda e: e.tensor_tensor(rowvar[:], rowvar[:], SA[0:1, 0:T],
                                                      ALU.subtract),
                  reads=["rowvar"] + bk(0), writes=["rowvar"])
            em.op("vector", lambda e: e.tensor_scalar(rowvar[:], rowvar[:], 1.0, EPS,
                                                      ALU.mult, ALU.add),
                  reads=["rowvar"], writes=["rowvar"])
            em.op("scalar", lambda e: e.activation(rowr[:], rowvar[:], AF.Ln),
                  reads=["rowvar"], writes=["rowr"])
            em.op("scalar", lambda e: e.activation(rowrb[:], rowr[:], AF.Exp, scale=-0.5),
                  reads=["rowr"], writes=["rowrb"])
            em.op("tensor", lambda e: e.matmul(SB[:, 0:T], onesrow512[0:1, 0:128],
                                               rowmb[:], start=True, stop=True),
                  reads=["rowmb", "onesrow512"], writes=bk(2))
            em.op("tensor", lambda e: e.matmul(SB[:, 512:512 + T], onesrow512[0:1, 0:128],
                                               rowrb[:], start=True, stop=True),
                  reads=["rowrb", "onesrow512"], writes=bk(3))
            em.op("vector", lambda e: e.tensor_copy(mb16[:], SB[:, 0:T]),
                  reads=bk(2), writes=["mb16"])
            em.op("vector", lambda e: e.tensor_copy(rb16[:], SB[:, 512:512 + T]),
                  reads=bk(3), writes=["rb16"])
            for c in range(NCH):
                em.op("vector",
                      lambda e, c=c: e.tensor_tensor(xhat[:, c * T:(c + 1) * T],
                                                     xsq[:, c * T:(c + 1) * T],
                                                     mb16[:], ALU.subtract),
                      reads=["xb", "mb16"], writes=[f"xhs{c}"])
            for c in range(NCH):
                em.op("vector",
                      lambda e, c=c: e.tensor_tensor(xhat[:, c * T:(c + 1) * T],
                                                     xhat[:, c * T:(c + 1) * T],
                                                     rb16[:], ALU.mult),
                      reads=[f"xhs{c}", "rb16"], writes=[f"xh{c}", f"xhs{c}"])
            if glob:
                em.op("scalar", lambda e: e.activation(xgsq[:, 0:128], xgres[:], AF.Copy),
                      reads=["xgres"], writes=["gxb"])
                em.op("vector", lambda e: e.tensor_tensor(xgsq[:, 128:256], xgsq[:, 0:128],
                                                          xgsq[:, 0:128], ALU.mult),
                      reads=["gxb"], writes=["gsq"])
                for c in range(NCH):
                    em.op("tensor",
                          lambda e, c=c: e.matmul(PSB3[0:1, 0:16], onesmean[:],
                                                  xgsq[:, c * 16:(c + 1) * 16],
                                                  start=(c == 0), stop=(c == NCH - 1)),
                          reads=["gxb", "onesmean"], writes=bk(6))
                for c in range(NCH):
                    em.op("tensor",
                          lambda e, c=c: e.matmul(PSB3[0:1, 512:528], onesmean[:],
                                                  xgsq[:, 128 + c * 16:128 + (c + 1) * 16],
                                                  start=(c == 0), stop=(c == NCH - 1)),
                          reads=["gsq", "onesmean"], writes=bk(7))
                em.op("vector", lambda e: e.tensor_copy(growm[:], PSB3[0:1, 0:16]),
                      reads=bk(6), writes=["growm"])
                em.op("vector", lambda e: e.tensor_copy(grows2[:], PSB3[0:1, 512:528]),
                      reads=bk(7), writes=["grows2"])
                em.op("vector", lambda e: e.tensor_tensor(growvar[:], growm[:], growm[:],
                                                          ALU.mult),
                      reads=["growm"], writes=["growvar"])
                em.op("vector", lambda e: e.tensor_tensor(growvar[:], grows2[:], growvar[:],
                                                          ALU.subtract),
                      reads=["grows2", "growvar"], writes=["growvar"])
                em.op("vector", lambda e: e.tensor_scalar(growvar[:], growvar[:], 1.0, EPS,
                                                          ALU.mult, ALU.add),
                      reads=["growvar"], writes=["growvar"])
                em.op("scalar", lambda e: e.activation(growr[:], growvar[:], AF.Ln),
                      reads=["growvar"], writes=["growr"])
                em.op("scalar", lambda e: e.activation(growr[:], growr[:], AF.Exp, scale=-0.5),
                      reads=["growr"], writes=["growr"])
                em.op("tensor", lambda e: e.matmul(PSB3[:, 16:32], onesrow[:], growm[:],
                                                   start=True, stop=True),
                      reads=["growm", "onesrow"], writes=bk(6))
                em.op("tensor", lambda e: e.matmul(PSB3[:, 528:544], onesrow[:], growr[:],
                                                   start=True, stop=True),
                      reads=["growr", "onesrow"], writes=bk(7))
                mgb = PSB3[:, 16:32].rearrange("p k -> p () k").broadcast_to(
                    [128, NCH, 16])
                rgb = PSB3[:, 528:544].rearrange("p k -> p () k").broadcast_to(
                    [128, NCH, 16])
                em.op("vector",
                      lambda e, mgb=mgb: e.tensor_tensor(
                          xghat[:].rearrange("p (c k) -> p c k", c=NCH),
                          xgres[:].rearrange("p (c k) -> p c k", c=NCH), mgb, ALU.subtract),
                      reads=["xgres"] + bk(6), writes=["xghat"])
                em.op("vector",
                      lambda e, rgb=rgb: e.tensor_tensor(
                          xghat[:].rearrange("p (c k) -> p c k", c=NCH),
                          xghat[:].rearrange("p (c k) -> p c k", c=NCH), rgb, ALU.mult),
                      reads=["xghat"] + bk(7), writes=["xghat"])

        def ffn1(xr, xrname, glob, side_jobs=None):
            def w1dma(g):
                src = bass.AP(w1q_s.tensor, g * NCH * 128 * 512,
                              [[512, 128], [128 * 512, NCH], [1, 512]])
                buf = w1b[g % 2]
                em.op("sync",
                      lambda e, src=src, buf=buf: e.dma_start(
                          buf[:].rearrange("p (c r) -> p c r", c=NCH), src),
                      reads=["w1q_s"], writes=[f"w1b{g % 2}"], sem=f"w1b{g % 2}")
            w1dma(0)
            w1dma(1)
            for j in range(NFC):
                g = j // 4
                buf = w1b[g % 2]
                pview, bank = p4view(j)
                for c in range(NCH):
                    lhs = buf[:, c * 512 + (j % 4) * 128:c * 512 + (j % 4) * 128 + 128]
                    em.op("tensor",
                          lambda e, lhs=lhs, c=c, pview=pview: e.matmul(
                              pview, lhs, xhat[:, c * T:(c + 1) * T],
                              start=(c == 0), stop=(c == NCH - 1)),
                          reads=[f"w1b{g % 2}", f"xh{c}"], writes=bank)
                em.op("scalar",
                      lambda e, j=j, pview=pview: e.activation(
                          h[:, j * T:(j + 1) * T], pview, AF.Relu, bias=b1s[:, j:j + 1]),
                      reads=bank + ["b1s"], writes=[f"hh{j % 4}"])
                if glob:
                    gview = SA[:, 0:16] if j % 2 == 0 else SA[:, 512:528]
                    gbank = bk(0 + j % 2)
                    for c in range(NCH):
                        lhs = buf[:, c * 512 + (j % 4) * 128:c * 512 + (j % 4) * 128 + 128]
                        em.op("tensor",
                              lambda e, lhs=lhs, c=c, gview=gview: e.matmul(
                                  gview, lhs, xghat[:, c * 16:(c + 1) * 16],
                                  start=(c == 0), stop=(c == NCH - 1)),
                              reads=[f"w1b{g % 2}", "xghat"], writes=gbank)
                    em.op("scalar",
                          lambda e, j=j, gview=gview: e.activation(
                              hg[:, j * 16:(j + 1) * 16], gview, AF.Relu,
                              bias=b1s[:, j:j + 1]),
                          reads=gbank + ["b1s"], writes=["hg"])
                if j % 4 == 3 and g + 2 < 8:
                    w1dma(g + 2)
                if side_jobs:
                    for _ in range(2):
                        if side_jobs:
                            side_jobs.pop(0)()

        def ffn2(xr, xrname, glob):
            def w2dma(k):
                src = bass.AP(w2q_s.tensor, k * 8 * 128 * 512,
                              [[512, 128], [128 * 512, 8], [1, 512]])
                buf = w2b[k % 2]
                em.op("sync", lambda e, src=src, buf=buf: e.dma_start(
                          buf[:].rearrange("p (cg r) -> p cg r", cg=8), src),
                      reads=["w2q_s"], writes=[f"w2b{k % 2}"], sem=f"w2b{k % 2}")
            w2dma(0)
            w2dma(1)
            for k in range(NCH):
                buf = w2b[k % 2]
                pview, bank = p4view(k + 2)
                for c2 in range(NFC):
                    em.op("tensor",
                          lambda e, buf=buf, c2=c2, pview=pview: e.matmul(
                              pview, buf[:, c2 * 128:(c2 + 1) * 128],
                              h[:, c2 * T:(c2 + 1) * T], start=(c2 == 0), stop=False),
                          reads=[f"w2b{k % 2}", f"hh{c2 % 4}"], writes=bank)
                em.op("tensor",
                      lambda e, k=k, pview=pview: e.matmul(
                          pview, b2rb[0:1, k * 128:(k + 1) * 128], onesrow512[0:1, :],
                          start=False, stop=True),
                      reads=["b2rb", "onesrow512"], writes=bank)
                half = "lo" if k < 4 else "hi"
                em.op("vector",
                      lambda e, k=k, pview=pview: e.scalar_tensor_tensor(
                          xr[:, k * T:(k + 1) * T], pview, 0.5, xr[:, k * T:(k + 1) * T],
                          ALU.mult, ALU.add),
                      reads=bank + [f"xr_{half}"], writes=[f"xr_{half}"])
                if glob:
                    gview = SA[:, 0:16] if k % 2 == 0 else SA[:, 512:528]
                    gbank = bk(0 + k % 2)
                    for c2 in range(NFC):
                        em.op("tensor",
                              lambda e, buf=buf, c2=c2, gview=gview: e.matmul(
                                  gview, buf[:, c2 * 128:(c2 + 1) * 128],
                                  hg[:, c2 * 16:(c2 + 1) * 16],
                                  start=(c2 == 0), stop=False),
                              reads=[f"w2b{k % 2}", "hg"], writes=gbank)
                    em.op("tensor",
                          lambda e, k=k, gview=gview: e.matmul(
                              gview, b2rb[0:1, k * 128:(k + 1) * 128],
                              onesrow512[0:1, 0:16], start=False, stop=True),
                          reads=["b2rb", "onesrow512"], writes=gbank)
                    em.op("vector",
                          lambda e, k=k, gview=gview: e.scalar_tensor_tensor(
                              xgres[:, k * 16:(k + 1) * 16], gview, 0.5,
                              xgres[:, k * 16:(k + 1) * 16], ALU.mult, ALU.add),
                          reads=gbank + ["xgres"], writes=["xgres"])
                if k + 2 < NCH:
                    w2dma(k + 2)


        def ffn(xr, xrname, glob):
            ffn1(xr, xrname, glob)
            ffn2(xr, xrname, glob)

        def proj(glob):
            stream_q = [("q", 0), ("q", 1), ("k", 0), ("k", 1), ("v", 0), ("v", 1)]

            def pjdma(idx):
                pname, half = stream_q[idx]
                wscr = {"q": wq_s, "k": wk_s, "v": wv_s}[pname]
                bid = idx % 2
                src = bass.AP(wscr.tensor, half * 512,
                              [[NCH * D, 128], [D, NCH], [1, 512]])
                buf = pjb[bid]
                em.op("sync",
                      lambda e, src=src, buf=buf: e.dma_start(
                          buf[:].rearrange("p (c i) -> p c i", c=NCH), src),
                      reads=[f"{pname}_s"], writes=[f"pjb{bid}"], sem=f"pjb{bid}")
            pjdma(0)
            pjdma(1)
            bi = 0
            for pname, dstb, dstn in (("q", q_hd, "q_hd"), ("k", k_hd, "k_hd")):
                for half in range(2):
                    bid = bi % 2
                    buf = pjb[bid]
                    bi += 1
                    for hh in range(8):
                        hd = half * 8 + hh
                        pview, bank = p4view(hd)
                        pview = pview[0:64, :]
                        for c in range(NCH):
                            lhs = buf[:, c * 512 + hh * 64:c * 512 + hh * 64 + 64]
                            em.op("tensor",
                                  lambda e, lhs=lhs, c=c, pview=pview: e.matmul(
                                      pview, lhs, xhat[:, c * T:(c + 1) * T],
                                      start=(c == 0), stop=(c == NCH - 1)),
                                  reads=[f"pjb{bid}", f"xh{c}"], writes=bank)
                        em.op("scalar",
                              lambda e, hd=hd, pview=pview, dstb=dstb: e.activation(
                                  dstb[0:64, hd * T:(hd + 1) * T], pview, AF.Copy),
                              reads=bank, writes=[dstn])
                    if bi + 1 < 6:
                        pjdma(bi + 1)
            for half in range(2):
                bid = bi % 2
                buf = pjb[bid]
                bi += 1
                for blk in range(T // 64):
                    pview = PSB3[:, 0:T] if blk % 2 == 0 else PSB3[:, T:2 * T]
                    bank = bk(6 + blk % 2)
                    for c in range(NCH):
                        em.op("tensor",
                              lambda e, buf=buf, c=c, blk=blk, pview=pview: e.matmul(
                                  pview[0:64, :],
                                  xhat[:, c * T + blk * 64:c * T + blk * 64 + 64],
                                  buf[:, c * 512:(c + 1) * 512],
                                  start=(c == 0), stop=(c == NCH - 1)),
                              reads=[f"pjb{bid}", f"xh{c}"], writes=bank)
                    em.op("vector",
                          lambda e, blk=blk, half=half, pview=pview: e.tensor_copy(
                              vaug[0:64, blk * D + half * 512:blk * D + half * 512 + 512],
                              pview[0:64, :]),
                          reads=bank, writes=["vaug"])
                if bi + 1 < 6:
                    pjdma(bi + 1)
                if glob:
                    gview = PSB2[:, 0:T]
                    gbank = bk(4)
                    for c in range(NCH):
                        em.op("tensor",
                              lambda e, buf=buf, c=c, gview=gview: e.matmul(
                                  gview[64:80, :], xghat[:, c * 16:(c + 1) * 16],
                                  buf[:, c * 512:(c + 1) * 512],
                                  start=(c == 0), stop=(c == NCH - 1),
                                  tile_position=(0, 64)),
                              reads=[f"pjb{bid}", "xghat"], writes=gbank)
                    for blk in range(T // 64):
                        em.op("vector",
                              lambda e, blk=blk, half=half, gview=gview: e.tensor_copy(
                                  vaug[64:80,
                                       blk * D + half * 512:blk * D + half * 512 + 512],
                                  gview[64:80, :]),
                              reads=gbank, writes=["vaug"])
            if glob:
                kb = []
                for half in range(2):
                    bid = bi % 2
                    buf = pjb[bid]
                    bi += 1
                    src = bass.AP(wk_s.tensor, half * 4096,
                                  [[NCH * D, 128], [1, 4096]])
                    em.op("sync",
                          lambda e, src=src, buf=buf: e.dma_start(buf[:, 0:4096], src),
                          reads=["k_s"], writes=[f"pjb{bid}"], sem=f"pjb{bid}")
                    kb.append((buf, bid))
                for hd in range(H):
                    gview = PSB3[0:64, 0:16] if hd % 2 == 0 else PSB3[0:64, 512:528]
                    gbank = bk(6 + hd % 2)
                    for c in range(NCH):
                        buf, bid = kb[c // 4]
                        lhs = buf[:, (c % 4) * 1024 + hd * 64:(c % 4) * 1024 + hd * 64 + 64]
                        em.op("tensor",
                              lambda e, lhs=lhs, gview=gview, c=c: e.matmul(
                                  gview, lhs, xghat[:, c * 16:(c + 1) * 16],
                                  start=(c == 0), stop=(c == NCH - 1)),
                              reads=[f"pjb{bid}", "xghat"], writes=gbank)
                    em.op("scalar",
                          lambda e, hd=hd, gview=gview: e.activation(
                              kg_hd[0:64, hd * 16:(hd + 1) * 16], gview, AF.Copy),
                          reads=gbank, writes=["kg_hd"])

        def attn(t, xr, xrname):
            halves = [(p, g) for p in range(NT) for g in range(2)]

            def scores_stage(k):
                p, g = halves[k]
                hp = k % 2
                Sreg = SA if hp == 0 else SB
                sbanks = bk(2 * hp, 2 * hp + 1)
                for hh in range(8):
                    hd = g * 8 + hh
                    for m in range(2):
                        qb = 2 * p + m
                        qs = hd * T + qb * 64
                        em.op("tensor",
                              lambda e, qs=qs, hh=hh, m=m, Sreg=Sreg: e.matmul(
                                  Sreg[64 * m:64 * m + 64, 128 * hh:128 * hh + 64],
                                  q_hd[0:64, qs:qs + 64],
                                  k_hd[0:64, qs:qs + 64],
                                  start=True, stop=True,
                                  tile_position=(0, 64 * m)),
                              reads=["q_hd", "k_hd"], writes=sbanks)
                        em.op("tensor",
                              lambda e, qs=qs, hh=hh, m=m, hd=hd, Sreg=Sreg: e.matmul(
                                  Sreg[64 * m:64 * m + 64,
                                       128 * hh + 64:128 * hh + 80],
                                  q_hd[0:64, qs:qs + 64],
                                  kg_hd[0:64, hd * 16:hd * 16 + 16],
                                  start=True, stop=True,
                                  tile_position=(0, 64 * m)),
                              reads=["q_hd", "kg_hd"], writes=sbanks)

            def smx_stage(k):
                p, g = halves[k]
                hp = k % 2
                hq = k % 3
                Sreg = SA if hp == 0 else SB
                sbanks = bk(2 * hp, 2 * hp + 1)
                nmv = negm[:, 8 * hp:8 * hp + 8]
                smv = sums[:, 8 * hp:8 * hp + 8]
                esv = esrc[:, 640 * hp:640 * hp + 640]
                psv = psb[:, 640 * hq:640 * hq + 640]
                s3d = Sreg[:].rearrange("p (h k) -> p h k", h=8)[:, :, 0:80]
                em.op("vector",
                      lambda e, s3d=s3d, nmv=nmv: e.tensor_reduce(
                          nmv, s3d, axis=AX.X, op=ALU.max, negate=True),
                      reads=sbanks, writes=[f"negm{hp}"])
                nb = nmv.rearrange("p h -> p h ()").broadcast_to([128, 8, 80])
                em.op("vector",
                      lambda e, s3d=s3d, nb=nb, esv=esv: e.tensor_tensor(
                          esv.rearrange("p (h k) -> p h k", h=8), s3d, nb, ALU.add),
                      reads=sbanks + [f"negm{hp}"], writes=[f"esrc{hp}"])
                em.op("scalar",
                      lambda e, psv=psv, esv=esv: e.activation(psv, esv, AF.Exp),
                      reads=[f"esrc{hp}"], writes=[f"psb{hq}"])
                mask = maskq0 if (t == 0 and p == 0) else maskstd
                mname = "maskq0" if (t == 0 and p == 0) else "maskstd"
                mb = mask[:].rearrange("p k -> p () k").broadcast_to([128, 8, 80])
                em.op("gpsimd",
                      lambda e, mb=mb, psv=psv: e.tensor_tensor(
                          psv.rearrange("p (h k) -> p h k", h=8),
                          psv.rearrange("p (h k) -> p h k", h=8), mb, ALU.mult),
                      reads=[f"psb{hq}", mname], writes=[f"psb{hq}"])
                em.op("vector",
                      lambda e, psv=psv, smv=smv: e.tensor_reduce(
                          smv, psv.rearrange("p (h k) -> p h k", h=8),
                          axis=AX.X, op=ALU.add),
                      reads=[f"psb{hq}"], writes=[f"sums{hp}"])
                em.op("vector",
                      lambda e, hp=hp, smv=smv: e.reciprocal(
                          rinvf[:, 8 * hp:8 * hp + 8], smv),
                      reads=[f"sums{hp}"], writes=[f"rinvf{hp}"])
                em.op("vector",
                      lambda e, hp=hp: e.tensor_copy(rinvb[:, 8 * hp:8 * hp + 8],
                                                     rinvf[:, 8 * hp:8 * hp + 8]),
                      reads=[f"rinvf{hp}"], writes=[f"rinvb{hp}"])
                rb = rinvb[:, 8 * hp:8 * hp + 8].rearrange(
                    "p h -> p h ()").broadcast_to([128, 8, 80])
                em.op("gpsimd",
                      lambda e, rb=rb, psv=psv: e.tensor_tensor(
                          psv.rearrange("p (h k) -> p h k", h=8),
                          psv.rearrange("p (h k) -> p h k", h=8), rb, ALU.mult),
                      reads=[f"psb{hq}", f"rinvb{hp}"], writes=[f"psb{hq}"])

            def pt_stage(k):
                p, g = halves[k]
                hp = k % 2
                hq = k % 3
                psv = psb[:, 640 * hq:640 * hq + 640]
                ph = hp
                ptbank = bk(4 + ph)
                for hh in range(8):
                    em.op("tensor",
                          lambda e, hh=hh, ph=ph, psv=psv: e.transpose(
                              PSB2_bf[0:80, ph * 1024 + 128 * hh:
                                      ph * 1024 + 128 * hh + 128],
                              psv[:, 80 * hh:80 * hh + 80], ident[:]),
                          reads=[f"psb{hq}", "ident"], writes=ptbank)
                em.op("scalar",
                      lambda e, ph=ph: e.copy(
                          ptsb[0:80, ph * 1024:ph * 1024 + 1024],
                          PSB2_bf[0:80, ph * 1024:ph * 1024 + 1024]),
                      reads=ptbank, writes=[f"ptsb{ph}"])

            def av_stage(k):
                p, g = halves[k]
                hp = k % 2
                Sreg = SA if hp == 0 else SB
                sbanks = bk(2 * hp, 2 * hp + 1)
                ph = hp
                avbank = bk(6 + hp)
                av = PSB3[:, hp * 512:hp * 512 + 512]
                for hh in range(8):
                    hd = g * 8 + hh
                    cw = hh // 2
                    ob = 64 * (hd % 2)
                    for m in range(2):
                        qb = 2 * p + m
                        em.op("tensor",
                              lambda e, hd=hd, hh=hh, m=m, qb=qb, cw=cw, ob=ob,
                              av=av, ph=ph: e.matmul(
                                  av[ob:ob + 64,
                                     128 * cw + 64 * m:128 * cw + 64 * m + 64],
                                  vaug[0:80, qb * D + 64 * hd:qb * D + 64 * hd + 64],
                                  ptsb[0:80, ph * 1024 + 128 * hh + 64 * m:
                                       ph * 1024 + 128 * hh + 64 * m + 64],
                                  start=True, stop=True, tile_position=(0, ob)),
                              reads=["vaug", f"ptsb{ph}"], writes=avbank)
                em.op("scalar",
                      lambda e, g=g, p=p, av=av: e.copy(
                          bass.AP(attnf, (4 * g) * T + p * 128,
                                  [[NCH * T, 128], [T, 4], [1, 128]]),
                          av.rearrange("p (c q) -> p c q", c=4)),
                      reads=avbank, writes=["attnf"])


            def ptav_stage(k):
                p, g = halves[k]
                hp = k % 2
                Sreg = SA if hp == 0 else SB
                sbanks = bk(2 * hp, 2 * hp + 1)
                psv = psb[:, 640 * hp:640 * hp + 640]
                ph = hp
                ptbank = bk(4 + ph)
                for hh in range(8):
                    em.op("tensor",
                          lambda e, hh=hh, ph=ph, psv=psv: e.transpose(
                              PSB2_bf[0:80, ph * 1024 + 128 * hh:
                                      ph * 1024 + 128 * hh + 128],
                              psv[:, 80 * hh:80 * hh + 80], ident[:]),
                          reads=[f"psb{hp}", "ident"], writes=ptbank)
                em.op("scalar",
                      lambda e, ph=ph: e.copy(
                          ptsb[0:80, ph * 1024:ph * 1024 + 1024],
                          PSB2_bf[0:80, ph * 1024:ph * 1024 + 1024]),
                      reads=ptbank, writes=[f"ptsb{ph}"])
                avbank = bk(6 + hp)
                av = PSB3[:, hp * 512:hp * 512 + 512]
                for hh in range(8):
                    hd = g * 8 + hh
                    cw = hh // 2
                    ob = 64 * (hd % 2)
                    for m in range(2):
                        qb = 2 * p + m
                        em.op("tensor",
                              lambda e, hd=hd, hh=hh, m=m, qb=qb, cw=cw, ob=ob,
                              av=av, ph=ph: e.matmul(
                                  av[ob:ob + 64,
                                     128 * cw + 64 * m:128 * cw + 64 * m + 64],
                                  vaug[0:80, qb * D + 64 * hd:qb * D + 64 * hd + 64],
                                  ptsb[0:80, ph * 1024 + 128 * hh + 64 * m:
                                       ph * 1024 + 128 * hh + 64 * m + 64],
                                  start=True, stop=True, tile_position=(0, ob)),
                              reads=["vaug", f"ptsb{ph}"], writes=avbank)
                em.op("scalar",
                      lambda e, g=g, p=p, av=av: e.copy(
                          bass.AP(attnf, (4 * g) * T + p * 128,
                                  [[NCH * T, 128], [T, 4], [1, 128]]),
                          av.rearrange("p (c q) -> p c q", c=4)),
                      reads=avbank, writes=["attnf"])


            NH = len(halves)
            for k in range(NH):
                scores_stage(k)
                smx_stage(k)
                if k >= 2:
                    pt_stage(k - 2)
                if k >= 3:
                    av_stage(k - 3)
            pt_stage(NH - 2)
            av_stage(NH - 3)
            pt_stage(NH - 1)
            av_stage(NH - 2)
            av_stage(NH - 1)
            for half in range(2):
                src = bass.AP(wo_s.tensor, half * 512, [[NCH * D, 128], [D, NCH], [1, 512]])
                buf = pjb[half]
                em.op("sync",
                      lambda e, src=src, buf=buf: e.dma_start(
                          buf[:].rearrange("p (c i) -> p c i", c=NCH), src),
                      reads=["o_s"], writes=[f"pjb{half}"], sem=f"pjb{half}")
            for jc in range(NCH):
                half = jc // 4
                buf = pjb[half]
                pview, bank = p4view(jc)
                for c in range(NCH):
                    lhs = buf[:, c * 512 + (jc % 4) * 128:c * 512 + (jc % 4) * 128 + 128]
                    em.op("tensor",
                          lambda e, lhs=lhs, c=c, pview=pview: e.matmul(
                              pview, lhs, attnf[:, c * T:(c + 1) * T],
                              start=(c == 0), stop=(c == NCH - 1)),
                          reads=[f"pjb{half}", "attnf"], writes=bank)
                rhalf = "lo" if jc < 4 else "hi"
                em.op("vector",
                      lambda e, jc=jc, pview=pview: e.scalar_tensor_tensor(
                          xr[:, jc * T:(jc + 1) * T], pview, 1.0,
                          xr[:, jc * T:(jc + 1) * T], ALU.mult, ALU.add),
                      reads=bank + [f"xr_{rhalf}"], writes=[f"xr_{rhalf}"])

        def xload(t, hf):
            src = bass.AP(xfm.tensor, (hf * 4) * 128 * TOK + t * T,
                          [[TOK, 128], [128 * TOK, 4], [1, T]])
            xr = xres[0]
            dma(lambda e, src=src, xr=xr, hf=hf: e.dma_start(
                xr[:, hf * 4 * T:(hf * 4 + 4) * T].rearrange("p (c k) -> p c k", c=4),
                src),
                writes=[f"xr_{'lo' if hf == 0 else 'hi'}"], sem=f"xl{hf}")

        dma(lambda e: e.dma_start(b1s[:], b1in), writes=["b1s"], sem="const", batch=True)
        dma(lambda e: e.dma_start(ident[:], identin), writes=["ident"], sem="const", batch=True)
        dma(lambda e: e.dma_start(onesmean[:], onesmean_in), writes=["onesmean"], sem="const", batch=True)
        dma(lambda e: e.dma_start(onesrow[:], onesrow_in), writes=["onesrow"], sem="const", batch=True)
        dma(lambda e: e.dma_start(onescol[:], onescol_in), writes=["onescol"], sem="const", batch=True)
        dma(lambda e: e.dma_start(onesrow512[:], onesrow512_in), writes=["onesrow512"], sem="const", batch=True)
        dma(lambda e: e.dma_start(maskstd[:], maskstd_in), writes=["maskstd"], sem="const", batch=True)
        dma(lambda e: e.dma_start(maskq0[:], maskq0_in), writes=["maskq0"], sem="const", batch=True)
        dma(lambda e: e.dma_start(rowm[0:1, 0:512], b2row[0:1, 0:512]), writes=["rowm"], sem="const", batch=True)
        dma(lambda e: e.dma_start(rows2[0:1, 0:512], b2row[0:1, 512:1024]), writes=["rows2"], sem="const", batch=True)
        em.settle(["b1s", "ident", "onesmean", "onesrow", "onescol", "onesrow512", "maskstd", "maskq0", "esel", "rowm", "rows2"], "const")
        em.op("scalar", lambda e: e.activation(b2rb[0:1, 0:512], rowm[0:1, 0:512], AF.Copy),
              reads=["rowm"], writes=["b2rb"])
        em.op("scalar", lambda e: e.activation(b2rb[0:1, 512:1024], rows2[0:1, 0:512], AF.Copy),
              reads=["rows2"], writes=["b2rb"])

        xload(0, 0)
        xload(0, 1)
        gsrc = bass.AP(xgin.tensor, 0, [[16, 128], [128 * 16, NCH], [1, 16]])
        dma(lambda e, gsrc=gsrc: e.dma_start(
            xgres[:].rearrange("p (c k) -> p c k", c=NCH), gsrc),
            writes=["xgres"], sem="xg")
        ln(xres[0], "xres0", True)

        for c in range(NCH):
            stg = wstage[c % 2]
            src = w1t[c * 128:(c + 1) * 128, :]
            dma(lambda e, stg=stg, src=src: e.dma_start(stg, src), writes=[f"wst{c % 2}"],
                sem=f"stg{c % 2}")
            em.op("vector",
                  lambda e, stg=stg, c=c: e.tensor_reduce(
                      gacc1[:, c:c + 1], stg, axis=AX.X, op=ALU.add,
                      apply_absolute_value=True),
                  reads=[f"wst{c % 2}"], writes=["gacc1"])
        em.op("tensor", lambda e: e.matmul(SA[0:1, 0:NCH], onescol[:], gacc1[:],
                                           start=True, stop=True),
              reads=["gacc1", "onescol"], writes=bk(0))
        em.op("vector", lambda e: e.tensor_reduce(growtot[0:1, 0:1], SA[0:1, 0:NCH],
                                                  axis=AX.X, op=ALU.add),
              reads=bk(0), writes=["growtot"])
        em.op("scalar", lambda e: e.activation(throw[0:1, 0:1], growtot[0:1, 0:1], AF.Copy,
                                               scale=0.5 / (D * DFF), bias=0.5 * QEPS),
              reads=["growtot"], writes=["throw"])
        em.op("scalar", lambda e: e.activation(nthrow[0:1, 0:1], growtot[0:1, 0:1], AF.Copy,
                                               scale=-0.5 / (D * DFF), bias=-0.5 * QEPS),
              reads=["growtot"], writes=["nthrow"])
        em.op("tensor", lambda e: e.matmul(SA[:, 0:1], onesrow[:], throw[0:1, 0:1],
                                           start=True, stop=True),
              reads=["throw", "onesrow"], writes=bk(0))
        em.op("tensor", lambda e: e.matmul(SA[:, 512:513], onesrow[:], nthrow[0:1, 0:1],
                                           start=True, stop=True),
              reads=["nthrow", "onesrow"], writes=bk(1))
        em.op("scalar", lambda e: e.copy(thrc[:, 0:1], SA[:, 0:1]), reads=bk(0),
              writes=["thrc"])
        em.op("scalar", lambda e: e.copy(nthrc[:, 0:1], SA[:, 512:513]), reads=bk(1),
              writes=["nthrc"])

        for c in range(NCH):
            stg = wstage[c % 2]
            src = w1t[c * 128:(c + 1) * 128, :]
            ia, ib = (2 * c) % 3, (2 * c + 1) % 3
            ta, tb = tmps3[ia], tmps3[ib]
            dma(lambda e, stg=stg, src=src: e.dma_start(stg, src),
                reads=[f"w1q{c % 2}"], writes=[f"wst{c % 2}"], sem=f"stg{c % 2}")
            em.op("vector",
                  lambda e, stg=stg, ta=ta: e.tensor_scalar(
                      ta, stg, thrc[:, 0:1], None, ALU.is_gt),
                  reads=[f"wst{c % 2}", "thrc"], writes=[f"tmp{ia}"] + TMPAL[ia])
            em.op("vector",
                  lambda e, stg=stg, tb=tb: e.tensor_scalar(
                      tb, stg, nthrc[:, 0:1], None, ALU.is_lt),
                  reads=[f"wst{c % 2}", "nthrc"], writes=[f"tmp{ib}"] + TMPAL[ib])
            em.op("vector",
                  lambda e, ta=ta, tb=tb: e.tensor_tensor(ta, ta, tb, ALU.subtract),
                  reads=[f"tmp{ia}", f"tmp{ib}"], writes=[f"tmp{ia}", f"wst{c % 2}done"])
            dst = bass.AP(w1q_s.tensor, c * 128 * 512,
                          [[512, 128], [NCH * 128 * 512, 8], [1, 512]])
            em.op("scalar",
                  lambda e, ta=ta, dst=dst: e.dma_start(
                      dst, ta.rearrange("p (jg r) -> p jg r", jg=8)),
                  reads=[f"tmp{ia}"], writes=["w1q_s", f"w1q{c % 2}"],
                  sem="w1qs", batch=True)

        vaug32 = vaug.bitcast(F32)
        vstg = [vaug32[:, 0:2048], vaug32[:, 2048:4096]]
        w2jobs = []

        def w2p1_job(i):
            stg = vstg[i % 2]
            srcw = bass.AP(w2t.tensor, (2 * i) * 128 * D,
                           [[D, 128], [128 * D, 2], [1, D]])
            dma(lambda e, stg=stg, srcw=srcw: e.dma_start(
                stg.rearrange("p (c e) -> p c e", c=2), srcw),
                writes=[f"w2stg{i % 2}"], sem=f"stg2{i % 2}")
            em.op("vector",
                  lambda e, stg=stg, i=i: e.tensor_reduce(
                      gacc2[:, i:i + 1], stg, axis=AX.X, op=ALU.add,
                      apply_absolute_value=True),
                  reads=[f"w2stg{i % 2}"], writes=["gacc2"])

        def w2gamma_job():
            em.op("tensor", lambda e: e.matmul(SA[0:1, 0:16], onescol[:], gacc2[:],
                                               start=True, stop=True),
                  reads=["gacc2", "onescol"], writes=bk(0))
            em.op("vector", lambda e: e.tensor_reduce(growtot[0:1, 1:2], SA[0:1, 0:16],
                                                      axis=AX.X, op=ALU.add),
                  reads=bk(0), writes=["growtot"])
            em.op("scalar", lambda e: e.activation(throw[0:1, 1:2], growtot[0:1, 1:2],
                                                   AF.Copy, scale=0.5 / (D * DFF),
                                                   bias=0.5 * QEPS),
                  reads=["growtot"], writes=["throw"])
            em.op("scalar", lambda e: e.activation(nthrow[0:1, 1:2], growtot[0:1, 1:2],
                                                   AF.Copy, scale=-0.5 / (D * DFF),
                                                   bias=-0.5 * QEPS),
                  reads=["growtot"], writes=["nthrow"])
            em.op("tensor", lambda e: e.matmul(SA[:, 0:1], onesrow[:], throw[0:1, 1:2],
                                               start=True, stop=True),
                  reads=["throw", "onesrow"], writes=bk(0))
            em.op("tensor", lambda e: e.matmul(SA[:, 512:513], onesrow[:],
                                               nthrow[0:1, 1:2], start=True, stop=True),
                  reads=["nthrow", "onesrow"], writes=bk(1))
            em.op("scalar", lambda e: e.copy(thrc[:, 1:2], SA[:, 0:1]), reads=bk(0),
                  writes=["thrc"])
            em.op("scalar", lambda e: e.copy(nthrc[:, 1:2], SA[:, 512:513]), reads=bk(1),
                  writes=["nthrc"])

        def w2p2_job(i):
            stg = vstg[i % 2]
            srcw = bass.AP(w2t.tensor, (2 * i) * 128 * D,
                           [[D, 128], [128 * D, 2], [1, D]])
            isl = i % 3
            slot = tmps3[isl]
            ta = slot[:, 0:2048].rearrange("p (k c e) -> p k c e", k=NCH, c=2)
            tb = slot[:, 2048:4096].rearrange("p (k c e) -> p k c e", k=NCH, c=2)
            sview = stg.rearrange("p (c k e) -> p k c e", c=2, k=NCH)
            dma(lambda e, stg=stg, srcw=srcw: e.dma_start(
                stg.rearrange("p (c e) -> p c e", c=2), srcw),
                reads=[f"w2w{i % 2}"], writes=[f"w2stg{i % 2}"], sem=f"stg2{i % 2}")
            em.op("vector",
                  lambda e, sview=sview, ta=ta: e.tensor_scalar(
                      ta, sview, thrc[:, 1:2], None, ALU.is_gt),
                  reads=[f"w2stg{i % 2}", "thrc"], writes=[f"tmp{isl}"] + TMPAL[isl])
            em.op("vector",
                  lambda e, sview=sview, tb=tb: e.tensor_scalar(
                      tb, sview, nthrc[:, 1:2], None, ALU.is_lt),
                  reads=[f"w2stg{i % 2}", "nthrc"],
                  writes=[f"tmp{isl}", f"w2w{i % 2}"])
            em.op("vector",
                  lambda e, ta=ta, tb=tb: e.tensor_tensor(ta, ta, tb, ALU.subtract),
                  reads=[f"tmp{isl}"], writes=[f"tmp{isl}"])
            dstw = bass.AP(w2q_s.tensor, (i // 2) * 128 * 512 + (i % 2) * 256,
                           [[512, 128], [8 * 128 * 512, NCH], [1, 256]])
            em.op("scalar",
                  lambda e, slot=slot, dstw=dstw: e.dma_start(
                      dstw, slot[:, 0:2048].rearrange("p (k r) -> p k r", k=NCH)),
                  reads=[f"tmp{isl}"], writes=["w2q_s"], sem="w2qs", batch=True)

        for i in range(16):
            w2jobs.append(lambda i=i: w2p1_job(i))
        w2jobs.append(w2gamma_job)
        for i in range(16):
            w2jobs.append(lambda i=i: w2p2_job(i))


        ffn1(xres[0], "xres0", True, side_jobs=w2jobs)
        while w2jobs:
            w2jobs.pop(0)()

        ffn2(xres[0], "xres0", True)


        qi = 0
        for name, win, wscr, scale in (("q", wqt, wq_s, 0.125), ("k", wkt, wk_s, 1.0),
                                       ("v", wvt, wv_s, 1.0), ("o", wot, wo_s, 1.0)):
            for half in range(2):
                srcw = bass.AP(win.tensor, (half * 4) * 128 * D,
                               [[D, 128], [128 * D, 4], [1, D]])
                isl = qi % 3
                slot = tmps3[isl]
                em.op("scalar", lambda e, srcw=srcw: e.dma_start(
                    vaug32[:].rearrange("p (c i) -> p c i", c=4), srcw),
                    writes=["w2stg0", "w2stg1"], sem="stg20")
                em.op("scalar",
                      lambda e, scale=scale, slot=slot: e.activation(
                          slot, vaug32[:], AF.Copy, scale=scale),
                      reads=["w2stg0", "w2stg1"],
                      writes=[f"tmp{isl}"] + TMPAL[isl])
                em.op("scalar",
                      lambda e, wscr=wscr, half=half, slot=slot: e.dma_start(
                          bass.AP(wscr.tensor, half * 4096, [[8192, 128], [1, 4096]]),
                          slot),
                      reads=[f"tmp{isl}"], writes=[f"{name}_s"], sem="qkvs")
                qi += 1

        em.op("vector", lambda e: e.engine_nop(),
              writes=["xb", "sq", "wst0", "wst1", "tmp0", "tmp1",
                      "tmp2", "w2stg0", "w2stg1", "vaug", "attnf"])

        import os
        dbg = os.environ.get("MK_DEBUG_PHASES", "")
        for t in range(NT):
            xr = xres[0]
            xrname = "xres0"
            if t > 0:
                xload(t, 0)
                xload(t, 1)
            glob = (t == 0)
            if t > 0:
                ln(xr, xrname, glob)
            if "noffn1" not in dbg and t > 0:
                ffn(xr, xrname, glob)
            if "s1" not in dbg:
                ln(xr, xrname, glob)
                proj(glob)
                if "noattn" not in dbg:
                    attn(t, xr, xrname)
                if "s2" not in dbg:
                    ln(xr, xrname, False)
                    ffn(xr, xrname, False)
            for hf in range(2):
                dst = bass.AP(yout.tensor, (hf * 4) * 128 * TOK + t * T,
                              [[TOK, 128], [128 * TOK, 4], [1, T]])
                dma(lambda e, dst=dst, xr=xr, hf=hf: e.dma_start(
                    dst, xr[:, hf * 4 * T:(hf * 4 + 4) * T].rearrange(
                        "p (c k) -> p c k", c=4)),
                    reads=[f"xr_{'lo' if hf == 0 else 'hi'}"],
                    writes=[f"yout{t}{hf}"], sem=f"ys{hf}")

        em.final_wait(["ys0", "ys1"])

        em.sparsify_incs()
        for e in ENGINES:
            if e != "sync":
                sems[e] = E(nc.semaphore(f"sem_{e}"))
        for s in em.dma_sems:
            sems[s] = E(nc.semaphore(f"sem_{s}"))
        block = E(nc.Block())

        def emit(engobj, ename):
            for fn, waits, inc, semname, _v in em.ops[ename]:
                for (e2, v) in waits:
                    engobj.wait_ge(sems[e2], v)
                if fn is not None:
                    ins = fn(engobj)
                    if inc:
                        ins.then_inc(sems[semname], inc)

        @block.sync
        def _(sync):
            emit(sync, "sync")

        @block.tensor
        def _(tensor):
            emit(tensor, "tensor")

        @block.vector
        def _(vector):
            emit(vector, "vector")

        @block.scalar
        def _(scalar):
            emit(scalar, "scalar")

        @block.gpsimd
        def _(gpsimd):
            emit(gpsimd, "gpsimd")

    return nc


_NC_CACHE = None


def kernel(L0_attn_hidden_states, L0_attn_Wq, L0_attn_Wk, L0_attn_Wv, L0_attn_Wo,
           L0_ffn_W1, L0_ffn_b1, L0_ffn_W2, L0_ffn_b2):
    global _NC_CACHE
    x = np.asarray(L0_attn_hidden_states, dtype=np.float32)
    bf = ml_dtypes.bfloat16
    w1tA = np.ascontiguousarray(np.asarray(L0_ffn_W1, dtype=np.float32).T)
    w2tA = np.ascontiguousarray(np.asarray(L0_ffn_W2, dtype=np.float32).T)
    wqtA = np.ascontiguousarray(np.asarray(L0_attn_Wq, dtype=np.float32).T)
    wktA = np.ascontiguousarray(np.asarray(L0_attn_Wk, dtype=np.float32).T)
    wvtA = np.ascontiguousarray(np.asarray(L0_attn_Wv, dtype=np.float32).T)
    wotA = np.ascontiguousarray(np.asarray(L0_attn_Wo, dtype=np.float32).T)
    b1r = np.ascontiguousarray(np.asarray(L0_ffn_b1, dtype=np.float32).reshape(NFC, 128).T)
    b2r = np.ascontiguousarray(np.asarray(L0_ffn_b2, dtype=np.float32)[None, :])
    ident = np.eye(128, dtype=bf)
    onesmean = np.full((128, 1), 1.0 / D, dtype=np.float32).astype(bf)
    onesrow = np.ones((1, 128), dtype=np.float32)
    onescol = np.ones((128, 1), dtype=np.float32)
    onesrow512 = np.ones((1, 512), dtype=bf)
    tril = np.tril(np.ones((64, 64), dtype=np.float32))
    mrow = np.concatenate([tril, np.ones((64, 16), np.float32)], 1)
    maskstd = np.concatenate([mrow, mrow], 0).astype(bf)
    eselA = np.zeros((16, D), dtype=np.float32)
    for hd in range(16):
        eselA[hd, hd * 64:(hd + 1) * 64] = 1.0
    eselA = eselA.astype(bf)

    in_maps = []
    for c in range(NCORE):
        b, half = c // 2, c % 2
        xfmA = np.ascontiguousarray(x[b, half * TOK:(half + 1) * TOK, :].T)
        xgA = np.ascontiguousarray(x[b, 0:16, :].T)
        gm = 1.0 if half == 1 else 0.0
        mq0 = np.concatenate([
            np.concatenate([tril, np.full((64, 16), gm, np.float32)], 1), mrow],
            0).astype(bf)
        in_maps.append({
            "xfm": xfmA, "xgin": xgA,
            "w1t": w1tA, "w2t": w2tA, "wqt": wqtA, "wkt": wktA, "wvt": wvtA, "wot": wotA,
            "b1in": b1r, "b2row": b2r,
            "identin": ident, "onesmean_in": onesmean, "onesrow_in": onesrow,
            "onescol_in": onescol, "onesrow512_in": onesrow512,
            "maskstd_in": maskstd, "maskq0_in": mq0, "esel_in": eselA,
        })

    if _NC_CACHE is None:
        _NC_CACHE = build_nc()
    res = run_bass_kernel_spmd(_NC_CACHE, in_maps, core_ids=list(range(NCORE)))

    out = np.empty((B, S, D), dtype=np.float32)
    for c in range(NCORE):
        b, half = c // 2, c % 2
        out[b, half * TOK:(half + 1) * TOK, :] = np.asarray(
            res.results[c]["y"], dtype=np.float32).T
    return out


# version 26
# speedup vs baseline: 1.0158x; 1.0055x over previous
import numpy as np
import ml_dtypes
from contextlib import ExitStack

import concourse.bass as bass
import concourse.mybir as mybir
from concourse.bass_utils import run_bass_kernel_spmd

F32 = mybir.dt.float32
BF16 = mybir.dt.bfloat16
AF = mybir.ActivationFunctionType
ALU = mybir.AluOpType
AX = mybir.AxisListType

B, S, D = 4, 4096, 1024
H, HD = 16, 64
DFF = 4096
NCORE = 8
TOK = 2048
T = 512
NT = TOK // T
NCH = D // 128
NFC = DFF // 128
EPS = 1e-5
QEPS = 1e-5

ENGINES = ("tensor", "vector", "scalar", "sync", "gpsimd")


class Em:

    def __init__(self):
        self.ops = {e: [] for e in ENGINES}
        self.count = {}
        self.last_w = {}
        self.readers = {}
        self.waited = {e: {} for e in ENGINES}
        self.dma_sems = {}

    def op(self, eng, fn, reads=(), writes=(), sem=None, batch=False):
        is_dma = sem is not None
        if is_dma:
            semname = sem
            if semname not in self.dma_sems:
                self.dma_sems[semname] = batch
        else:
            semname = eng
        need = {}

        def add(dep):
            s2, v = dep
            if s2 == semname and eng == "tensor" and not is_dma:
                return
            if v > need.get(s2, 0):
                need[s2] = v

        for r in reads:
            lw = self.last_w.get(r)
            if lw:
                add(lw)
        for w in writes:
            lw = self.last_w.get(w)
            if lw:
                add(lw)
            for rd in self.readers.get(w, ()):
                add(rd)
        if is_dma and not batch:
            prev = self.count.get(semname, 0)
            if prev > need.get(semname, 0):
                need[semname] = prev
        wd = self.waited[eng]
        waits = []
        for s2, v in sorted(need.items()):
            if v > wd.get(s2, 0):
                waits.append((s2, v))
                wd[s2] = v
        inc = 16 if is_dma else 1
        self.count[semname] = self.count.get(semname, 0) + inc
        me = (semname, self.count[semname])
        for r in reads:
            self.readers.setdefault(r, []).append(me)
        for w in writes:
            self.last_w[w] = me
            self.readers[w] = []
        self.ops[eng].append((fn, waits, inc, semname))

    def settle(self, resources, semname):
        me = (semname, self.count.get(semname, 0))
        for r in resources:
            self.last_w[r] = me
            self.readers[r] = []

    def final_wait(self, semnames):
        waits = [(s, self.count.get(s, 0)) for s in semnames]
        self.ops["scalar"].append((None, waits, 0, "scalar"))


def build_nc():
    nc = bass.Bass("TRN2", target_bir_lowering=False)

    xfm = nc.dram_tensor("xfm", [D, TOK], F32, kind="ExternalInput").ap()
    xgin = nc.dram_tensor("xgin", [D, 16], F32, kind="ExternalInput").ap()
    w1t = nc.dram_tensor("w1t", [D, DFF], F32, kind="ExternalInput").ap()
    w2t = nc.dram_tensor("w2t", [DFF, D], F32, kind="ExternalInput").ap()
    wqt = nc.dram_tensor("wqt", [D, D], F32, kind="ExternalInput").ap()
    wkt = nc.dram_tensor("wkt", [D, D], F32, kind="ExternalInput").ap()
    wvt = nc.dram_tensor("wvt", [D, D], F32, kind="ExternalInput").ap()
    wot = nc.dram_tensor("wot", [D, D], F32, kind="ExternalInput").ap()
    b1in = nc.dram_tensor("b1in", [128, NFC], F32, kind="ExternalInput").ap()
    b2row = nc.dram_tensor("b2row", [1, D], F32, kind="ExternalInput").ap()
    identin = nc.dram_tensor("identin", [128, 128], BF16, kind="ExternalInput").ap()
    onesmean_in = nc.dram_tensor("onesmean_in", [128, 1], BF16, kind="ExternalInput").ap()
    onesrow_in = nc.dram_tensor("onesrow_in", [1, 128], F32, kind="ExternalInput").ap()
    onescol_in = nc.dram_tensor("onescol_in", [128, 1], F32, kind="ExternalInput").ap()
    onesrow512_in = nc.dram_tensor("onesrow512_in", [1, 512], BF16, kind="ExternalInput").ap()
    maskstd_in = nc.dram_tensor("maskstd_in", [128, 80], BF16, kind="ExternalInput").ap()
    maskq0_in = nc.dram_tensor("maskq0_in", [128, 80], BF16, kind="ExternalInput").ap()
    esel_in = nc.dram_tensor("esel_in", [16, D], BF16, kind="ExternalInput").ap()
    yout = nc.dram_tensor("y", [D, TOK], F32, kind="ExternalOutput").ap()

    w1q_s = nc.dram_tensor("w1q_s", [NFC * 128 * NCH * 128], BF16, kind="Internal").ap()
    w2q_s = nc.dram_tensor("w2q_s", [NCH * 128 * NFC * 128], BF16, kind="Internal").ap()
    wq_s = nc.dram_tensor("wq_s", [128 * NCH * D], BF16, kind="Internal").ap()
    wk_s = nc.dram_tensor("wk_s", [128 * NCH * D], BF16, kind="Internal").ap()
    wv_s = nc.dram_tensor("wv_s", [128 * NCH * D], BF16, kind="Internal").ap()
    wo_s = nc.dram_tensor("wo_s", [128 * NCH * D], BF16, kind="Internal").ap()

    with ExitStack() as st:
        E = st.enter_context
        xres = [E(nc.sbuf_tensor("xres0", [128, NCH * T], F32))] * 2
        xsq = E(nc.sbuf_tensor("xsq", [128, 2 * NCH * T], BF16))
        xhat = E(nc.sbuf_tensor("xhat", [128, NCH * T], BF16))
        h = E(nc.sbuf_tensor("h", [128, NFC * T], BF16))
        q_hd = E(nc.sbuf_tensor("q_hd", [64, H * T], BF16))
        k_hd = E(nc.sbuf_tensor("k_hd", [64, H * 640], BF16))
        vaug = E(nc.sbuf_tensor("vaug", [128, (T // 64) * D], BF16))

        attnf = E(nc.sbuf_tensor("attnf", [128, NCH * T], BF16))
        esrc = E(nc.sbuf_tensor("esrc", [128, 8 * 80], F32))
        psb = E(nc.sbuf_tensor("psb", [128, 3 * 8 * 80], BF16))
        ptsb = E(nc.sbuf_tensor("ptsb", [128, 2 * 8 * 128], BF16))
        negm = E(nc.sbuf_tensor("negm", [128, 16], F32))
        sums = E(nc.sbuf_tensor("sums", [128, 16], F32))
        rinvf = E(nc.sbuf_tensor("rinvf", [128, 16], F32))
        rinvb = E(nc.sbuf_tensor("rinvb", [128, 16], BF16))
        mb16 = E(nc.sbuf_tensor("mb16", [128, T], BF16))
        rb16 = E(nc.sbuf_tensor("rb16", [128, T], BF16))
        w1b = [E(nc.sbuf_tensor(f"w1b{i}", [128, 4 * NCH * 128], BF16)) for i in range(2)]
        w2b = [E(nc.sbuf_tensor(f"w2b{i}", [128, NFC * 128], BF16)) for i in range(2)]
        pjb = [E(nc.sbuf_tensor(f"pjb{i}", [128, NCH * 512], BF16)) for i in range(2)]
        xgres = E(nc.sbuf_tensor("xgres", [128, NCH * 16], F32))
        xgsq = E(nc.sbuf_tensor("xgsq", [128, 2 * NCH * 16], BF16))
        xghat = E(nc.sbuf_tensor("xghat", [128, NCH * 16], BF16))
        hg = E(nc.sbuf_tensor("hg", [128, NFC * 16], BF16))
        rowmb = E(nc.sbuf_tensor("rowmb", [1, T], BF16))
        rowrb = E(nc.sbuf_tensor("rowrb", [1, T], BF16))
        rowvar = E(nc.sbuf_tensor("rowvar", [1, T], F32))
        rowm = rowvar
        rows2 = rowvar
        rowr = E(nc.sbuf_tensor("rowr", [1, T], F32))
        growm = E(nc.sbuf_tensor("growm", [1, 16], F32))
        grows2 = E(nc.sbuf_tensor("grows2", [1, 16], F32))
        growvar = E(nc.sbuf_tensor("growvar", [1, 16], F32))
        growr = E(nc.sbuf_tensor("growr", [1, 16], F32))
        b1s = E(nc.sbuf_tensor("b1s", [128, NFC], F32))
        b2rb = E(nc.sbuf_tensor("b2rb", [1, D], BF16))
        ident = E(nc.sbuf_tensor("ident", [128, 128], BF16))
        onesmean = E(nc.sbuf_tensor("onesmean", [128, 1], BF16))
        onesrow = E(nc.sbuf_tensor("onesrow", [1, 128], F32))
        onescol = E(nc.sbuf_tensor("onescol", [128, 1], F32))
        onesrow512 = E(nc.sbuf_tensor("onesrow512", [1, 512], BF16))
        maskstd = E(nc.sbuf_tensor("maskstd", [128, 80], BF16))
        maskq0 = E(nc.sbuf_tensor("maskq0", [128, 80], BF16))
        gacc1 = E(nc.sbuf_tensor("gacc1", [128, NCH], F32))
        gacc2 = E(nc.sbuf_tensor("gacc2", [128, 16], F32))
        growtot = E(nc.sbuf_tensor("growtot", [1, 2], F32))
        throw = E(nc.sbuf_tensor("throw", [1, 2], F32))
        nthrow = E(nc.sbuf_tensor("nthrow", [1, 2], F32))
        thrc = E(nc.sbuf_tensor("thrc", [128, 2], F32))
        nthrc = E(nc.sbuf_tensor("nthrc", [128, 2], F32))
        SA = E(nc.psum_tensor("SA", [128, 1024], F32))
        SB = E(nc.psum_tensor("SB", [128, 1024], F32))
        PSB2 = E(nc.psum_tensor("PSB2", [128, 1024], F32))
        PSB3 = E(nc.psum_tensor("PSB3", [128, 1024], F32))
        SA_bf = SA.bitcast(BF16)
        SB_bf = SB.bitcast(BF16)
        PSB2_bf = PSB2.bitcast(BF16)
        h32 = h.bitcast(F32)
        wstage = [h32[:, 0:4096], h32[:, 4096:8192]]
        wstage_full = h32[:, 0:8192]
        tmps3 = [xsq[:, 0:4096], xsq[:, 4096:8192], attnf[:, 0:4096]]
        TMPAL = [["xb"], ["sq"], ["attnf"]]
        tmps = [xsq[:, 0:4096], xsq[:, 4096:8192]]
        tmp_full = xsq[:, 0:8192]

        em = Em()
        sems = {}
        P4 = [(PSB2, 0, 4), (PSB2, T, 5), (PSB3, 0, 6), (PSB3, T, 7)]

        def p4view(i, rows=128, width=T):
            tns, off, bankid = P4[i % 4]
            return tns[0:rows, off:off + width], [f"pb{bankid}"]

        def bk(*ids):
            return [f"pb{i}" for i in ids]

        def dma(fn, reads=(), writes=(), sem=None, batch=False):
            em.op("sync", fn, reads, writes, sem=sem, batch=batch)

        def ln(xr, xrname, glob):
            em.op("scalar", lambda e: e.activation(xsq[:, 0:2048], xr[:, 0:2048],
                                                   AF.Copy),
                  reads=["xr_lo"], writes=["xb"])
            em.op("scalar", lambda e: e.activation(xsq[:, 2048:4096], xr[:, 2048:4096],
                                                   AF.Copy),
                  reads=["xr_hi"], writes=["xb"])
            em.op("vector", lambda e: e.tensor_tensor(xsq[:, 4096:8192], xsq[:, 0:4096],
                                                      xsq[:, 0:4096], ALU.mult),
                  reads=["xb"], writes=["sq"])
            for c in range(NCH):
                em.op("tensor",
                      lambda e, c=c: e.matmul(SA[0:1, 0:T], onesmean[:],
                                              xsq[:, c * T:(c + 1) * T],
                                              start=(c == 0), stop=(c == NCH - 1)),
                      reads=["xb", "onesmean"], writes=bk(0))
            for c in range(NCH):
                em.op("tensor",
                      lambda e, c=c: e.matmul(SA[0:1, 512:512 + T], onesmean[:],
                                              xsq[:, 4096 + c * T:4096 + (c + 1) * T],
                                              start=(c == 0), stop=(c == NCH - 1)),
                      reads=["sq", "onesmean"], writes=bk(1))
            em.op("vector", lambda e: e.tensor_copy(rowmb[:], SA[0:1, 0:T]),
                  reads=bk(0), writes=["rowmb"])
            em.op("vector", lambda e: e.tensor_tensor(SA[0:1, 0:T], rowmb[:], rowmb[:],
                                                      ALU.mult),
                  reads=["rowmb"], writes=bk(0))
            em.op("vector", lambda e: e.tensor_copy(rowvar[:], SA[0:1, 512:512 + T]),
                  reads=bk(1), writes=["rowvar"])
            em.op("vector", lambda e: e.tensor_tensor(rowvar[:], rowvar[:], SA[0:1, 0:T],
                                                      ALU.subtract),
                  reads=["rowvar"] + bk(0), writes=["rowvar"])
            em.op("vector", lambda e: e.tensor_scalar(rowvar[:], rowvar[:], 1.0, EPS,
                                                      ALU.mult, ALU.add),
                  reads=["rowvar"], writes=["rowvar"])
            em.op("scalar", lambda e: e.activation(rowr[:], rowvar[:], AF.Ln),
                  reads=["rowvar"], writes=["rowr"])
            em.op("scalar", lambda e: e.activation(rowrb[:], rowr[:], AF.Exp, scale=-0.5),
                  reads=["rowr"], writes=["rowrb"])
            em.op("tensor", lambda e: e.matmul(SB[:, 0:T], onesrow512[0:1, 0:128],
                                               rowmb[:], start=True, stop=True),
                  reads=["rowmb", "onesrow512"], writes=bk(2))
            em.op("tensor", lambda e: e.matmul(SB[:, 512:512 + T], onesrow512[0:1, 0:128],
                                               rowrb[:], start=True, stop=True),
                  reads=["rowrb", "onesrow512"], writes=bk(3))
            em.op("vector", lambda e: e.tensor_copy(mb16[:], SB[:, 0:T]),
                  reads=bk(2), writes=["mb16"])
            em.op("vector", lambda e: e.tensor_copy(rb16[:], SB[:, 512:512 + T]),
                  reads=bk(3), writes=["rb16"])
            for c in range(NCH):
                em.op("vector",
                      lambda e, c=c: e.tensor_tensor(xhat[:, c * T:(c + 1) * T],
                                                     xsq[:, c * T:(c + 1) * T],
                                                     mb16[:], ALU.subtract),
                      reads=["xb", "mb16"], writes=[f"xhs{c}"])
            for c in range(NCH):
                em.op("vector",
                      lambda e, c=c: e.tensor_tensor(xhat[:, c * T:(c + 1) * T],
                                                     xhat[:, c * T:(c + 1) * T],
                                                     rb16[:], ALU.mult),
                      reads=[f"xhs{c}", "rb16"], writes=[f"xh{c}", f"xhs{c}"])
            if glob:
                em.op("scalar", lambda e: e.activation(xgsq[:, 0:128], xgres[:], AF.Copy),
                      reads=["xgres"], writes=["gxb"])
                em.op("vector", lambda e: e.tensor_tensor(xgsq[:, 128:256], xgsq[:, 0:128],
                                                          xgsq[:, 0:128], ALU.mult),
                      reads=["gxb"], writes=["gsq"])
                for c in range(NCH):
                    em.op("tensor",
                          lambda e, c=c: e.matmul(PSB3[0:1, 0:16], onesmean[:],
                                                  xgsq[:, c * 16:(c + 1) * 16],
                                                  start=(c == 0), stop=(c == NCH - 1)),
                          reads=["gxb", "onesmean"], writes=bk(6))
                for c in range(NCH):
                    em.op("tensor",
                          lambda e, c=c: e.matmul(PSB3[0:1, 512:528], onesmean[:],
                                                  xgsq[:, 128 + c * 16:128 + (c + 1) * 16],
                                                  start=(c == 0), stop=(c == NCH - 1)),
                          reads=["gsq", "onesmean"], writes=bk(7))
                em.op("vector", lambda e: e.tensor_copy(growm[:], PSB3[0:1, 0:16]),
                      reads=bk(6), writes=["growm"])
                em.op("vector", lambda e: e.tensor_copy(grows2[:], PSB3[0:1, 512:528]),
                      reads=bk(7), writes=["grows2"])
                em.op("vector", lambda e: e.tensor_tensor(growvar[:], growm[:], growm[:],
                                                          ALU.mult),
                      reads=["growm"], writes=["growvar"])
                em.op("vector", lambda e: e.tensor_tensor(growvar[:], grows2[:], growvar[:],
                                                          ALU.subtract),
                      reads=["grows2", "growvar"], writes=["growvar"])
                em.op("vector", lambda e: e.tensor_scalar(growvar[:], growvar[:], 1.0, EPS,
                                                          ALU.mult, ALU.add),
                      reads=["growvar"], writes=["growvar"])
                em.op("scalar", lambda e: e.activation(growr[:], growvar[:], AF.Ln),
                      reads=["growvar"], writes=["growr"])
                em.op("scalar", lambda e: e.activation(growr[:], growr[:], AF.Exp, scale=-0.5),
                      reads=["growr"], writes=["growr"])
                em.op("tensor", lambda e: e.matmul(PSB3[:, 16:32], onesrow[:], growm[:],
                                                   start=True, stop=True),
                      reads=["growm", "onesrow"], writes=bk(6))
                em.op("tensor", lambda e: e.matmul(PSB3[:, 528:544], onesrow[:], growr[:],
                                                   start=True, stop=True),
                      reads=["growr", "onesrow"], writes=bk(7))
                mgb = PSB3[:, 16:32].rearrange("p k -> p () k").broadcast_to(
                    [128, NCH, 16])
                rgb = PSB3[:, 528:544].rearrange("p k -> p () k").broadcast_to(
                    [128, NCH, 16])
                em.op("vector",
                      lambda e, mgb=mgb: e.tensor_tensor(
                          xghat[:].rearrange("p (c k) -> p c k", c=NCH),
                          xgres[:].rearrange("p (c k) -> p c k", c=NCH), mgb, ALU.subtract),
                      reads=["xgres"] + bk(6), writes=["xghat"])
                em.op("vector",
                      lambda e, rgb=rgb: e.tensor_tensor(
                          xghat[:].rearrange("p (c k) -> p c k", c=NCH),
                          xghat[:].rearrange("p (c k) -> p c k", c=NCH), rgb, ALU.mult),
                      reads=["xghat"] + bk(7), writes=["xghat"])

        def ffn1(xr, xrname, glob, side_jobs=None):
            def w1dma(g):
                src = bass.AP(w1q_s.tensor, g * NCH * 128 * 512,
                              [[512, 128], [128 * 512, NCH], [1, 512]])
                buf = w1b[g % 2]
                em.op("sync",
                      lambda e, src=src, buf=buf: e.dma_start(
                          buf[:].rearrange("p (c r) -> p c r", c=NCH), src),
                      reads=["w1q_s"], writes=[f"w1b{g % 2}"], sem=f"w1b{g % 2}")
            w1dma(0)
            w1dma(1)
            for j in range(NFC):
                g = j // 4
                buf = w1b[g % 2]
                pview, bank = p4view(j)
                for c in range(NCH):
                    lhs = buf[:, c * 512 + (j % 4) * 128:c * 512 + (j % 4) * 128 + 128]
                    em.op("tensor",
                          lambda e, lhs=lhs, c=c, pview=pview: e.matmul(
                              pview, lhs, xhat[:, c * T:(c + 1) * T],
                              start=(c == 0), stop=(c == NCH - 1)),
                          reads=[f"w1b{g % 2}", f"xh{c}"], writes=bank)
                em.op("scalar",
                      lambda e, j=j, pview=pview: e.activation(
                          h[:, j * T:(j + 1) * T], pview, AF.Relu, bias=b1s[:, j:j + 1]),
                      reads=bank + ["b1s"], writes=[f"hh{j % 4}"])
                if glob:
                    gview = SA[:, 0:16] if j % 2 == 0 else SA[:, 512:528]
                    gbank = bk(0 + j % 2)
                    for c in range(NCH):
                        lhs = buf[:, c * 512 + (j % 4) * 128:c * 512 + (j % 4) * 128 + 128]
                        em.op("tensor",
                              lambda e, lhs=lhs, c=c, gview=gview: e.matmul(
                                  gview, lhs, xghat[:, c * 16:(c + 1) * 16],
                                  start=(c == 0), stop=(c == NCH - 1)),
                              reads=[f"w1b{g % 2}", "xghat"], writes=gbank)
                    em.op("scalar",
                          lambda e, j=j, gview=gview: e.activation(
                              hg[:, j * 16:(j + 1) * 16], gview, AF.Relu,
                              bias=b1s[:, j:j + 1]),
                          reads=gbank + ["b1s"], writes=["hg"])
                if j % 4 == 3 and g + 2 < 8:
                    w1dma(g + 2)
                if side_jobs:
                    for _ in range(2):
                        if side_jobs:
                            side_jobs.pop(0)()

        def ffn2(xr, xrname, glob):
            def w2dma(k):
                src = bass.AP(w2q_s.tensor, k * 8 * 128 * 512,
                              [[512, 128], [128 * 512, 8], [1, 512]])
                buf = w2b[k % 2]
                em.op("sync", lambda e, src=src, buf=buf: e.dma_start(
                          buf[:].rearrange("p (cg r) -> p cg r", cg=8), src),
                      reads=["w2q_s"], writes=[f"w2b{k % 2}"], sem=f"w2b{k % 2}")
            w2dma(0)
            w2dma(1)
            for k in range(NCH):
                buf = w2b[k % 2]
                pview, bank = p4view(k + 2)
                for c2 in range(NFC):
                    em.op("tensor",
                          lambda e, buf=buf, c2=c2, pview=pview: e.matmul(
                              pview, buf[:, c2 * 128:(c2 + 1) * 128],
                              h[:, c2 * T:(c2 + 1) * T], start=(c2 == 0), stop=False),
                          reads=[f"w2b{k % 2}", f"hh{c2 % 4}"], writes=bank)
                em.op("tensor",
                      lambda e, k=k, pview=pview: e.matmul(
                          pview, b2rb[0:1, k * 128:(k + 1) * 128], onesrow512[0:1, :],
                          start=False, stop=True),
                      reads=["b2rb", "onesrow512"], writes=bank)
                half = "lo" if k < 4 else "hi"
                em.op("vector",
                      lambda e, k=k, pview=pview: e.scalar_tensor_tensor(
                          xr[:, k * T:(k + 1) * T], pview, 0.5, xr[:, k * T:(k + 1) * T],
                          ALU.mult, ALU.add),
                      reads=bank + [f"xr_{half}"], writes=[f"xr_{half}"])
                if glob:
                    gview = SA[:, 0:16] if k % 2 == 0 else SA[:, 512:528]
                    gbank = bk(0 + k % 2)
                    for c2 in range(NFC):
                        em.op("tensor",
                              lambda e, buf=buf, c2=c2, gview=gview: e.matmul(
                                  gview, buf[:, c2 * 128:(c2 + 1) * 128],
                                  hg[:, c2 * 16:(c2 + 1) * 16],
                                  start=(c2 == 0), stop=False),
                              reads=[f"w2b{k % 2}", "hg"], writes=gbank)
                    em.op("tensor",
                          lambda e, k=k, gview=gview: e.matmul(
                              gview, b2rb[0:1, k * 128:(k + 1) * 128],
                              onesrow512[0:1, 0:16], start=False, stop=True),
                          reads=["b2rb", "onesrow512"], writes=gbank)
                    em.op("vector",
                          lambda e, k=k, gview=gview: e.scalar_tensor_tensor(
                              xgres[:, k * 16:(k + 1) * 16], gview, 0.5,
                              xgres[:, k * 16:(k + 1) * 16], ALU.mult, ALU.add),
                          reads=gbank + ["xgres"], writes=["xgres"])
                if k + 2 < NCH:
                    w2dma(k + 2)


        def ffn(xr, xrname, glob):
            ffn1(xr, xrname, glob)
            ffn2(xr, xrname, glob)

        def proj(glob):
            stream_q = [("q", 0), ("q", 1), ("k", 0), ("k", 1), ("v", 0), ("v", 1)]

            def pjdma(idx):
                pname, half = stream_q[idx]
                wscr = {"q": wq_s, "k": wk_s, "v": wv_s}[pname]
                bid = idx % 2
                src = bass.AP(wscr.tensor, half * 512,
                              [[NCH * D, 128], [D, NCH], [1, 512]])
                buf = pjb[bid]
                em.op("sync",
                      lambda e, src=src, buf=buf: e.dma_start(
                          buf[:].rearrange("p (c i) -> p c i", c=NCH), src),
                      reads=[f"{pname}_s"], writes=[f"pjb{bid}"], sem=f"pjb{bid}")
            pjdma(0)
            pjdma(1)
            bi = 0
            for pname, dstb, dstn in (("q", q_hd, "q_hd"), ("k", k_hd, "k_hd")):
                for half in range(2):
                    bid = bi % 2
                    buf = pjb[bid]
                    bi += 1
                    for hh in range(8):
                        hd = half * 8 + hh
                        pview, bank = p4view(hd)
                        pview = pview[0:64, :]
                        for c in range(NCH):
                            lhs = buf[:, c * 512 + hh * 64:c * 512 + hh * 64 + 64]
                            em.op("tensor",
                                  lambda e, lhs=lhs, c=c, pview=pview: e.matmul(
                                      pview, lhs, xhat[:, c * T:(c + 1) * T],
                                      start=(c == 0), stop=(c == NCH - 1)),
                                  reads=[f"pjb{bid}", f"xh{c}"], writes=bank)
                        if pname == "q":
                            dst = dstb[0:64, hd * T:(hd + 1) * T]
                        else:
                            dst = dstb[0:64, hd * 640:(hd + 1) * 640].rearrange(
                                "p (s c) -> p s c", s=8)[:, :, 0:64]
                            pview = pview.rearrange("p (s c) -> p s c", s=8)
                        em.op("scalar",
                              lambda e, dst=dst, pview=pview: e.activation(
                                  dst, pview, AF.Copy),
                              reads=bank, writes=[dstn])
                    if bi + 1 < 6:
                        pjdma(bi + 1)
            for half in range(2):
                bid = bi % 2
                buf = pjb[bid]
                bi += 1
                for blk in range(T // 64):
                    pview = PSB3[:, 0:T] if blk % 2 == 0 else PSB3[:, T:2 * T]
                    bank = bk(6 + blk % 2)
                    for c in range(NCH):
                        em.op("tensor",
                              lambda e, buf=buf, c=c, blk=blk, pview=pview: e.matmul(
                                  pview[0:64, :],
                                  xhat[:, c * T + blk * 64:c * T + blk * 64 + 64],
                                  buf[:, c * 512:(c + 1) * 512],
                                  start=(c == 0), stop=(c == NCH - 1)),
                              reads=[f"pjb{bid}", f"xh{c}"], writes=bank)
                    em.op("vector",
                          lambda e, blk=blk, half=half, pview=pview: e.tensor_copy(
                              vaug[0:64, blk * D + half * 512:blk * D + half * 512 + 512],
                              pview[0:64, :]),
                          reads=bank, writes=["vaug"])
                if bi + 1 < 6:
                    pjdma(bi + 1)
                if glob:
                    gview = PSB2[:, 0:T]
                    gbank = bk(4)
                    for c in range(NCH):
                        em.op("tensor",
                              lambda e, buf=buf, c=c, gview=gview: e.matmul(
                                  gview[64:80, :], xghat[:, c * 16:(c + 1) * 16],
                                  buf[:, c * 512:(c + 1) * 512],
                                  start=(c == 0), stop=(c == NCH - 1),
                                  tile_position=(0, 64)),
                              reads=[f"pjb{bid}", "xghat"], writes=gbank)
                    for blk in range(T // 64):
                        em.op("vector",
                              lambda e, blk=blk, half=half, gview=gview: e.tensor_copy(
                                  vaug[64:80,
                                       blk * D + half * 512:blk * D + half * 512 + 512],
                                  gview[64:80, :]),
                              reads=gbank, writes=["vaug"])
            if glob:
                kb = []
                for half in range(2):
                    bid = bi % 2
                    buf = pjb[bid]
                    bi += 1
                    src = bass.AP(wk_s.tensor, half * 4096,
                                  [[NCH * D, 128], [1, 4096]])
                    em.op("sync",
                          lambda e, src=src, buf=buf: e.dma_start(buf[:, 0:4096], src),
                          reads=["k_s"], writes=[f"pjb{bid}"], sem=f"pjb{bid}")
                    kb.append((buf, bid))
                for hd in range(H):
                    gview = PSB3[0:64, 0:16] if hd % 2 == 0 else PSB3[0:64, 512:528]
                    gbank = bk(6 + hd % 2)
                    for c in range(NCH):
                        buf, bid = kb[c // 4]
                        lhs = buf[:, (c % 4) * 1024 + hd * 64:(c % 4) * 1024 + hd * 64 + 64]
                        em.op("tensor",
                              lambda e, lhs=lhs, gview=gview, c=c: e.matmul(
                                  gview, lhs, xghat[:, c * 16:(c + 1) * 16],
                                  start=(c == 0), stop=(c == NCH - 1)),
                              reads=[f"pjb{bid}", "xghat"], writes=gbank)
                    kgdst = k_hd[0:64, hd * 640:(hd + 1) * 640].rearrange(
                        "p (s c) -> p s c", s=8)[:, :, 64:80]
                    em.op("vector",
                          lambda e, kgdst=kgdst, gview=gview: e.tensor_copy(
                              kgdst, gview.rearrange("p k -> p () k").broadcast_to(
                                  (64, 8, 16))),
                          reads=gbank, writes=["k_hd"])

        def attn(t, xr, xrname):
            halves = [(p, g) for p in range(NT) for g in range(2)]

            def scores_stage(k):
                p, g = halves[k]
                hp = k % 2
                Sreg = SA if hp == 0 else SB
                sbanks = bk(2 * hp, 2 * hp + 1)
                for hh in range(8):
                    hd = g * 8 + hh
                    for m in range(2):
                        qb = 2 * p + m
                        qs = hd * T + qb * 64
                        ks = hd * 640 + qb * 80
                        em.op("tensor",
                              lambda e, qs=qs, ks=ks, hh=hh, m=m, Sreg=Sreg: e.matmul(
                                  Sreg[64 * m:64 * m + 64, 128 * hh:128 * hh + 80],
                                  q_hd[0:64, qs:qs + 64],
                                  k_hd[0:64, ks:ks + 80],
                                  start=True, stop=True,
                                  tile_position=(0, 64 * m)),
                              reads=["q_hd", "k_hd"], writes=sbanks)

            def smx_stage(k):
                p, g = halves[k]
                hp = k % 2
                hq = k % 3
                Sreg = SA if hp == 0 else SB
                sbanks = bk(2 * hp, 2 * hp + 1)
                nmv = negm[:, 8 * hp:8 * hp + 8]
                smv = sums[:, 8 * hp:8 * hp + 8]
                esv = esrc[:]
                psv = psb[:, 640 * hq:640 * hq + 640]
                s3d = Sreg[:].rearrange("p (h k) -> p h k", h=8)[:, :, 0:80]
                em.op("vector",
                      lambda e, s3d=s3d, nmv=nmv: e.tensor_reduce(
                          nmv, s3d, axis=AX.X, op=ALU.max, negate=True),
                      reads=sbanks, writes=[f"negm{hp}"])
                nb = nmv.rearrange("p h -> p h ()").broadcast_to([128, 8, 80])
                em.op("vector",
                      lambda e, s3d=s3d, nb=nb, esv=esv: e.tensor_tensor(
                          esv.rearrange("p (h k) -> p h k", h=8), s3d, nb, ALU.add),
                      reads=sbanks + [f"negm{hp}"], writes=["esrc"])
                em.op("scalar",
                      lambda e, psv=psv, esv=esv: e.activation(psv, esv, AF.Exp),
                      reads=["esrc"], writes=[f"psb{hq}"])
                mask = maskq0 if (t == 0 and p == 0) else maskstd
                mname = "maskq0" if (t == 0 and p == 0) else "maskstd"
                mb = mask[:].rearrange("p k -> p () k").broadcast_to([128, 8, 80])
                em.op("gpsimd",
                      lambda e, mb=mb, psv=psv: e.tensor_tensor(
                          psv.rearrange("p (h k) -> p h k", h=8),
                          psv.rearrange("p (h k) -> p h k", h=8), mb, ALU.mult),
                      reads=[f"psb{hq}", mname], writes=[f"psb{hq}"])
                em.op("vector",
                      lambda e, psv=psv, smv=smv: e.tensor_reduce(
                          smv, psv.rearrange("p (h k) -> p h k", h=8),
                          axis=AX.X, op=ALU.add),
                      reads=[f"psb{hq}"], writes=[f"sums{hp}"])
                em.op("vector",
                      lambda e, hp=hp, smv=smv: e.reciprocal(
                          rinvf[:, 8 * hp:8 * hp + 8], smv),
                      reads=[f"sums{hp}"], writes=[f"rinvf{hp}"])
                em.op("vector",
                      lambda e, hp=hp: e.tensor_copy(rinvb[:, 8 * hp:8 * hp + 8],
                                                     rinvf[:, 8 * hp:8 * hp + 8]),
                      reads=[f"rinvf{hp}"], writes=[f"rinvb{hp}"])
                rb = rinvb[:, 8 * hp:8 * hp + 8].rearrange(
                    "p h -> p h ()").broadcast_to([128, 8, 80])
                em.op("gpsimd",
                      lambda e, rb=rb, psv=psv: e.tensor_tensor(
                          psv.rearrange("p (h k) -> p h k", h=8),
                          psv.rearrange("p (h k) -> p h k", h=8), rb, ALU.mult),
                      reads=[f"psb{hq}", f"rinvb{hp}"], writes=[f"psb{hq}"])

            def pt_stage(k):
                p, g = halves[k]
                hp = k % 2
                hq = k % 3
                psv = psb[:, 640 * hq:640 * hq + 640]
                ph = hp
                ptbank = bk(4 + ph)
                for hh in range(8):
                    em.op("tensor",
                          lambda e, hh=hh, ph=ph, psv=psv: e.transpose(
                              PSB2_bf[0:80, ph * 1024 + 128 * hh:
                                      ph * 1024 + 128 * hh + 128],
                              psv[:, 80 * hh:80 * hh + 80], ident[:]),
                          reads=[f"psb{hq}", "ident"], writes=ptbank)
                em.op("scalar",
                      lambda e, ph=ph: e.copy(
                          ptsb[0:80, ph * 1024:ph * 1024 + 1024],
                          PSB2_bf[0:80, ph * 1024:ph * 1024 + 1024]),
                      reads=ptbank, writes=[f"ptsb{ph}"])

            def av_stage(k):
                p, g = halves[k]
                hp = k % 2
                Sreg = SA if hp == 0 else SB
                sbanks = bk(2 * hp, 2 * hp + 1)
                ph = hp
                avbank = bk(6 + hp)
                av = PSB3[:, hp * 512:hp * 512 + 512]
                for hh in range(8):
                    hd = g * 8 + hh
                    cw = hh // 2
                    ob = 64 * (hd % 2)
                    for m in range(2):
                        qb = 2 * p + m
                        em.op("tensor",
                              lambda e, hd=hd, hh=hh, m=m, qb=qb, cw=cw, ob=ob,
                              av=av, ph=ph: e.matmul(
                                  av[ob:ob + 64,
                                     128 * cw + 64 * m:128 * cw + 64 * m + 64],
                                  vaug[0:80, qb * D + 64 * hd:qb * D + 64 * hd + 64],
                                  ptsb[0:80, ph * 1024 + 128 * hh + 64 * m:
                                       ph * 1024 + 128 * hh + 64 * m + 64],
                                  start=True, stop=True, tile_position=(0, ob)),
                              reads=["vaug", f"ptsb{ph}"], writes=avbank)
                em.op("scalar",
                      lambda e, g=g, p=p, av=av: e.copy(
                          bass.AP(attnf, (4 * g) * T + p * 128,
                                  [[NCH * T, 128], [T, 4], [1, 128]]),
                          av.rearrange("p (c q) -> p c q", c=4)),
                      reads=avbank, writes=["attnf"])


            def ptav_stage(k):
                p, g = halves[k]
                hp = k % 2
                Sreg = SA if hp == 0 else SB
                sbanks = bk(2 * hp, 2 * hp + 1)
                psv = psb[:, 640 * hp:640 * hp + 640]
                ph = hp
                ptbank = bk(4 + ph)
                for hh in range(8):
                    em.op("tensor",
                          lambda e, hh=hh, ph=ph, psv=psv: e.transpose(
                              PSB2_bf[0:80, ph * 1024 + 128 * hh:
                                      ph * 1024 + 128 * hh + 128],
                              psv[:, 80 * hh:80 * hh + 80], ident[:]),
                          reads=[f"psb{hp}", "ident"], writes=ptbank)
                em.op("scalar",
                      lambda e, ph=ph: e.copy(
                          ptsb[0:80, ph * 1024:ph * 1024 + 1024],
                          PSB2_bf[0:80, ph * 1024:ph * 1024 + 1024]),
                      reads=ptbank, writes=[f"ptsb{ph}"])
                avbank = bk(6 + hp)
                av = PSB3[:, hp * 512:hp * 512 + 512]
                for hh in range(8):
                    hd = g * 8 + hh
                    cw = hh // 2
                    ob = 64 * (hd % 2)
                    for m in range(2):
                        qb = 2 * p + m
                        em.op("tensor",
                              lambda e, hd=hd, hh=hh, m=m, qb=qb, cw=cw, ob=ob,
                              av=av, ph=ph: e.matmul(
                                  av[ob:ob + 64,
                                     128 * cw + 64 * m:128 * cw + 64 * m + 64],
                                  vaug[0:80, qb * D + 64 * hd:qb * D + 64 * hd + 64],
                                  ptsb[0:80, ph * 1024 + 128 * hh + 64 * m:
                                       ph * 1024 + 128 * hh + 64 * m + 64],
                                  start=True, stop=True, tile_position=(0, ob)),
                              reads=["vaug", f"ptsb{ph}"], writes=avbank)
                em.op("scalar",
                      lambda e, g=g, p=p, av=av: e.copy(
                          bass.AP(attnf, (4 * g) * T + p * 128,
                                  [[NCH * T, 128], [T, 4], [1, 128]]),
                          av.rearrange("p (c q) -> p c q", c=4)),
                      reads=avbank, writes=["attnf"])


            NH = len(halves)
            for k in range(NH):
                scores_stage(k)
                smx_stage(k)
                if k >= 2:
                    pt_stage(k - 2)
                if k >= 3:
                    av_stage(k - 3)
            pt_stage(NH - 2)
            av_stage(NH - 3)
            pt_stage(NH - 1)
            av_stage(NH - 2)
            av_stage(NH - 1)
            for half in range(2):
                src = bass.AP(wo_s.tensor, half * 512, [[NCH * D, 128], [D, NCH], [1, 512]])
                buf = pjb[half]
                em.op("sync",
                      lambda e, src=src, buf=buf: e.dma_start(
                          buf[:].rearrange("p (c i) -> p c i", c=NCH), src),
                      reads=["o_s"], writes=[f"pjb{half}"], sem=f"pjb{half}")
            for jc in range(NCH):
                half = jc // 4
                buf = pjb[half]
                pview, bank = p4view(jc)
                for c in range(NCH):
                    lhs = buf[:, c * 512 + (jc % 4) * 128:c * 512 + (jc % 4) * 128 + 128]
                    em.op("tensor",
                          lambda e, lhs=lhs, c=c, pview=pview: e.matmul(
                              pview, lhs, attnf[:, c * T:(c + 1) * T],
                              start=(c == 0), stop=(c == NCH - 1)),
                          reads=[f"pjb{half}", "attnf"], writes=bank)
                rhalf = "lo" if jc < 4 else "hi"
                em.op("vector",
                      lambda e, jc=jc, pview=pview: e.scalar_tensor_tensor(
                          xr[:, jc * T:(jc + 1) * T], pview, 1.0,
                          xr[:, jc * T:(jc + 1) * T], ALU.mult, ALU.add),
                      reads=bank + [f"xr_{rhalf}"], writes=[f"xr_{rhalf}"])

        def xload(t, hf):
            src = bass.AP(xfm.tensor, (hf * 4) * 128 * TOK + t * T,
                          [[TOK, 128], [128 * TOK, 4], [1, T]])
            xr = xres[0]
            dma(lambda e, src=src, xr=xr, hf=hf: e.dma_start(
                xr[:, hf * 4 * T:(hf * 4 + 4) * T].rearrange("p (c k) -> p c k", c=4),
                src),
                writes=[f"xr_{'lo' if hf == 0 else 'hi'}"], sem=f"xl{hf}")

        dma(lambda e: e.dma_start(b1s[:], b1in), writes=["b1s"], sem="const", batch=True)
        dma(lambda e: e.dma_start(ident[:], identin), writes=["ident"], sem="const", batch=True)
        dma(lambda e: e.dma_start(onesmean[:], onesmean_in), writes=["onesmean"], sem="const", batch=True)
        dma(lambda e: e.dma_start(onesrow[:], onesrow_in), writes=["onesrow"], sem="const", batch=True)
        dma(lambda e: e.dma_start(onescol[:], onescol_in), writes=["onescol"], sem="const", batch=True)
        dma(lambda e: e.dma_start(onesrow512[:], onesrow512_in), writes=["onesrow512"], sem="const", batch=True)
        dma(lambda e: e.dma_start(maskstd[:], maskstd_in), writes=["maskstd"], sem="const", batch=True)
        dma(lambda e: e.dma_start(maskq0[:], maskq0_in), writes=["maskq0"], sem="const", batch=True)
        dma(lambda e: e.dma_start(rowm[0:1, 0:512], b2row[0:1, 0:512]), writes=["rowm"], sem="const", batch=True)
        em.settle(["b1s", "ident", "onesmean", "onesrow", "onescol", "onesrow512", "maskstd", "maskq0", "esel", "rowm", "rows2"], "const")
        em.op("scalar", lambda e: e.activation(b2rb[0:1, 0:512], rowm[0:1, 0:512], AF.Copy),
              reads=["rowm"], writes=["b2rb"])
        dma(lambda e: e.dma_start(rows2[0:1, 0:512], b2row[0:1, 512:1024]),
            reads=["b2rb"], writes=["rows2", "rowm"], sem="b2h")
        em.op("scalar", lambda e: e.activation(b2rb[0:1, 512:1024], rows2[0:1, 0:512], AF.Copy),
              reads=["rows2"], writes=["b2rb"])

        xload(0, 0)
        xload(0, 1)
        gsrc = bass.AP(xgin.tensor, 0, [[16, 128], [128 * 16, NCH], [1, 16]])
        dma(lambda e, gsrc=gsrc: e.dma_start(
            xgres[:].rearrange("p (c k) -> p c k", c=NCH), gsrc),
            writes=["xgres"], sem="xg")
        ln(xres[0], "xres0", True)

        for c in range(NCH):
            stg = wstage[c % 2]
            src = w1t[c * 128:(c + 1) * 128, :]
            dma(lambda e, stg=stg, src=src: e.dma_start(stg, src), writes=[f"wst{c % 2}"],
                sem=f"stg{c % 2}")
            em.op("vector",
                  lambda e, stg=stg, c=c: e.tensor_reduce(
                      gacc1[:, c:c + 1], stg, axis=AX.X, op=ALU.add,
                      apply_absolute_value=True),
                  reads=[f"wst{c % 2}"], writes=["gacc1"])
        em.op("tensor", lambda e: e.matmul(SA[0:1, 0:NCH], onescol[:], gacc1[:],
                                           start=True, stop=True),
              reads=["gacc1", "onescol"], writes=bk(0))
        em.op("vector", lambda e: e.tensor_reduce(growtot[0:1, 0:1], SA[0:1, 0:NCH],
                                                  axis=AX.X, op=ALU.add),
              reads=bk(0), writes=["growtot"])
        em.op("scalar", lambda e: e.activation(throw[0:1, 0:1], growtot[0:1, 0:1], AF.Copy,
                                               scale=0.5 / (D * DFF), bias=0.5 * QEPS),
              reads=["growtot"], writes=["throw"])
        em.op("scalar", lambda e: e.activation(nthrow[0:1, 0:1], growtot[0:1, 0:1], AF.Copy,
                                               scale=-0.5 / (D * DFF), bias=-0.5 * QEPS),
              reads=["growtot"], writes=["nthrow"])
        em.op("tensor", lambda e: e.matmul(SA[:, 0:1], onesrow[:], throw[0:1, 0:1],
                                           start=True, stop=True),
              reads=["throw", "onesrow"], writes=bk(0))
        em.op("tensor", lambda e: e.matmul(SA[:, 512:513], onesrow[:], nthrow[0:1, 0:1],
                                           start=True, stop=True),
              reads=["nthrow", "onesrow"], writes=bk(1))
        em.op("scalar", lambda e: e.copy(thrc[:, 0:1], SA[:, 0:1]), reads=bk(0),
              writes=["thrc"])
        em.op("scalar", lambda e: e.copy(nthrc[:, 0:1], SA[:, 512:513]), reads=bk(1),
              writes=["nthrc"])

        for c in range(NCH):
            stg = wstage[c % 2]
            src = w1t[c * 128:(c + 1) * 128, :]
            ia, ib = (2 * c) % 3, (2 * c + 1) % 3
            ta, tb = tmps3[ia], tmps3[ib]
            dma(lambda e, stg=stg, src=src: e.dma_start(stg, src),
                reads=[f"w1q{c % 2}"], writes=[f"wst{c % 2}"], sem=f"stg{c % 2}")
            em.op("vector",
                  lambda e, stg=stg, ta=ta: e.tensor_scalar(
                      ta, stg, thrc[:, 0:1], None, ALU.is_gt),
                  reads=[f"wst{c % 2}", "thrc"], writes=[f"tmp{ia}"] + TMPAL[ia])
            em.op("vector",
                  lambda e, stg=stg, tb=tb: e.tensor_scalar(
                      tb, stg, nthrc[:, 0:1], None, ALU.is_lt),
                  reads=[f"wst{c % 2}", "nthrc"], writes=[f"tmp{ib}"] + TMPAL[ib])
            em.op("vector",
                  lambda e, ta=ta, tb=tb: e.tensor_tensor(ta, ta, tb, ALU.subtract),
                  reads=[f"tmp{ia}", f"tmp{ib}"], writes=[f"tmp{ia}", f"wst{c % 2}done"])
            dst = bass.AP(w1q_s.tensor, c * 128 * 512,
                          [[512, 128], [NCH * 128 * 512, 8], [1, 512]])
            em.op("scalar",
                  lambda e, ta=ta, dst=dst: e.dma_start(
                      dst, ta.rearrange("p (jg r) -> p jg r", jg=8)),
                  reads=[f"tmp{ia}"], writes=["w1q_s", f"w1q{c % 2}"],
                  sem="w1qs", batch=True)

        vaug32 = vaug.bitcast(F32)
        vstg = [vaug32[:, 0:2048], vaug32[:, 2048:4096]]
        w2jobs = []

        def w2p1_job(i):
            stg = vstg[i % 2]
            srcw = bass.AP(w2t.tensor, (2 * i) * 128 * D,
                           [[D, 128], [128 * D, 2], [1, D]])
            dma(lambda e, stg=stg, srcw=srcw: e.dma_start(
                stg.rearrange("p (c e) -> p c e", c=2), srcw),
                writes=[f"w2stg{i % 2}"], sem=f"stg2{i % 2}")
            em.op("vector",
                  lambda e, stg=stg, i=i: e.tensor_reduce(
                      gacc2[:, i:i + 1], stg, axis=AX.X, op=ALU.add,
                      apply_absolute_value=True),
                  reads=[f"w2stg{i % 2}"], writes=["gacc2"])

        def w2gamma_job():
            em.op("tensor", lambda e: e.matmul(SA[0:1, 0:16], onescol[:], gacc2[:],
                                               start=True, stop=True),
                  reads=["gacc2", "onescol"], writes=bk(0))
            em.op("vector", lambda e: e.tensor_reduce(growtot[0:1, 1:2], SA[0:1, 0:16],
                                                      axis=AX.X, op=ALU.add),
                  reads=bk(0), writes=["growtot"])
            em.op("scalar", lambda e: e.activation(throw[0:1, 1:2], growtot[0:1, 1:2],
                                                   AF.Copy, scale=0.5 / (D * DFF),
                                                   bias=0.5 * QEPS),
                  reads=["growtot"], writes=["throw"])
            em.op("scalar", lambda e: e.activation(nthrow[0:1, 1:2], growtot[0:1, 1:2],
                                                   AF.Copy, scale=-0.5 / (D * DFF),
                                                   bias=-0.5 * QEPS),
                  reads=["growtot"], writes=["nthrow"])
            em.op("tensor", lambda e: e.matmul(SA[:, 0:1], onesrow[:], throw[0:1, 1:2],
                                               start=True, stop=True),
                  reads=["throw", "onesrow"], writes=bk(0))
            em.op("tensor", lambda e: e.matmul(SA[:, 512:513], onesrow[:],
                                               nthrow[0:1, 1:2], start=True, stop=True),
                  reads=["nthrow", "onesrow"], writes=bk(1))
            em.op("scalar", lambda e: e.copy(thrc[:, 1:2], SA[:, 0:1]), reads=bk(0),
                  writes=["thrc"])
            em.op("scalar", lambda e: e.copy(nthrc[:, 1:2], SA[:, 512:513]), reads=bk(1),
                  writes=["nthrc"])

        def w2p2_job(i):
            stg = vstg[i % 2]
            srcw = bass.AP(w2t.tensor, (2 * i) * 128 * D,
                           [[D, 128], [128 * D, 2], [1, D]])
            isl = i % 3
            slot = tmps3[isl]
            ta = slot[:, 0:2048].rearrange("p (k c e) -> p k c e", k=NCH, c=2)
            tb = slot[:, 2048:4096].rearrange("p (k c e) -> p k c e", k=NCH, c=2)
            sview = stg.rearrange("p (c k e) -> p k c e", c=2, k=NCH)
            dma(lambda e, stg=stg, srcw=srcw: e.dma_start(
                stg.rearrange("p (c e) -> p c e", c=2), srcw),
                reads=[f"w2w{i % 2}"], writes=[f"w2stg{i % 2}"], sem=f"stg2{i % 2}")
            em.op("vector",
                  lambda e, sview=sview, ta=ta: e.tensor_scalar(
                      ta, sview, thrc[:, 1:2], None, ALU.is_gt),
                  reads=[f"w2stg{i % 2}", "thrc"], writes=[f"tmp{isl}"] + TMPAL[isl])
            em.op("vector",
                  lambda e, sview=sview, tb=tb: e.tensor_scalar(
                      tb, sview, nthrc[:, 1:2], None, ALU.is_lt),
                  reads=[f"w2stg{i % 2}", "nthrc"],
                  writes=[f"tmp{isl}", f"w2w{i % 2}"])
            em.op("vector",
                  lambda e, ta=ta, tb=tb: e.tensor_tensor(ta, ta, tb, ALU.subtract),
                  reads=[f"tmp{isl}"], writes=[f"tmp{isl}"])
            dstw = bass.AP(w2q_s.tensor, (i // 2) * 128 * 512 + (i % 2) * 256,
                           [[512, 128], [8 * 128 * 512, NCH], [1, 256]])
            em.op("scalar",
                  lambda e, slot=slot, dstw=dstw: e.dma_start(
                      dstw, slot[:, 0:2048].rearrange("p (k r) -> p k r", k=NCH)),
                  reads=[f"tmp{isl}"], writes=["w2q_s"], sem="w2qs", batch=True)

        for i in range(16):
            w2jobs.append(lambda i=i: w2p1_job(i))
        w2jobs.append(w2gamma_job)
        for i in range(16):
            w2jobs.append(lambda i=i: w2p2_job(i))


        ffn1(xres[0], "xres0", True, side_jobs=w2jobs)
        while w2jobs:
            w2jobs.pop(0)()

        ffn2(xres[0], "xres0", True)


        qi = 0
        for name, win, wscr, scale in (("q", wqt, wq_s, 0.125), ("k", wkt, wk_s, 1.0),
                                       ("v", wvt, wv_s, 1.0), ("o", wot, wo_s, 1.0)):
            for half in range(2):
                srcw = bass.AP(win.tensor, (half * 4) * 128 * D,
                               [[D, 128], [128 * D, 4], [1, D]])
                isl = qi % 3
                slot = tmps3[isl]
                em.op("scalar", lambda e, srcw=srcw: e.dma_start(
                    vaug32[:].rearrange("p (c i) -> p c i", c=4), srcw),
                    writes=["w2stg0", "w2stg1"], sem="stg20")
                em.op("scalar",
                      lambda e, scale=scale, slot=slot: e.activation(
                          slot, vaug32[:], AF.Copy, scale=scale),
                      reads=["w2stg0", "w2stg1"],
                      writes=[f"tmp{isl}"] + TMPAL[isl])
                em.op("scalar",
                      lambda e, wscr=wscr, half=half, slot=slot: e.dma_start(
                          bass.AP(wscr.tensor, half * 4096, [[8192, 128], [1, 4096]]),
                          slot),
                      reads=[f"tmp{isl}"], writes=[f"{name}_s"], sem="qkvs")
                qi += 1

        em.op("vector", lambda e: e.engine_nop(),
              writes=["xb", "sq", "wst0", "wst1", "tmp0", "tmp1",
                      "tmp2", "w2stg0", "w2stg1", "vaug", "attnf"])

        import os
        dbg = os.environ.get("MK_DEBUG_PHASES", "")
        for t in range(NT):
            xr = xres[0]
            xrname = "xres0"
            if t > 0:
                xload(t, 0)
                xload(t, 1)
            glob = (t == 0)
            if t > 0:
                ln(xr, xrname, glob)
            if "noffn1" not in dbg and t > 0:
                ffn(xr, xrname, glob)
            if "s1" not in dbg:
                ln(xr, xrname, glob)
                proj(glob)
                if "noattn" not in dbg:
                    attn(t, xr, xrname)
                if "s2" not in dbg:
                    ln(xr, xrname, False)
                    ffn(xr, xrname, False)
            for hf in range(2):
                dst = bass.AP(yout.tensor, (hf * 4) * 128 * TOK + t * T,
                              [[TOK, 128], [128 * TOK, 4], [1, T]])
                dma(lambda e, dst=dst, xr=xr, hf=hf: e.dma_start(
                    dst, xr[:, hf * 4 * T:(hf * 4 + 4) * T].rearrange(
                        "p (c k) -> p c k", c=4)),
                    reads=[f"xr_{'lo' if hf == 0 else 'hi'}"],
                    writes=[f"yout{t}{hf}"], sem=f"ys{hf}")

        em.final_wait(["ys0", "ys1"])

        for e in ENGINES:
            if e != "sync":
                sems[e] = E(nc.semaphore(f"sem_{e}"))
        for s in em.dma_sems:
            sems[s] = E(nc.semaphore(f"sem_{s}"))
        block = E(nc.Block())

        def emit(engobj, ename):
            for fn, waits, inc, semname in em.ops[ename]:
                for (e2, v) in waits:
                    engobj.wait_ge(sems[e2], v)
                if fn is not None:
                    ins = fn(engobj)
                    if inc:
                        ins.then_inc(sems[semname], inc)

        @block.sync
        def _(sync):
            emit(sync, "sync")

        @block.tensor
        def _(tensor):
            emit(tensor, "tensor")

        @block.vector
        def _(vector):
            emit(vector, "vector")

        @block.scalar
        def _(scalar):
            emit(scalar, "scalar")

        @block.gpsimd
        def _(gpsimd):
            emit(gpsimd, "gpsimd")

    return nc


_NC_CACHE = None


def kernel(L0_attn_hidden_states, L0_attn_Wq, L0_attn_Wk, L0_attn_Wv, L0_attn_Wo,
           L0_ffn_W1, L0_ffn_b1, L0_ffn_W2, L0_ffn_b2):
    global _NC_CACHE
    x = np.asarray(L0_attn_hidden_states, dtype=np.float32)
    bf = ml_dtypes.bfloat16
    w1tA = np.ascontiguousarray(np.asarray(L0_ffn_W1, dtype=np.float32).T)
    w2tA = np.ascontiguousarray(np.asarray(L0_ffn_W2, dtype=np.float32).T)
    wqtA = np.ascontiguousarray(np.asarray(L0_attn_Wq, dtype=np.float32).T)
    wktA = np.ascontiguousarray(np.asarray(L0_attn_Wk, dtype=np.float32).T)
    wvtA = np.ascontiguousarray(np.asarray(L0_attn_Wv, dtype=np.float32).T)
    wotA = np.ascontiguousarray(np.asarray(L0_attn_Wo, dtype=np.float32).T)
    b1r = np.ascontiguousarray(np.asarray(L0_ffn_b1, dtype=np.float32).reshape(NFC, 128).T)
    b2r = np.ascontiguousarray(np.asarray(L0_ffn_b2, dtype=np.float32)[None, :])
    ident = np.eye(128, dtype=bf)
    onesmean = np.full((128, 1), 1.0 / D, dtype=np.float32).astype(bf)
    onesrow = np.ones((1, 128), dtype=np.float32)
    onescol = np.ones((128, 1), dtype=np.float32)
    onesrow512 = np.ones((1, 512), dtype=bf)
    tril = np.tril(np.ones((64, 64), dtype=np.float32))
    mrow = np.concatenate([tril, np.ones((64, 16), np.float32)], 1)
    maskstd = np.concatenate([mrow, mrow], 0).astype(bf)
    eselA = np.zeros((16, D), dtype=np.float32)
    for hd in range(16):
        eselA[hd, hd * 64:(hd + 1) * 64] = 1.0
    eselA = eselA.astype(bf)

    in_maps = []
    for c in range(NCORE):
        b, half = c // 2, c % 2
        xfmA = np.ascontiguousarray(x[b, half * TOK:(half + 1) * TOK, :].T)
        xgA = np.ascontiguousarray(x[b, 0:16, :].T)
        gm = 1.0 if half == 1 else 0.0
        mq0 = np.concatenate([
            np.concatenate([tril, np.full((64, 16), gm, np.float32)], 1), mrow],
            0).astype(bf)
        in_maps.append({
            "xfm": xfmA, "xgin": xgA,
            "w1t": w1tA, "w2t": w2tA, "wqt": wqtA, "wkt": wktA, "wvt": wvtA, "wot": wotA,
            "b1in": b1r, "b2row": b2r,
            "identin": ident, "onesmean_in": onesmean, "onesrow_in": onesrow,
            "onescol_in": onescol, "onesrow512_in": onesrow512,
            "maskstd_in": maskstd, "maskq0_in": mq0, "esel_in": eselA,
        })

    if _NC_CACHE is None:
        _NC_CACHE = build_nc()
    res = run_bass_kernel_spmd(_NC_CACHE, in_maps, core_ids=list(range(NCORE)))

    out = np.empty((B, S, D), dtype=np.float32)
    for c in range(NCORE):
        b, half = c // 2, c % 2
        out[b, half * TOK:(half + 1) * TOK, :] = np.asarray(
            res.results[c]["y"], dtype=np.float32).T
    return out
